# version 11
# baseline (speedup 1.0000x reference)
import contextlib
import math
import numpy as np
import ml_dtypes
import concourse.bass as bass
import concourse.mybir as mybir
from concourse.bass_utils import run_bass_kernel_spmd
from concourse.aot_env import allow_ici_all_to_all

F32 = mybir.dt.float32
BF16 = mybir.dt.bfloat16
AF = mybir.ActivationFunctionType
ALU = mybir.AluOpType
bfnp = ml_dtypes.bfloat16

NCORES = 8
E = 2048
ECH = 16
TFULL = 8192
TOWN = 1024
NTT = 16
HID = 8192
HCH = 64
SC0 = 1.0 / math.sqrt(128.0)
SC1 = 0.125

ENGS = ("sync", "gpsimd", "tensor", "scalar", "vector")


class Sem:
    def __init__(self, h):
        self.h = h
        self.n = 0

    def inc(self, k=1):
        self.n += k
        return self.n


class Prog:
    def __init__(self):
        self.q = {e: [] for e in ENGS}

    def emit_into(self, eng, name):
        for kind, a in self.q[name]:
            if kind == "wait":
                eng.wait_ge(a[0], a[1])
                continue
            if kind == "dma":
                inst = eng.dma_start(out=a["out"], in_=a["in_"])
            elif kind == "mm":
                kw = dict(out=a["out"], lhsT=a["lhsT"], rhs=a["rhs"],
                          start=a["start"], stop=a["stop"])
                if a.get("tp") is not None:
                    kw["tile_position"] = a["tp"]
                inst = eng.matmul(**kw)
            elif kind == "tr":
                kw = {}
                if a.get("tp") is not None:
                    kw["tile_position"] = a["tp"]
                inst = eng.transpose(a["out"], a["in_"], a["ident"], **kw)
            elif kind == "act":
                kw = dict(out=a["out"], in_=a["in_"], func=a["func"])
                if a.get("bias") is not None:
                    kw["bias"] = a["bias"]
                if a.get("scale") is not None:
                    kw["scale"] = a["scale"]
                inst = eng.activation(**kw)
            elif kind == "tt":
                inst = eng.tensor_tensor(out=a["out"], in0=a["in0"], in1=a["in1"], op=a["op"])
            elif kind == "tc":
                inst = eng.tensor_copy(out=a["out"], in_=a["in_"])
            elif kind == "ts":
                inst = eng.tensor_single_scalar(out=a["out"], in_=a["in_"], scalar=a["s"], op=a["op"])
            elif kind == "recipf":
                inst = eng.reciprocal_approx_fast(a["out"], a["in_"])
            elif kind == "rmax":
                inst = eng.reduce_max(a["out"], a["in_"], axis=mybir.AxisListType.X)
            elif kind == "memset":
                inst = eng.memset(a["ap"], a["val"])
            elif kind == "cc":
                inst = eng.collective_compute(a["kind"], a["op"], replica_groups=a["rg"],
                                              ins=a["ins"], outs=a["outs"])
            else:
                raise AssertionError(kind)
            if a.get("inc") is not None:
                s, k = a["inc"]
                inst.then_inc(s, k)


def build(taps=(), upto=99):
    taps = set(taps)
    nc = bass.Bass("TRN2", target_bir_lowering=False)

    x_in = nc.dram_tensor("x_in", [E, TOWN], F32, kind="ExternalInput").ap()
    wqkv = [nc.dram_tensor("wqkv0", [E, 512], F32, kind="ExternalInput").ap(),
            nc.dram_tensor("wqkv1", [E, 384], F32, kind="ExternalInput").ap()]
    wo = [nc.dram_tensor("wo0", [E, E], F32, kind="ExternalInput").ap(),
          nc.dram_tensor("wo1", [E, E], F32, kind="ExternalInput").ap()]
    w1 = [nc.dram_tensor("w10", [HCH * 128, 4096], F32, kind="ExternalInput").ap(),
          nc.dram_tensor("w11", [HCH * 128, 4096], F32, kind="ExternalInput").ap()]
    w2 = [nc.dram_tensor("w20", [16 * 128, 8192], F32, kind="ExternalInput").ap(),
          nc.dram_tensor("w21", [16 * 128, 8192], F32, kind="ExternalInput").ap()]
    w1b = [nc.dram_tensor(f"w1b{l}", [HCH * 128, 4096], BF16, kind="Internal").ap() for l in range(2)]
    w2b = [nc.dram_tensor(f"w2b{l}", [16 * 128, 8192], BF16, kind="Internal").ap() for l in range(2)]
    wob = [nc.dram_tensor(f"wob{l}", [E, E], BF16, kind="Internal").ap() for l in range(2)]
    wqb1 = nc.dram_tensor("wqb1", [E, 384], BF16, kind="Internal").ap()
    cs = [nc.dram_tensor("cs0", [256, TFULL], F32, kind="ExternalInput").ap(),
          nc.dram_tensor("cs1", [256, TFULL], F32, kind="ExternalInput").ap()]
    consts = nc.dram_tensor("consts", [5 * 128, 128], F32, kind="ExternalInput").ap()
    y_out = nc.dram_tensor("y_out", [E, TOWN], F32, kind="ExternalOutput").ap()

    agi = [nc.dram_tensor(f"ag{l}i", [E, TOWN], BF16, kind="Internal").ap() for l in range(2)]
    ago = [nc.dram_tensor(f"ag{l}o", [NCORES * E, TOWN], BF16, kind="Internal", addr_space="Shared").ap() for l in range(2)]
    avi = [nc.dram_tensor(f"av{l}i", [E, TOWN], BF16, kind="Internal").ap() for l in range(2)]
    avo = [nc.dram_tensor(f"av{l}o", [E, TOWN], BF16, kind="Internal").ap() for l in range(2)]
    aod = nc.dram_tensor("aod", [E, 512], F32, kind="Internal").ap()

    tap_aps = {}

    def mktap(name, shape, dtype):
        if name in taps:
            tap_aps[name] = nc.dram_tensor("tap_" + name, shape, dtype, kind="ExternalOutput").ap()
        return tap_aps.get(name)

    st_ = contextlib.ExitStack()
    with st_:
        xr = st_.enter_context(nc.sbuf_tensor("xr", [128, ECH * TOWN], F32))
        A1 = st_.enter_context(nc.sbuf_tensor("A1", [128, 32768], BF16))
        A2 = st_.enter_context(nc.sbuf_tensor("A2", [128, 16384], BF16))
        P3 = st_.enter_context(nc.sbuf_tensor("P3", [128, 3072], F32))
        P4 = st_.enter_context(nc.sbuf_tensor("P4", [128, 8192], BF16))
        CONS = st_.enter_context(nc.sbuf_tensor("CONS", [128, 640], BF16))
        ONESM = st_.enter_context(nc.sbuf_tensor("ONESM", [128, 128], BF16))
        STAT = st_.enter_context(nc.sbuf_tensor("STAT", [128, 2048], F32))
        STATB = st_.enter_context(nc.sbuf_tensor("STATB", [128, 2048], BF16))
        CSb = st_.enter_context(nc.sbuf_tensor("CSb", [128, 1024], F32))
        BIAS = st_.enter_context(nc.sbuf_tensor("BIAS", [128, 16], F32))
        IDF = st_.enter_context(nc.sbuf_tensor("IDF", [128, 128], F32))

        SAB = st_.enter_context(nc.psum_tensor("SAB", [128, 1024], F32))
        SCD = st_.enter_context(nc.psum_tensor("SCD", [128, 1024], F32))
        AVB = st_.enter_context(nc.psum_tensor("AVB", [128, 2048], F32))
        pb = [SAB[:, 0:512], SAB[:, 512:1024], SCD[:, 0:512], SCD[:, 512:1024],
              AVB[:, 0:512], AVB[:, 512:1024], AVB[:, 1024:1536], AVB[:, 1536:2048]]

        s_sy = Sem(st_.enter_context(nc.semaphore("s_sy")))
        s_gq = Sem(st_.enter_context(nc.semaphore("s_gq")))
        s_cc = Sem(st_.enter_context(nc.semaphore("s_cc")))
        s_te = Sem(st_.enter_context(nc.semaphore("s_te")))
        s_se = Sem(st_.enter_context(nc.semaphore("s_se")))
        s_ve = Sem(st_.enter_context(nc.semaphore("s_ve")))
        s_ge = Sem(st_.enter_context(nc.semaphore("s_ge")))
        s_cast = Sem(st_.enter_context(nc.semaphore("s_cast")))
        ALLSEMS = [s_sy, s_gq, s_cc, s_te, s_se, s_ve, s_ge]
        ESEM = {"tensor": s_te, "scalar": s_se, "vector": s_ve, "gpsimd": s_ge}

        p = Prog()

        def W(eng, sem, val):
            if val:
                p.q[eng].append(("wait", (sem.h, val)))

        def dma(eng, out, in_, sem=None):
            s = sem if sem is not None else (s_sy if eng == "sync" else s_gq)
            v = s.inc(16)
            p.q[eng].append(("dma", dict(out=out, in_=in_, inc=(s.h, 16))))
            return v

        def op(eng, kind, inc=False, **a):
            s = ESEM[eng]
            v = None
            if inc:
                v = s.inc(1)
                a["inc"] = (s.h, 1)
            p.q[eng].append((kind, a))
            return v

        def cc(kind, ins, outs):
            v = s_cc.inc(1)
            p.q["gpsimd"].append(("cc", dict(kind=kind, op=ALU.bypass,
                                             rg=[list(range(NCORES))], ins=ins, outs=outs,
                                             inc=(s_cc.h, 1))))
            return v

        def barrier():
            vals = [(s, s.n) for s in ALLSEMS]
            for eng in ENGS:
                for s, v in vals:
                    W(eng, s, v)

        TRI = CONS[:, 0:128]
        EDGE = CONS[:, 128:256]
        IDENT = CONS[:, 256:384]
        SWAP = [CONS[:, 384:512], CONS[:, 512:640]]

        dma("gpsimd", CONS[:].rearrange("p (n c) -> p n c", c=128),
            consts.rearrange("(n p) c -> p n c", p=128))
        dma("sync", IDF[:], consts[256:384, :])
        op("gpsimd", "memset", ap=ONESM[:], val=1.0, inc=True)
        qkv0_wv = dma("gpsimd", P4[:, 0:ECH * 512].rearrange("p (c f) -> p c f", f=512),
                      wqkv[0].rearrange("(c p) f -> p c f", p=128))
        for ec in range(ECH):
            dma("sync", xr[:, ec * TOWN:(ec + 1) * TOWN], x_in[ec * 128:(ec + 1) * 128, :])
        cast_done = {}

        def bg_cast(key, dst, srcten, nrows, piece_rows):
            for p0 in range(0, nrows, piece_rows):
                v = dma("gpsimd", dst[p0:p0 + piece_rows, :], srcten[p0:p0 + piece_rows, :],
                        sem=s_cast)
                cast_done[(key, p0 // piece_rows)] = v

        def bg_cast_all():
            bg_cast("wo0", wob[0], wo[0], E, 1024)
            bg_cast("w20", w2b[0], w2[0], 16 * 128, 512)
            bg_cast("wo1", wob[1], wo[1], E, 1024)
            bg_cast("w21", w2b[1], w2[1], 16 * 128, 512)
        barrier()

        def ln_tail(mm_v, ncols):
            A = STAT[0:1, 0:512][:, 0:ncols]
            B = STAT[0:1, 512:1024][:, 0:ncols]
            C = STAT[0:1, 1024:1536][:, 0:ncols]
            D = STAT[0:1, 1536:2048][:, 0:ncols]
            W("vector", s_te, mm_v)
            op("vector", "ts", out=A, in_=pb[6][0:1, 0:ncols], s=1.0 / E, op=ALU.mult)
            op("vector", "ts", out=B, in_=pb[6][32:33, 0:ncols], s=1.0 / E, op=ALU.mult)
            op("vector", "tt", out=C, in0=A, in1=A, op=ALU.mult)
            op("vector", "tt", out=B, in0=B, in1=C, op=ALU.subtract)
            v = op("vector", "ts", out=B, in_=B, s=1e-5, op=ALU.add, inc=True)
            W("scalar", s_ve, v)
            op("scalar", "act", out=B, in_=B, func=AF.Ln, bias=0.0, scale=1.0)
            sv = op("scalar", "act", out=D, in_=B, func=AF.Exp, bias=0.0, scale=-0.5, inc=True)
            W("vector", s_se, sv)
            op("vector", "tt", out=C, in0=A, in1=D, op=ALU.mult)
            op("vector", "tc", out=STATB[0:1, 0:ncols], in_=D)
            op("vector", "tc", out=A, in_=STATB[0:1, 0:ncols])
            op("vector", "tt", out=A, in0=D, in1=A, op=ALU.subtract)
            op("vector", "tc", out=STATB[0:1, 512:512 + ncols], in_=A)
            op("vector", "tc", out=STATB[0:1, 1024:1024 + ncols], in_=C)
            op("vector", "tc", out=B, in_=STATB[0:1, 1024:1024 + ncols])
            op("vector", "tt", out=B, in0=C, in1=B, op=ALU.subtract)
            v = op("vector", "tc", out=STATB[0:1, 1536:1536 + ncols], in_=B, inc=True)
            W("tensor", s_ve, v)
            op("tensor", "mm", out=pb[0][:, 0:ncols], lhsT=ONESM[0:1, 0:128],
               rhs=STATB[0:1, 0:ncols], start=True, stop=False)
            op("tensor", "mm", out=pb[0][:, 0:ncols], lhsT=ONESM[0:1, 0:128],
               rhs=STATB[0:1, 512:512 + ncols], start=False, stop=True)
            op("tensor", "mm", out=pb[1][:, 0:ncols], lhsT=ONESM[0:1, 0:128],
               rhs=STATB[0:1, 1024:1024 + ncols], start=True, stop=False)
            v = op("tensor", "mm", out=pb[1][:, 0:ncols], lhsT=ONESM[0:1, 0:128],
                   rhs=STATB[0:1, 1536:1536 + ncols], start=False, stop=True, inc=True)
            return v

        def emit_ln_pre(lay, to_ag, tt):
            vv = tv = None
            for ec in range(ECH):
                src = xr[:, ec * TOWN + tt * 512: ec * TOWN + (tt + 1) * 512]
                vv = op("vector", "tc", out=A2[:, ec * 512:(ec + 1) * 512], in_=src, inc=True)
                vv = op("vector", "tt", out=A2[:, 8192 + ec * 512: 8192 + (ec + 1) * 512],
                        in0=src, in1=src, op=ALU.mult, inc=True)
            W("tensor", s_ve, vv)
            mmv = None
            for ec in range(ECH):
                op("tensor", "mm", out=pb[6][0:1, 0:512], lhsT=ONESM[:, 0:1],
                   rhs=A2[:, ec * 512:(ec + 1) * 512], start=(ec == 0), stop=(ec == ECH - 1),
                   tp=(0, 0))
                mmv = op("tensor", "mm", out=pb[6][32:33, 0:512], lhsT=ONESM[:, 0:1],
                         rhs=A2[:, 8192 + ec * 512: 8192 + (ec + 1) * 512], start=(ec == 0),
                         stop=(ec == ECH - 1), tp=(0, 32), inc=True)
            bcv = ln_tail(mmv, 512)
            for ec in range(ECH):
                src = xr[:, ec * TOWN + tt * 512: ec * TOWN + (tt + 1) * 512]
                W("vector", s_te, bcv)
                op("vector", "tt", out=STAT[:, 1024:1536], in0=src, in1=pb[0][:, 0:512], op=ALU.mult)
                if to_ag:
                    dst = A2[:, 8192 + ec * 512: 8192 + (ec + 1) * 512]
                else:
                    dst = A2[:, ec * 512:(ec + 1) * 512]
                vv = op("vector", "tt", out=dst, in0=STAT[:, 1024:1536], in1=pb[1][:, 0:512],
                        op=ALU.subtract, inc=True)
                if to_ag:
                    W("sync", s_ve, vv)
                    dma("sync", agi[lay][ec * 128:(ec + 1) * 128, tt * 512:(tt + 1) * 512], dst)
            barrier()

        def emit_qkv(lay):
            nw = 512 if lay == 0 else 384
            noc = 4 if lay == 0 else 3
            vstride = 128 if lay == 0 else 65
            vdw = 128 if lay == 0 else 64
            if lay == 0:
                wv = qkv0_wv
                W("tensor", s_gq, wv)
            else:
                wv = dma("gpsimd", P4[:, 0:ECH * nw].rearrange("p (c f) -> p c f", f=nw),
                         wqkv[lay].rearrange("(c p) f -> p c f", p=128))
                W("tensor", s_gq, wv)
            if lay == 1:
                mz = op("gpsimd", "memset", ap=A1[:, 24576:24576 + 65 * 64], val=1.0, inc=True)
                W("vector", s_ge, mz)
            slot_rv = {0: 0, 1: 0}
            slot_tv = {0: 0, 1: 0}
            last_ev = {}
            last_mm = {0: 0, 1: 0}
            last_rope = 0
            last_tr = {}
            raw32_rv = 0
            for tt in range(NTT):
                buf = tt % 2
                xh = A2[:, buf * 8192:(buf + 1) * 8192]
                r, c0 = tt // 2, (tt % 2) * 512
                W("sync", s_te, last_mm[buf])
                lv = dma("sync", xh.rearrange("p (c t) -> p c t", t=512),
                         ago[lay][E * r:E * (r + 1), c0:c0 + 512].rearrange("(c p) t -> p c t", p=128))
                W("sync", s_ve, last_rope)
                dma("sync", CSb[:, 0:512], cs[lay][0:128, tt * 512:(tt + 1) * 512])
                cv = dma("sync", CSb[:, 512:1024], cs[lay][128:256, tt * 512:(tt + 1) * 512])
                W("tensor", s_sy, lv)
                mmv = {}
                for oc in range(noc):
                    W("tensor", s_se, last_ev.get(oc, 0))
                    for ec in range(ECH):
                        mmv[oc] = op("tensor", "mm", out=pb[oc][:, 0:512],
                                     lhsT=P4[:, ec * nw + oc * 128: ec * nw + (oc + 1) * 128],
                                     rhs=xh[:, ec * 512:(ec + 1) * 512],
                                     start=(ec == 0), stop=(ec == ECH - 1),
                                     inc=(ec == ECH - 1))
                last_mm[buf] = mmv[noc - 1]
                for oc in range(noc):
                    is_v0 = (lay == 0 and oc == 3)
                    W("scalar", s_te, mmv[oc])
                    if is_v0:
                        W("scalar", s_ve, raw32_rv)
                        ev = op("scalar", "act", out=P3[:, 1024:1536], in_=pb[oc][:, 0:512],
                                func=AF.Copy, inc=True)
                        last_ev[oc] = ev
                        W("tensor", s_se, ev)
                        for b in range(4):
                            tb = pb[6 + (b % 2)]
                            W("tensor", s_ve, last_tr.get(b % 2, 0))
                            tv = op("tensor", "tr", out=tb[:, 0:128],
                                    in_=P3[:, 1024 + b * 128:1024 + (b + 1) * 128],
                                    ident=IDF[:, 0:128], inc=True)
                            W("vector", s_te, tv)
                            n = tt * 4 + b
                            cpv = op("vector", "tc",
                                     out=A1[:, 24576 + n * 128: 24576 + (n + 1) * 128],
                                     in_=tb[:, 0:128], inc=True)
                            last_tr[b % 2] = cpv
                            raw32_rv = cpv
                        continue
                    raw = STATB[:, (oc % 2) * 512:(oc % 2) * 512 + 512]
                    W("scalar", s_ve, slot_rv[oc % 2])
                    W("scalar", s_te, slot_tv[oc % 2])
                    ev = op("scalar", "act", out=raw, in_=pb[oc][:, 0:512], func=AF.Copy, inc=True)
                    last_ev[oc] = ev
                    rows = 64 if (lay == 1 and oc == 2) else 128
                    sw = pb[4 + (oc % 2)]
                    W("tensor", s_se, ev)
                    W("tensor", s_ve, slot_rv[oc % 2])
                    swv = op("tensor", "mm", out=sw[:, 0:512], lhsT=SWAP[lay], rhs=raw,
                             start=True, stop=True, inc=True)
                    slot_tv[oc % 2] = swv
                    C = CSb[0:rows, 0:512]
                    S = CSb[0:rows, 512:1024]
                    W("vector", s_sy, cv)
                    W("vector", s_se, ev)
                    W("vector", s_te, swv)
                    t1 = P3[0:rows, 0:512]
                    t2 = P3[0:rows, 512:1024]
                    op("vector", "tt", out=t1, in0=raw[0:rows, :], in1=C, op=ALU.mult)
                    op("vector", "tt", out=t2, in0=sw[0:rows, 0:512], in1=S, op=ALU.mult)
                    if lay == 1 and oc == 2:
                        dst = A1[0:64, 16384 + tt * 512: 16384 + (tt + 1) * 512]
                    else:
                        dst = A1[:, oc * 8192 + tt * 512: oc * 8192 + (tt + 1) * 512]
                    rv = op("vector", "tt", out=dst, in0=t1, in1=t2, op=ALU.add, inc=True)
                    slot_rv[oc % 2] = rv
                    last_rope = rv
                    if lay == 1 and oc == 2:
                        W("scalar", s_ve, raw32_rv)
                        ev2 = op("scalar", "act", out=P3[64:128, 1024:1536],
                                 in_=pb[oc][64:128, 0:512], func=AF.Copy, inc=True)
                        last_ev[oc] = ev2
                        W("tensor", s_se, ev2)
                        for b in range(4):
                            tb = pb[6 + (b % 2)]
                            W("tensor", s_ve, last_tr.get(b % 2, 0))
                            tv = op("tensor", "tr", out=tb[:, 0:64],
                                    in_=P3[64:128, 1024 + b * 128:1024 + (b + 1) * 128],
                                    ident=IDF[64:128, 64:128], tp=(64, 0), inc=True)
                            W("vector", s_te, tv)
                            n = tt * 4 + b
                            cpv = op("vector", "tc",
                                     out=A1[:, 24576 + n * 65: 24576 + n * 65 + 64],
                                     in_=tb[:, 0:64], inc=True)
                            last_tr[b % 2] = cpv
                            raw32_rv = cpv
            barrier()
            if lay == 1:
                dv = dma("sync", A1[64:128, 16384:24576], A1[0:64, 16384:24576])
                for eng in ENGS:
                    W(eng, s_sy, dv)

        def emit_bias(lay):
            scale = SC0 if lay == 0 else SC1
            krows = 128 if lay == 0 else 64
            srcs = [(A1[:, 0:8192], 128), (A1[:, 8192:16384], 128),
                    (A1[0:krows, 16384:24576], krows)]
            last_mm = 0
            for si, (src, rows) in enumerate(srcs):
                for tt in range(NTT):
                    sq = P4[0:rows, 0:512]
                    W("gpsimd", s_te, last_mm)
                    gv = op("gpsimd", "tt", out=sq, in0=src[:, tt * 512:(tt + 1) * 512],
                            in1=src[:, tt * 512:(tt + 1) * 512], op=ALU.mult, inc=True)
                    W("tensor", s_ge, gv)
                    last_mm = op("tensor", "mm", out=pb[6][0:1, 0:512], lhsT=ONESM[0:rows, 0:1],
                                 rhs=sq, start=True, stop=True, inc=True)
                    W("vector", s_te, last_mm)
                    v = op("vector", "tc", out=STAT[0:1, 0:512], in_=pb[6][0:1, 0:512], inc=True)
                    W("vector", s_ve, v)
                    v = op("vector", "rmax", out=STAT[0:1, 1024 + si * 16 + tt: 1025 + si * 16 + tt],
                           in_=STAT[0:1, 0:512], inc=True)
                    W("vector", s_ve, v)
                v = op("vector", "rmax", out=STAT[0:1, 1072 + si:1073 + si],
                       in_=STAT[0:1, 1024 + si * 16:1024 + si * 16 + 16], inc=True)
                W("vector", s_ve, v)
            for c in range(2):
                v = op("vector", "tt", out=STAT[0:1, 1076 + c:1077 + c],
                       in0=STAT[0:1, 1072 + c:1073 + c], in1=STAT[0:1, 1074:1075],
                       op=ALU.mult, inc=True)
                W("scalar", s_ve, v)
                op("scalar", "act", out=STAT[0:1, 1080 + c:1081 + c],
                   in_=STAT[0:1, 1076 + c:1077 + c], func=AF.Ln, bias=0.0, scale=1.0)
                v = op("scalar", "act", out=STAT[0:1, 1084 + c:1085 + c],
                       in_=STAT[0:1, 1080 + c:1081 + c], func=AF.Exp, bias=0.0, scale=0.5, inc=True)
                W("vector", s_se, v)
                v = op("vector", "ts", out=STAT[0:1, 1088 + c:1089 + c],
                       in_=STAT[0:1, 1084 + c:1085 + c], s=-scale, op=ALU.mult, inc=True)
                W("vector", s_ve, v)
                v = op("vector", "tc", out=STATB[0:1, 1024 + c:1025 + c],
                       in_=STAT[0:1, 1088 + c:1089 + c], inc=True)
                W("vector", s_ve, v)
            W("tensor", s_ve, s_ve.n)
            v = op("tensor", "mm", out=pb[6][:, 0:2], lhsT=ONESM[0:1, 0:128],
                   rhs=STATB[0:1, 1024:1026], start=True, stop=True, inc=True)
            W("vector", s_te, v)
            op("vector", "tc", out=BIAS[:, 0:2], in_=pb[6][:, 0:2], inc=True)
            barrier()

        def emit_attn_l0():
            exp_done = [0, 0]
            av_mm = [0, 0]
            av_mm_old = [0, 0]
            av_read = [0, 0]
            sums_read = 0
            last_norm = [0, 0]
            bc_done = [0, 0]
            dma_done = [0, 0]

            def emit_av(pm, pmi, pblo, pbhi, plast, pexp, ppar):
                for h in range(2):
                    W("tensor", s_se, pexp[h])
                    if pmi == 0:
                        W("tensor", s_se, av_read[h])
                        W("tensor", s_se, sums_read)
                    vap = A1[:, 24576 + pm * 128:24576 + (pm + 1) * 128]
                    pt = P4[:, ppar * 2048 + h * 1024 + pblo: ppar * 2048 + h * 1024 + pbhi]
                    op("tensor", "mm", out=AVB[:, h * 512 + pblo: h * 512 + pbhi],
                       lhsT=vap, rhs=pt, start=(pmi == 0), stop=plast)
                    av_mm[h] = op("tensor", "mm",
                                  out=AVB[32 * h:32 * h + 1, 1024 + pblo:1024 + pbhi],
                                  lhsT=ONESM[:, 0:1], rhs=pt, start=(pmi == 0), stop=plast,
                                  tp=(0, 32 * h), inc=True)

            for j in range(NTT):
                if j >= 1:
                    ms = [4 * j - 1] + [m for m in range(max(0, 4 * j - 4), 4 * j + 4)
                                        if m != 4 * j - 1]
                else:
                    ms = list(range(0, 4))
                pend = None
                for mi, m in enumerate(ms):
                    smin = max(m, 4 * j)
                    smax = min(m + 4, 4 * j + 3)
                    blo = (smin - 4 * j) * 128
                    bhi = (smax - 4 * j) * 128 + 128
                    last = (mi == len(ms) - 1)
                    par = mi % 2
                    smv = {}
                    for h in range(2):
                        W("tensor", s_se, exp_done[h])
                        if mi == 0:
                            W("tensor", s_ve, last_norm[h])
                        kap = A1[:, 16384 + m * 128:16384 + (m + 1) * 128]
                        qap = A1[:, h * 8192 + j * 512 + blo: h * 8192 + j * 512 + bhi]
                        smv[h] = op("tensor", "mm", out=SAB[:, h * 512 + blo: h * 512 + bhi],
                                    lhsT=kap, rhs=qap, start=True, stop=True, inc=True)
                    if pend is not None:
                        av_mm_old[:] = av_mm
                        emit_av(*pend)
                    evals = [0, 0]
                    for h in range(2):
                        mv = None
                        if 4 * j <= m + 4 <= 4 * j + 3:
                            c0 = (m + 4 - 4 * j) * 128
                            W("vector", s_te, smv[h])
                            mv = op("vector", "tt", out=SAB[:, h * 512 + c0: h * 512 + c0 + 128],
                                    in0=SAB[:, h * 512 + c0: h * 512 + c0 + 128],
                                    in1=EDGE, op=ALU.add, inc=True)
                        if m >= 4 * j:
                            c0 = (m - 4 * j) * 128
                            W("vector", s_te, smv[h])
                            mv = op("vector", "tt", out=SAB[:, h * 512 + c0: h * 512 + c0 + 128],
                                    in0=SAB[:, h * 512 + c0: h * 512 + c0 + 128],
                                    in1=TRI, op=ALU.add, inc=True)
                        W("scalar", s_te, smv[h])
                        W("scalar", s_te, av_mm_old[h])
                        if mv:
                            W("scalar", s_ve, mv)
                        pt = P4[:, par * 2048 + h * 1024 + blo: par * 2048 + h * 1024 + bhi]
                        evals[h] = op("scalar", "act", out=pt, in_=SAB[:, h * 512 + blo: h * 512 + bhi],
                                      func=AF.Exp, bias=BIAS[:, h:h + 1], scale=SC0, inc=True)
                        exp_done[h] = evals[h]
                    pend = (m, mi, blo, bhi, last, list(evals), par)
                emit_av(*pend)
                jb, jc = j // 2, (j % 2) * 512
                W("scalar", s_te, max(av_mm))
                W("scalar", s_te, max(bc_done))
                op("scalar", "act", out=STAT[0:1, 0:512], in_=AVB[0:1, 1024:1536],
                   func=AF.Ln, bias=0.0, scale=1.0)
                op("scalar", "act", out=STAT[32:33, 0:512], in_=AVB[32:33, 1024:1536],
                   func=AF.Ln, bias=0.0, scale=1.0)
                op("scalar", "act", out=STATB[0:1, 0:512], in_=STAT[0:1, 0:512],
                   func=AF.Exp, bias=0.0, scale=-1.0)
                sums_read = op("scalar", "act", out=STATB[32:33, 0:512], in_=STAT[32:33, 0:512],
                               func=AF.Exp, bias=0.0, scale=-1.0, inc=True)
                for h in range(2):
                    W("tensor", s_se, sums_read)
                    W("tensor", s_ve, last_norm[h])
                    bc_done[h] = op("tensor", "mm", out=SCD[:, h * 512:(h + 1) * 512],
                                    lhsT=ONESM[32 * h:32 * h + 1, 0:128],
                                    rhs=STATB[32 * h:32 * h + 1, 0:512], start=True, stop=True,
                                    tp=(32 * h, 0), inc=True)
                    W("scalar", s_te, bc_done[h])
                    W("scalar", s_ve, last_norm[h])
                    av_read[h] = op("scalar", "act",
                                    out=P4[:, 4096 + h * 512: 4096 + (h + 1) * 512],
                                    in_=AVB[:, h * 512:(h + 1) * 512], func=AF.Copy, inc=True)
                    W("vector", s_se, av_read[h])
                    W("vector", s_te, bc_done[h])
                    W("vector", s_sy, dma_done[h])
                    last_norm[h] = op("vector", "tt",
                                      out=P4[:, 5120 + h * 512: 5120 + (h + 1) * 512],
                                      in0=P4[:, 4096 + h * 512: 4096 + (h + 1) * 512],
                                      in1=SCD[:, h * 512:(h + 1) * 512], op=ALU.mult, inc=True)
                    W("sync", s_ve, last_norm[h])
                    dma_done[h] = dma("sync",
                                      avi[0][jb * 256 + h * 128: jb * 256 + (h + 1) * 128,
                                             jc:jc + 512],
                                      P4[:, 5120 + h * 512: 5120 + (h + 1) * 512])
            barrier()

        def emit_attn_l1():
            exp_done = [0, 0]
            av_mm = [0, 0, 0, 0]
            av_mm_old = [0, 0, 0, 0]
            av_read = 0
            last_norm = 0
            last_dma = 0
            bc_done = [0, 0, 0, 0]
            SP = [SAB, SCD]

            def emit_av(pn, poff, plast, pexp):
                ppar = pn % 2
                for h in range(4):
                    c = h // 2
                    W("tensor", s_se, pexp[c])
                    if pn == 0:
                        W("tensor", s_se, av_read)
                    vap = A1[:, 24576 + pn * 65: 24576 + pn * 65 + 65]
                    base = ppar * 4096 + c * 1024 + (h % 2) * 512
                    pt = P4[:, base + poff: base + 512]
                    av_mm[h] = op("tensor", "mm",
                                  out=AVB[0:65, h * 512 + poff: h * 512 + 512],
                                  lhsT=vap, rhs=pt, start=(pn == 0), stop=plast, inc=True)

            for j in range(NTT):
                nmax = 4 * j + 3
                pend = None
                for n in range(nmax + 1):
                    diag = n >= 4 * j
                    off = (n - 4 * j) * 128 if diag else 0
                    last = (n == nmax)
                    par = n % 2
                    smv = {}
                    for h in range(4):
                        c = h // 2
                        rb = (h % 2) * 64
                        W("tensor", s_se, exp_done[c])
                        if n == 0:
                            W("tensor", s_ve, last_norm)
                        kap = A1[rb:rb + 64, 16384 + n * 128:16384 + (n + 1) * 128]
                        qap = A1[rb:rb + 64, c * 8192 + j * 512 + off: c * 8192 + (j + 1) * 512]
                        smv[h] = op("tensor", "mm",
                                    out=SP[c][:, (h % 2) * 512 + off:(h % 2) * 512 + 512],
                                    lhsT=kap, rhs=qap, start=True, stop=True,
                                    tp=(rb, 0), inc=True)
                    if pend is not None:
                        av_mm_old[:] = av_mm
                        emit_av(*pend)
                    evals = [0, 0]
                    for c in range(2):
                        mv = None
                        if diag:
                            for hh in range(2):
                                W("vector", s_te, smv[2 * c + hh])
                                mv = op("vector", "tt",
                                        out=SP[c][:, hh * 512 + off: hh * 512 + off + 128],
                                        in0=SP[c][:, hh * 512 + off: hh * 512 + off + 128],
                                        in1=TRI, op=ALU.add, inc=True)
                        W("scalar", s_te, smv[2 * c + 1])
                        W("scalar", s_te, av_mm_old[2 * c])
                        W("scalar", s_te, av_mm_old[2 * c + 1])
                        if mv:
                            W("scalar", s_ve, mv)
                        base = par * 4096 + c * 1024
                        if off == 0:
                            evals[c] = op("scalar", "act", out=P4[:, base: base + 1024],
                                          in_=SP[c][:, 0:1024], func=AF.Exp,
                                          bias=BIAS[:, c:c + 1], scale=SC1, inc=True)
                        else:
                            op("scalar", "act", out=P4[:, base + off: base + 512],
                               in_=SP[c][:, off:512], func=AF.Exp,
                               bias=BIAS[:, c:c + 1], scale=SC1)
                            evals[c] = op("scalar", "act", out=P4[:, base + 512 + off: base + 1024],
                                          in_=SP[c][:, 512 + off:1024], func=AF.Exp,
                                          bias=BIAS[:, c:c + 1], scale=SC1, inc=True)
                        exp_done[c] = evals[c]
                    pend = (n, off, last, list(exp_done))
                emit_av(*pend)
                jb, jc = j // 2, (j % 2) * 512
                for h in range(4):
                    c = h // 2
                    avb = AVB[:, h * 512:(h + 1) * 512]
                    W("scalar", s_te, av_mm[h])
                    W("scalar", s_te, bc_done[h])
                    op("scalar", "act", out=STAT[64:65, h * 512:(h + 1) * 512],
                       in_=avb[64:65, 0:512], func=AF.Ln, bias=0.0, scale=1.0)
                    rcpv = op("scalar", "act", out=STATB[64:65, h * 512:(h + 1) * 512],
                              in_=STAT[64:65, h * 512:(h + 1) * 512], func=AF.Exp,
                              bias=0.0, scale=-1.0, inc=True)
                    W("scalar", s_ve, last_norm)
                    av_read = op("scalar", "act", out=STATB[0:64, 0:512],
                                 in_=avb[0:64, 0:512], func=AF.Copy, inc=True)
                    W("tensor", s_se, rcpv)
                    W("tensor", s_se, exp_done[c])
                    bc_done[h] = op("tensor", "mm",
                                    out=SP[c][0:64, (h % 2) * 512:(h % 2) * 512 + 512],
                                    lhsT=ONESM[64:65, 0:64],
                                    rhs=STATB[64:65, h * 512:(h + 1) * 512],
                                    start=True, stop=True, tp=(64, 0), inc=True)
                    W("vector", s_se, av_read)
                    W("vector", s_te, bc_done[h])
                    W("vector", s_sy, last_dma)
                    last_norm = op("vector", "tt", out=STATB[0:64, 512:1024],
                                   in0=STATB[0:64, 0:512],
                                   in1=SP[c][0:64, (h % 2) * 512:(h % 2) * 512 + 512],
                                   op=ALU.mult, inc=True)
                    W("sync", s_ve, last_norm)
                    last_dma = dma("sync",
                                   avi[1][jb * 256 + c * 128 + (h % 2) * 64:
                                          jb * 256 + c * 128 + (h % 2) * 64 + 64,
                                          jc:jc + 512],
                                   STATB[0:64, 512:1024])
            barrier()

        def emit_out_ln_add(tt, mms):
            evac_v = {}
            store_v = {}
            cast_v = {}
            stat_v = None
            pending = []

            def flush_stat(oc):
                nonlocal stat_v
                cv, gv = cast_v[oc]
                W("tensor", s_ve, cv)
                W("tensor", s_ge, gv)
                s = (oc % 2) * 512
                op("tensor", "mm", out=pb[6][0:1, 0:512], lhsT=ONESM[:, 0:1],
                   rhs=STATB[:, s:s + 512], start=(oc == 0), stop=(oc == ECH - 1), tp=(0, 0))
                stat_v = op("tensor", "mm", out=pb[6][32:33, 0:512], lhsT=ONESM[:, 0:1],
                            rhs=STATB[:, 1024 + s:1024 + s + 512], start=(oc == 0),
                            stop=(oc == ECH - 1), tp=(0, 32), inc=True)

            for oc in range(ECH):
                mv = mms(oc)
                W("scalar", s_te, mv)
                if oc >= 2:
                    W("scalar", s_sy, store_v[oc - 2])
                slot = P3[:, 1024 + (oc % 2) * 512: 1024 + (oc % 2) * 512 + 512]
                evac_v[oc] = op("scalar", "act", out=slot, in_=pb[4 + oc % 2][:, 0:512],
                                func=AF.Copy, inc=True)
                W("sync", s_se, evac_v[oc])
                store_v[oc] = dma("sync", aod[oc * 128:(oc + 1) * 128, :], slot)
                s = (oc % 2) * 512
                W("vector", s_se, evac_v[oc])
                if oc >= 2:
                    W("vector", s_te, stat_v)
                cv = op("vector", "tc", out=STATB[:, s:s + 512], in_=slot, inc=True)
                W("gpsimd", s_se, evac_v[oc])
                if oc >= 2:
                    W("gpsimd", s_te, stat_v)
                gv = op("gpsimd", "tt", out=STATB[:, 1024 + s:1024 + s + 512], in0=slot,
                        in1=slot, op=ALU.mult, inc=True)
                cast_v[oc] = (cv, gv)
                if oc >= 1:
                    flush_stat(oc - 1)
            flush_stat(ECH - 1)
            bcv = ln_tail(stat_v, 512)
            bcv = ln_tail(stat_v, 512)
            lv = {}
            for ec in range(ECH):
                buf = ec % 2
                aslot = P3[:, 2048 + buf * 512: 2048 + buf * 512 + 512]
                if ec >= 2:
                    W("sync", s_ve, lv[ec - 2])
                adv = dma("sync", aslot, aod[ec * 128:(ec + 1) * 128, :])
                W("vector", s_sy, adv)
                W("vector", s_te, bcv)
                op("vector", "tt", out=STAT[:, 1024:1536], in0=aslot, in1=pb[0][:, 0:512],
                   op=ALU.mult)
                op("vector", "tt", out=STAT[:, 1536:2048], in0=STAT[:, 1024:1536],
                   in1=pb[1][:, 0:512], op=ALU.subtract)
                dst = xr[:, ec * TOWN + tt * 512: ec * TOWN + (tt + 1) * 512]
                lv[ec] = op("vector", "tt", out=dst, in0=dst, in1=STAT[:, 1536:2048],
                            op=ALU.add, inc=True)
            barrier()

        def emit_wo(lay):
            W("sync", s_cast, cast_done[(f"wo{lay}", 1)])
            wv = dma("sync", A1[:, 0:32768].rearrange("p (dc e) -> p dc e", e=2048),
                     wob[lay].rearrange("(dc p) e -> p dc e", p=128))
            for tt in range(2):
                buf = tt % 2
                avt = A2[:, buf * 8192: (buf + 1) * 8192]
                lv = dma("sync", avt.rearrange("p (dc t) -> p dc t", t=512),
                         avo[lay][:, tt * 512:(tt + 1) * 512].rearrange("(dc p) t -> p dc t", p=128))

                def mms(oc, avt=avt, lv=lv, wv=wv):
                    W("tensor", s_sy, lv)
                    W("tensor", s_sy, wv)
                    v = None
                    for dc in range(ECH):
                        v = op("tensor", "mm", out=pb[4 + oc % 2][:, 0:512],
                               lhsT=A1[:, dc * 2048 + oc * 128: dc * 2048 + (oc + 1) * 128],
                               rhs=avt[:, dc * 512:(dc + 1) * 512],
                               start=(dc == 0), stop=(dc == ECH - 1), inc=(dc == ECH - 1))
                    return v

                emit_out_ln_add(tt, mms)

        def emit_ffn(lay):
            for tt in range(2):
                emit_ln_pre(lay, to_ag=False, tt=tt)
                if lay == 0:
                    order = list(range(HCH))
                else:
                    order = [c + 16 * m for c in range(16) for m in range(4)]
                wdma_v = {}
                humul_v = {}
                W1SLOT = [A2[:, 8192:12288], A2[:, 12288:16384], P4[:, 4096:8192]]
                for i, hc in enumerate(order):
                    slot = i % 3
                    if i >= 3:
                        W("gpsimd", s_te, wdma_v[i - 3][1])
                    wv = dma("gpsimd", W1SLOT[slot], w1[lay][hc * 128:(hc + 1) * 128, :])
                    gb, ub = pb[2 * (i % 2)], pb[2 * (i % 2) + 1]
                    W("tensor", s_gq, wv)
                    if i >= 2:
                        W("tensor", s_se, humul_v[i - 2][0])
                        W("tensor", s_ve, humul_v[i - 2][1])
                    mv = None
                    wt = W1SLOT[slot]
                    for ec in range(ECH):
                        op("tensor", "mm", out=gb[:, 0:512], lhsT=wt[:, ec * 256:ec * 256 + 128],
                           rhs=A2[:, ec * 512:(ec + 1) * 512], start=(ec == 0), stop=(ec == ECH - 1))
                        mv = op("tensor", "mm", out=ub[:, 0:512],
                                lhsT=wt[:, ec * 256 + 128:ec * 256 + 256],
                                rhs=A2[:, ec * 512:(ec + 1) * 512], start=(ec == 0),
                                stop=(ec == ECH - 1), inc=(ec == ECH - 1))
                    wdma_v[i] = (wv, mv)
                    W("scalar", s_te, mv)
                    if i >= 2:
                        W("scalar", s_ve, humul_v[i - 2][1])
                    s2 = i % 2
                    sv = op("scalar", "act", out=P4[:, s2 * 512: (s2 + 1) * 512],
                            in_=gb[:, 0:512], func=AF.Silu, inc=True)
                    W("vector", s_te, mv)
                    W("vector", s_se, sv)
                    uv = op("vector", "tc", out=P4[:, 1024 + s2 * 512: 1024 + (s2 + 1) * 512],
                            in_=ub[:, 0:512], inc=True)
                    hv = op("vector", "tt", out=A1[:, hc * 512:(hc + 1) * 512],
                            in0=P4[:, s2 * 512:(s2 + 1) * 512],
                            in1=P4[:, 1024 + s2 * 512: 1024 + (s2 + 1) * 512],
                            op=ALU.mult, inc=True)
                    humul_v[i] = (sv, hv)
                    if lay == 1 and i % 4 == 3:
                        emit_sparsify(order[i - 3:i + 1], hv)
                barrier()
                wdown_v = {}
                mm_done = {}

                def mkmms():
                    W2SLOT = [A2[:, 0:8192], A2[:, 8192:16384], P4[:, 0:8192]]

                    def mms(oc):
                        slot = oc % 3
                        if oc >= 3:
                            W("gpsimd", s_te, mm_done[oc - 3])
                        W("gpsimd", s_cast, cast_done[(f"w2{lay}", oc // 4)])
                        wv = dma("gpsimd", W2SLOT[slot], w2b[lay][oc * 128:(oc + 1) * 128, :])
                        W("tensor", s_gq, wv)
                        v = None
                        wt = W2SLOT[slot]
                        for hc in range(HCH):
                            v = op("tensor", "mm", out=pb[4 + oc % 2][:, 0:512],
                                   lhsT=wt[:, hc * 128:(hc + 1) * 128],
                                   rhs=A1[:, hc * 512:(hc + 1) * 512],
                                   start=(hc == 0), stop=(hc == HCH - 1), inc=(hc == HCH - 1))
                        mm_done[oc] = v
                        return v
                    return mms

                emit_out_ln_add(tt, mkmms())

        def emit_sparsify(quartet, last_hv):
            H = [A1[:, hc * 512:(hc + 1) * 512] for hc in quartet]
            sq = [P3[:, k * 512:(k + 1) * 512] for k in range(4)]
            s1 = STAT[:, 1024:1536]
            t1 = STAT[:, 1536:2048]
            s2 = STAT[:, 0:512]
            t2 = STAT[:, 512:1024]
            W("vector", s_ve, last_hv)
            for k in range(4):
                op("vector", "tt", out=sq[k], in0=H[k], in1=H[k], op=ALU.mult)
            op("vector", "tt", out=s1, in0=sq[0], in1=sq[1], op=ALU.max)
            op("vector", "tt", out=t1, in0=sq[0], in1=sq[1], op=ALU.min)
            op("vector", "tt", out=s2, in0=sq[2], in1=sq[3], op=ALU.max)
            op("vector", "tt", out=t2, in0=sq[2], in1=sq[3], op=ALU.min)
            op("vector", "tt", out=s1, in0=s1, in1=s2, op=ALU.min)
            op("vector", "tt", out=t1, in0=t1, in1=t2, op=ALU.max)
            op("vector", "tt", out=s1, in0=s1, in1=t1, op=ALU.max)
            for k in range(4):
                op("vector", "tt", out=t2, in0=sq[k], in1=s1, op=ALU.is_ge)
                op("vector", "tt", out=H[k], in0=H[k], in1=t2, op=ALU.mult,
                   inc=(k == 3))

        def tap_dram(name, src, shape, dtype):
            t = mktap(name, shape, dtype)
            if t is not None:
                v = dma("sync", t, src)
                for eng in ENGS:
                    W(eng, s_sy, v)

        def tap_xr(name):
            t = mktap(name, [E, TOWN], F32)
            if t is not None:
                v = None
                for ec in range(ECH):
                    v = dma("sync", t[ec * 128:(ec + 1) * 128, :], xr[:, ec * TOWN:(ec + 1) * TOWN])
                for eng in ENGS:
                    W(eng, s_sy, v)

        def tap_sbuf(name, src, shape, dtype):
            t = mktap(name, shape, dtype)
            if t is not None:
                v = dma("sync", t, src)
                for eng in ENGS:
                    W(eng, s_sy, v)

        step = [0]

        def go():
            step[0] += 1
            return step[0] <= upto

        for lay in range(2):
            if go():
                emit_ln_pre(lay, to_ag=True, tt=0)
                emit_ln_pre(lay, to_ag=True, tt=1)
                tap_dram(f"xh{lay}", agi[lay][:], [E, TOWN], BF16)
            if go():
                v = cc("AllGather", [agi[lay][:]], [ago[lay][:]])
                if lay == 0:
                    bg_cast_all()
                for eng in ENGS:
                    W(eng, s_cc, v)
            if go():
                emit_qkv(lay)
                tap_sbuf(f"qkv{lay}", A1[:, 0:32768], [128, 32768], BF16)
            if go():
                emit_bias(lay)
                tap_sbuf(f"bias{lay}", BIAS[:, 0:16], [128, 16], F32)
            if go():
                if lay == 0:
                    emit_attn_l0()
                else:
                    emit_attn_l1()
                tap_dram(f"av{lay}", avi[lay][:], [E, TOWN], BF16)
            if go():
                v = cc("AllToAll", [avi[lay][:]], [avo[lay][:]])
                for eng in ENGS:
                    W(eng, s_cc, v)
            if go():
                emit_wo(lay)
                tap_xr(f"xa{lay}")
            if go():
                emit_ffn(lay)
                tap_xr(f"xf{lay}")

        barrier()
        v = None
        for ec in range(ECH):
            v = dma("sync", y_out[ec * 128:(ec + 1) * 128, :], xr[:, ec * TOWN:(ec + 1) * TOWN])
        for eng in ENGS:
            W(eng, s_sy, v)

        block = st_.enter_context(nc.Block())
        with allow_ici_all_to_all():
            @block.sync
            def _(e):
                p.emit_into(e, "sync")

            @block.gpsimd
            def _(e):
                p.emit_into(e, "gpsimd")

            @block.tensor
            def _(e):
                p.emit_into(e, "tensor")

            @block.scalar
            def _(e):
                p.emit_into(e, "scalar")

            @block.vector
            def _(e):
                p.emit_into(e, "vector")

    ninstr = {e: len(p.q[e]) for e in ENGS}
    return nc, tap_aps, ninstr


def _host_tables():
    inv0 = (10000.0 ** (-(np.arange(0, 128, 2, dtype=np.float64)) / 128.0))
    ang0 = np.arange(TFULL, dtype=np.float64)[None, :] * inv0[:, None]
    c0 = np.cos(ang0)
    s0 = np.sin(ang0)
    cs0 = np.empty((256, TFULL), np.float32)
    cs0[0:64] = c0
    cs0[64:128] = c0
    cs0[128:192] = -s0
    cs0[192:256] = s0
    inv1 = (10000.0 ** (-(np.arange(0, 64, 2, dtype=np.float64)) / 64.0))
    ang1 = np.arange(TFULL, dtype=np.float64)[None, :] * inv1[:, None]
    c1 = np.cos(ang1)
    s1 = np.sin(ang1)
    cs1 = np.empty((256, TFULL), np.float32)
    for blk in range(2):
        b = blk * 64
        cs1[b + 0:b + 32] = c1
        cs1[b + 32:b + 64] = c1
        cs1[128 + b + 0:128 + b + 32] = -s1
        cs1[128 + b + 32:128 + b + 64] = s1
    return cs0, cs1


def _host_consts():
    pq = np.arange(128)
    tri = np.where(pq[:, None] <= pq[None, :], 0.0, -1e9).astype(np.float32)
    edge = np.where(pq[None, :] < pq[:, None], 0.0, -1e9).astype(np.float32)
    ident = np.eye(128, dtype=np.float32)
    sw0 = np.zeros((128, 128), np.float32)
    sw0[(pq + 64) % 128, pq] = 1.0
    sig1 = np.where(pq % 64 < 32, pq + 32, pq - 32)
    sw1 = np.zeros((128, 128), np.float32)
    sw1[sig1, pq] = 1.0
    return np.concatenate([tri, edge, ident, sw0, sw1], axis=0)


def _blocks(a, nblk, rows):
    R, C = a.shape
    assert C == nblk * rows
    return np.ascontiguousarray(a.reshape(R, nblk, rows).transpose(1, 0, 2).reshape(nblk * R, rows))


_BUILD_CACHE = {}


def _get_build(taps=(), upto=99):
    key = (tuple(sorted(taps)), upto)
    if key not in _BUILD_CACHE:
        _BUILD_CACHE[key] = build(taps, upto)
    return _BUILD_CACHE[key]


def kernel(L0_attn_x, L0_attn_Wq, L0_attn_Wk, L0_attn_Wv, L0_attn_Wo,
           L0_ffn_w_gate, L0_ffn_w_up, L0_ffn_w_down,
           L1_attn_wq, L1_attn_wk, L1_attn_wv, L1_attn_wo,
           L1_ffn_w_gate, L1_ffn_w_up, L1_ffn_w_down, _taps=(), _upto=99):
    f = np.float32
    x = np.asarray(L0_attn_x, f).reshape(TFULL, E)
    cs0, cs1 = _host_tables()
    consts = _host_consts()

    wq0T = np.asarray(L0_attn_Wq, f).T
    wk0T = np.asarray(L0_attn_Wk, f).T
    wv0T = np.asarray(L0_attn_Wv, f).T
    wo0 = np.ascontiguousarray(np.asarray(L0_attn_Wo, f).T)
    g0T = np.asarray(L0_ffn_w_gate, f).T
    u0T = np.asarray(L0_ffn_w_up, f).T
    d0T = np.asarray(L0_ffn_w_down, f).T
    def _packw1(gT, uT):
        blk = np.concatenate([gT.reshape(E, HCH, 128), uT.reshape(E, HCH, 128)],
                             axis=2).transpose(1, 0, 2)
        return np.ascontiguousarray(
            blk.reshape(HCH, 16, 128, 256).transpose(0, 2, 1, 3).reshape(HCH * 128, 4096))

    def _packw2(dT):
        blk = dT.reshape(HID, ECH, 128).transpose(1, 0, 2)
        return np.ascontiguousarray(
            blk.reshape(ECH, 64, 128, 128).transpose(0, 2, 1, 3).reshape(ECH * 128, 8192))

    w10 = _packw1(g0T, u0T)
    w20 = _packw2(d0T)

    wq1 = np.asarray(L1_attn_wq, f)
    wk1 = np.asarray(L1_attn_wk, f)
    wv1 = np.asarray(L1_attn_wv, f)
    wo1 = np.ascontiguousarray(np.asarray(L1_attn_wo, f))
    perm = 4 * (np.arange(HID) % 2048) + (np.arange(HID) // 2048)
    g1 = np.asarray(L1_ffn_w_gate, f)[:, perm]
    u1 = np.asarray(L1_ffn_w_up, f)[:, perm]
    d1 = np.asarray(L1_ffn_w_down, f)[perm, :]
    w11 = _packw1(g1, u1)
    w21 = _packw2(d1)

    in_maps = []
    for i in range(NCORES):
        kv0 = i // 2
        wqkv0 = np.ascontiguousarray(np.concatenate(
            [wq0T[:, 256 * i:256 * (i + 1)],
             wk0T[:, 128 * kv0:128 * (kv0 + 1)],
             wv0T[:, 128 * kv0:128 * (kv0 + 1)]], axis=1))
        wqkv1 = np.ascontiguousarray(np.concatenate(
            [wq1[:, 256 * i:256 * (i + 1)],
             wk1[:, 64 * i:64 * (i + 1)],
             wv1[:, 64 * i:64 * (i + 1)]], axis=1))
        x_own = np.ascontiguousarray(x[TOWN * i:TOWN * (i + 1), :].T)
        in_maps.append(dict(
            x_in=x_own, wqkv0=wqkv0, wo0=wo0, w10=w10, w20=w20,
            wqkv1=wqkv1, wo1=wo1, w11=w11, w21=w21,
            cs0=cs0, cs1=cs1, consts=consts))

    nc, tap_aps, _ = _get_build(_taps, _upto)
    res = run_bass_kernel_spmd(nc, in_maps, core_ids=list(range(NCORES)))
    outs = [np.asarray(res.results[i]["y_out"]).T for i in range(NCORES)]
    y = np.concatenate(outs, axis=0).reshape(1, TFULL, E)
    if _taps:
        kernel.tap_results = [
            {k: np.asarray(res.results[i]["tap_" + k]) for k in _taps}
            for i in range(NCORES)]
    return y


if __name__ == "__main__":
    nc, tap_aps, ninstr = build()
    print("instr counts:", ninstr, "total:", sum(ninstr.values()))


# version 12
# speedup vs baseline: 1.0675x; 1.0675x over previous
import contextlib
import math
import numpy as np
import ml_dtypes
import concourse.bass as bass
import concourse.mybir as mybir
from concourse.bass_utils import run_bass_kernel_spmd
from concourse.aot_env import allow_ici_all_to_all

F32 = mybir.dt.float32
BF16 = mybir.dt.bfloat16
AF = mybir.ActivationFunctionType
ALU = mybir.AluOpType
bfnp = ml_dtypes.bfloat16

NCORES = 8
E = 2048
ECH = 16
TFULL = 8192
TOWN = 1024
NTT = 16
HID = 8192
HCH = 64
SC0 = 1.0 / math.sqrt(128.0)
SC1 = 0.125

ENGS = ("sync", "gpsimd", "tensor", "scalar", "vector")


class Sem:
    def __init__(self, h):
        self.h = h
        self.n = 0

    def inc(self, k=1):
        self.n += k
        return self.n


class Prog:
    def __init__(self):
        self.q = {e: [] for e in ENGS}

    def emit_into(self, eng, name):
        for kind, a in self.q[name]:
            if kind == "wait":
                eng.wait_ge(a[0], a[1])
                continue
            if kind == "dma":
                inst = eng.dma_start(out=a["out"], in_=a["in_"])
            elif kind == "mm":
                kw = dict(out=a["out"], lhsT=a["lhsT"], rhs=a["rhs"],
                          start=a["start"], stop=a["stop"])
                if a.get("tp") is not None:
                    kw["tile_position"] = a["tp"]
                inst = eng.matmul(**kw)
            elif kind == "tr":
                kw = {}
                if a.get("tp") is not None:
                    kw["tile_position"] = a["tp"]
                inst = eng.transpose(a["out"], a["in_"], a["ident"], **kw)
            elif kind == "act":
                kw = dict(out=a["out"], in_=a["in_"], func=a["func"])
                if a.get("bias") is not None:
                    kw["bias"] = a["bias"]
                if a.get("scale") is not None:
                    kw["scale"] = a["scale"]
                inst = eng.activation(**kw)
            elif kind == "tt":
                inst = eng.tensor_tensor(out=a["out"], in0=a["in0"], in1=a["in1"], op=a["op"])
            elif kind == "tc":
                inst = eng.tensor_copy(out=a["out"], in_=a["in_"])
            elif kind == "ts":
                inst = eng.tensor_single_scalar(out=a["out"], in_=a["in_"], scalar=a["s"], op=a["op"])
            elif kind == "recipf":
                inst = eng.reciprocal_approx_fast(a["out"], a["in_"])
            elif kind == "rmax":
                inst = eng.reduce_max(a["out"], a["in_"], axis=mybir.AxisListType.X)
            elif kind == "memset":
                inst = eng.memset(a["ap"], a["val"])
            elif kind == "cc":
                inst = eng.collective_compute(a["kind"], a["op"], replica_groups=a["rg"],
                                              ins=a["ins"], outs=a["outs"])
            else:
                raise AssertionError(kind)
            if a.get("inc") is not None:
                s, k = a["inc"]
                inst.then_inc(s, k)


def build(taps=(), upto=99):
    taps = set(taps)
    nc = bass.Bass("TRN2", target_bir_lowering=False)

    x_in = nc.dram_tensor("x_in", [E, TOWN], F32, kind="ExternalInput").ap()
    wqkv = [nc.dram_tensor("wqkv0", [E, 512], F32, kind="ExternalInput").ap(),
            nc.dram_tensor("wqkv1", [E, 384], F32, kind="ExternalInput").ap()]
    wo = [nc.dram_tensor("wo0", [E, E], F32, kind="ExternalInput").ap(),
          nc.dram_tensor("wo1", [E, E], F32, kind="ExternalInput").ap()]
    w1 = [nc.dram_tensor("w10", [HCH * 128, 4096], F32, kind="ExternalInput").ap(),
          nc.dram_tensor("w11", [HCH * 128, 4096], F32, kind="ExternalInput").ap()]
    w2 = [nc.dram_tensor("w20", [16 * 128, 8192], F32, kind="ExternalInput").ap(),
          nc.dram_tensor("w21", [16 * 128, 8192], F32, kind="ExternalInput").ap()]
    w1b = [nc.dram_tensor(f"w1b{l}", [HCH * 128, 4096], BF16, kind="Internal").ap() for l in range(2)]
    w2b = [nc.dram_tensor(f"w2b{l}", [16 * 128, 8192], BF16, kind="Internal").ap() for l in range(2)]
    wob = [nc.dram_tensor(f"wob{l}", [E, E], BF16, kind="Internal").ap() for l in range(2)]
    wqb1 = nc.dram_tensor("wqb1", [E, 384], BF16, kind="Internal").ap()
    cs = [nc.dram_tensor("cs0", [256, TFULL], F32, kind="ExternalInput").ap(),
          nc.dram_tensor("cs1", [256, TFULL], F32, kind="ExternalInput").ap()]
    consts = nc.dram_tensor("consts", [5 * 128, 128], F32, kind="ExternalInput").ap()
    y_out = nc.dram_tensor("y_out", [E, TOWN], F32, kind="ExternalOutput").ap()

    agi = [[nc.dram_tensor(f"ag{l}i{t}", [E, 512], BF16, kind="Internal").ap() for t in range(2)]
           for l in range(2)]
    ago = [[nc.dram_tensor(f"ag{l}o{t}", [NCORES * E, 512], BF16, kind="Internal",
                           addr_space="Shared").ap() for t in range(2)] for l in range(2)]
    avi = [nc.dram_tensor(f"av{l}i", [E, TOWN], BF16, kind="Internal").ap() for l in range(2)]
    avo = [nc.dram_tensor(f"av{l}o", [E, TOWN], BF16, kind="Internal").ap() for l in range(2)]
    aod = nc.dram_tensor("aod", [E, 512], F32, kind="Internal").ap()

    tap_aps = {}

    def mktap(name, shape, dtype):
        if name in taps:
            tap_aps[name] = nc.dram_tensor("tap_" + name, shape, dtype, kind="ExternalOutput").ap()
        return tap_aps.get(name)

    st_ = contextlib.ExitStack()
    with st_:
        xr = st_.enter_context(nc.sbuf_tensor("xr", [128, ECH * TOWN], F32))
        A1 = st_.enter_context(nc.sbuf_tensor("A1", [128, 32768], BF16))
        A2 = st_.enter_context(nc.sbuf_tensor("A2", [128, 16384], BF16))
        P3 = st_.enter_context(nc.sbuf_tensor("P3", [128, 3072], F32))
        P4 = st_.enter_context(nc.sbuf_tensor("P4", [128, 8192], BF16))
        CONS = st_.enter_context(nc.sbuf_tensor("CONS", [128, 640], BF16))
        ONESM = st_.enter_context(nc.sbuf_tensor("ONESM", [128, 128], BF16))
        STAT = st_.enter_context(nc.sbuf_tensor("STAT", [128, 2048], F32))
        STATB = st_.enter_context(nc.sbuf_tensor("STATB", [128, 2048], BF16))
        CSb = st_.enter_context(nc.sbuf_tensor("CSb", [128, 1024], F32))
        BIAS = st_.enter_context(nc.sbuf_tensor("BIAS", [128, 16], F32))
        IDF = st_.enter_context(nc.sbuf_tensor("IDF", [128, 128], F32))

        SALL = st_.enter_context(nc.psum_tensor("SALL", [128, 2048], F32))
        SAB = SALL[:, 0:1024]
        SCD = SALL[:, 1024:2048]
        AVB = st_.enter_context(nc.psum_tensor("AVB", [128, 2048], F32))
        pb = [SAB[:, 0:512], SAB[:, 512:1024], SCD[:, 0:512], SCD[:, 512:1024],
              AVB[:, 0:512], AVB[:, 512:1024], AVB[:, 1024:1536], AVB[:, 1536:2048]]

        s_sy = Sem(st_.enter_context(nc.semaphore("s_sy")))
        s_gq = Sem(st_.enter_context(nc.semaphore("s_gq")))
        s_cc = Sem(st_.enter_context(nc.semaphore("s_cc")))
        s_te = Sem(st_.enter_context(nc.semaphore("s_te")))
        s_se = Sem(st_.enter_context(nc.semaphore("s_se")))
        s_ve = Sem(st_.enter_context(nc.semaphore("s_ve")))
        s_ge = Sem(st_.enter_context(nc.semaphore("s_ge")))
        s_cast = Sem(st_.enter_context(nc.semaphore("s_cast")))
        ALLSEMS = [s_sy, s_gq, s_cc, s_te, s_se, s_ve, s_ge]
        ESEM = {"tensor": s_te, "scalar": s_se, "vector": s_ve, "gpsimd": s_ge}

        p = Prog()

        def W(eng, sem, val):
            if val:
                p.q[eng].append(("wait", (sem.h, val)))

        def dma(eng, out, in_, sem=None):
            s = sem if sem is not None else (s_sy if eng == "sync" else s_gq)
            v = s.inc(16)
            p.q[eng].append(("dma", dict(out=out, in_=in_, inc=(s.h, 16))))
            return v

        def op(eng, kind, inc=False, **a):
            s = ESEM[eng]
            v = None
            if inc:
                v = s.inc(1)
                a["inc"] = (s.h, 1)
            p.q[eng].append((kind, a))
            return v

        def cc(kind, ins, outs):
            v = s_cc.inc(1)
            p.q["gpsimd"].append(("cc", dict(kind=kind, op=ALU.bypass,
                                             rg=[list(range(NCORES))], ins=ins, outs=outs,
                                             inc=(s_cc.h, 1))))
            return v

        def barrier():
            vals = [(s, s.n) for s in ALLSEMS]
            for eng in ENGS:
                for s, v in vals:
                    W(eng, s, v)

        TRI = CONS[:, 0:128]
        EDGE = CONS[:, 128:256]
        IDENT = CONS[:, 256:384]
        SWAP = [CONS[:, 384:512], CONS[:, 512:640]]

        dma("gpsimd", CONS[:].rearrange("p (n c) -> p n c", c=128),
            consts.rearrange("(n p) c -> p n c", p=128))
        dma("sync", IDF[:], consts[256:384, :])
        op("gpsimd", "memset", ap=ONESM[:], val=1.0, inc=True)
        qkv0_wv = dma("gpsimd", P4[:, 0:ECH * 512].rearrange("p (c f) -> p c f", f=512),
                      wqkv[0].rearrange("(c p) f -> p c f", p=128))
        for ec in range(ECH):
            dma("sync", xr[:, ec * TOWN:(ec + 1) * TOWN], x_in[ec * 128:(ec + 1) * 128, :])
        cast_done = {}

        def bg_cast(key, dst, srcten, nrows, piece_rows):
            for p0 in range(0, nrows, piece_rows):
                v = dma("gpsimd", dst[p0:p0 + piece_rows, :], srcten[p0:p0 + piece_rows, :],
                        sem=s_cast)
                cast_done[(key, p0 // piece_rows)] = v

        def bg_cast_all():
            bg_cast("wo0", wob[0], wo[0], E, 1024)
            bg_cast("wo1", wob[1], wo[1], E, 1024)
        barrier()

        def ln_tail(mm_v, ncols):
            A = STAT[0:1, 0:512][:, 0:ncols]
            B = STAT[0:1, 512:1024][:, 0:ncols]
            C = STAT[0:1, 1024:1536][:, 0:ncols]
            D = STAT[0:1, 1536:2048][:, 0:ncols]
            W("vector", s_te, mm_v)
            op("vector", "ts", out=A, in_=pb[6][0:1, 0:ncols], s=1.0 / E, op=ALU.mult)
            op("vector", "ts", out=B, in_=pb[6][32:33, 0:ncols], s=1.0 / E, op=ALU.mult)
            op("vector", "tt", out=C, in0=A, in1=A, op=ALU.mult)
            op("vector", "tt", out=B, in0=B, in1=C, op=ALU.subtract)
            v = op("vector", "ts", out=B, in_=B, s=1e-5, op=ALU.add, inc=True)
            W("scalar", s_ve, v)
            op("scalar", "act", out=B, in_=B, func=AF.Ln, bias=0.0, scale=1.0)
            sv = op("scalar", "act", out=D, in_=B, func=AF.Exp, bias=0.0, scale=-0.5, inc=True)
            W("vector", s_se, sv)
            op("vector", "tt", out=C, in0=A, in1=D, op=ALU.mult)
            op("vector", "tc", out=STATB[0:1, 0:ncols], in_=D)
            op("vector", "tc", out=A, in_=STATB[0:1, 0:ncols])
            op("vector", "tt", out=A, in0=D, in1=A, op=ALU.subtract)
            op("vector", "tc", out=STATB[0:1, 512:512 + ncols], in_=A)
            op("vector", "tc", out=STATB[0:1, 1024:1024 + ncols], in_=C)
            op("vector", "tc", out=B, in_=STATB[0:1, 1024:1024 + ncols])
            op("vector", "tt", out=B, in0=C, in1=B, op=ALU.subtract)
            v = op("vector", "tc", out=STATB[0:1, 1536:1536 + ncols], in_=B, inc=True)
            W("tensor", s_ve, v)
            op("tensor", "mm", out=pb[0][:, 0:ncols], lhsT=ONESM[0:1, 0:128],
               rhs=STATB[0:1, 0:ncols], start=True, stop=False)
            op("tensor", "mm", out=pb[0][:, 0:ncols], lhsT=ONESM[0:1, 0:128],
               rhs=STATB[0:1, 512:512 + ncols], start=False, stop=True)
            op("tensor", "mm", out=pb[1][:, 0:ncols], lhsT=ONESM[0:1, 0:128],
               rhs=STATB[0:1, 1024:1024 + ncols], start=True, stop=False)
            v = op("tensor", "mm", out=pb[1][:, 0:ncols], lhsT=ONESM[0:1, 0:128],
                   rhs=STATB[0:1, 1536:1536 + ncols], start=False, stop=True, inc=True)
            return v

        def emit_ln_pre(lay, to_ag, tt):
            vv = tv = None
            for ec in range(ECH):
                src = xr[:, ec * TOWN + tt * 512: ec * TOWN + (tt + 1) * 512]
                vv = op("vector", "tc", out=A2[:, ec * 512:(ec + 1) * 512], in_=src, inc=True)
                vv = op("vector", "tt", out=A2[:, 8192 + ec * 512: 8192 + (ec + 1) * 512],
                        in0=src, in1=src, op=ALU.mult, inc=True)
            W("tensor", s_ve, vv)
            mmv = None
            for ec in range(ECH):
                op("tensor", "mm", out=pb[6][0:1, 0:512], lhsT=ONESM[:, 0:1],
                   rhs=A2[:, ec * 512:(ec + 1) * 512], start=(ec == 0), stop=(ec == ECH - 1),
                   tp=(0, 0))
                mmv = op("tensor", "mm", out=pb[6][32:33, 0:512], lhsT=ONESM[:, 0:1],
                         rhs=A2[:, 8192 + ec * 512: 8192 + (ec + 1) * 512], start=(ec == 0),
                         stop=(ec == ECH - 1), tp=(0, 32), inc=True)
            bcv = ln_tail(mmv, 512)
            for ec in range(ECH):
                src = xr[:, ec * TOWN + tt * 512: ec * TOWN + (tt + 1) * 512]
                W("vector", s_te, bcv)
                op("vector", "tt", out=STAT[:, 1024:1536], in0=src, in1=pb[0][:, 0:512], op=ALU.mult)
                if to_ag:
                    dst = A2[:, 8192 + ec * 512: 8192 + (ec + 1) * 512]
                else:
                    dst = A2[:, ec * 512:(ec + 1) * 512]
                vv = op("vector", "tt", out=dst, in0=STAT[:, 1024:1536], in1=pb[1][:, 0:512],
                        op=ALU.subtract, inc=True)
                if to_ag:
                    W("sync", s_ve, vv)
                    dma("sync", agi[lay][tt][ec * 128:(ec + 1) * 128, :], dst)
            barrier()

        def emit_qkv(lay):
            nw = 512 if lay == 0 else 384
            noc = 4 if lay == 0 else 3
            vstride = 128 if lay == 0 else 65
            vdw = 128 if lay == 0 else 64
            if lay == 0:
                wv = qkv0_wv
                W("tensor", s_gq, wv)
            else:
                wv = dma("gpsimd", P4[:, 0:ECH * nw].rearrange("p (c f) -> p c f", f=nw),
                         wqkv[lay].rearrange("(c p) f -> p c f", p=128))
                W("tensor", s_gq, wv)
            if lay == 1:
                mz = op("gpsimd", "memset", ap=A1[:, 24576:24576 + 65 * 64], val=1.0, inc=True)
                W("vector", s_ge, mz)
            slot_rv = {0: 0, 1: 0}
            slot_tv = {0: 0, 1: 0}
            last_ev = {}
            last_mm = {0: 0, 1: 0}
            last_rope = 0
            last_tr = {}
            raw32_rv = 0
            for tt in range(NTT):
                buf = tt % 2
                xh = A2[:, buf * 8192:(buf + 1) * 8192]
                r, half = tt // 2, tt % 2
                W("sync", s_te, last_mm[buf])
                lv = dma("sync", xh.rearrange("p (c t) -> p c t", t=512),
                         ago[lay][half][E * r:E * (r + 1), :].rearrange("(c p) t -> p c t", p=128))
                W("sync", s_ve, last_rope)
                dma("sync", CSb[:, 0:512], cs[lay][0:128, tt * 512:(tt + 1) * 512])
                cv = dma("sync", CSb[:, 512:1024], cs[lay][128:256, tt * 512:(tt + 1) * 512])
                W("tensor", s_sy, lv)
                mmv = {}
                for oc in range(noc):
                    W("tensor", s_se, last_ev.get(oc, 0))
                    for ec in range(ECH):
                        mmv[oc] = op("tensor", "mm", out=pb[oc][:, 0:512],
                                     lhsT=P4[:, ec * nw + oc * 128: ec * nw + (oc + 1) * 128],
                                     rhs=xh[:, ec * 512:(ec + 1) * 512],
                                     start=(ec == 0), stop=(ec == ECH - 1),
                                     inc=(ec == ECH - 1))
                last_mm[buf] = mmv[noc - 1]
                for oc in range(noc):
                    is_v0 = (lay == 0 and oc == 3)
                    W("scalar", s_te, mmv[oc])
                    if is_v0:
                        W("scalar", s_ve, raw32_rv)
                        ev = op("scalar", "act", out=P3[:, 1024:1536], in_=pb[oc][:, 0:512],
                                func=AF.Copy, inc=True)
                        last_ev[oc] = ev
                        W("tensor", s_se, ev)
                        for b in range(4):
                            tb = pb[6 + (b % 2)]
                            W("tensor", s_ve, last_tr.get(b % 2, 0))
                            tv = op("tensor", "tr", out=tb[:, 0:128],
                                    in_=P3[:, 1024 + b * 128:1024 + (b + 1) * 128],
                                    ident=IDF[:, 0:128], inc=True)
                            W("vector", s_te, tv)
                            n = tt * 4 + b
                            cpv = op("vector", "tc",
                                     out=A1[:, 24576 + n * 128: 24576 + (n + 1) * 128],
                                     in_=tb[:, 0:128], inc=True)
                            last_tr[b % 2] = cpv
                            raw32_rv = cpv
                        continue
                    raw = STATB[:, (oc % 2) * 512:(oc % 2) * 512 + 512]
                    W("scalar", s_ve, slot_rv[oc % 2])
                    W("scalar", s_te, slot_tv[oc % 2])
                    ev = op("scalar", "act", out=raw, in_=pb[oc][:, 0:512], func=AF.Copy, inc=True)
                    last_ev[oc] = ev
                    rows = 64 if (lay == 1 and oc == 2) else 128
                    sw = pb[4 + (oc % 2)]
                    W("tensor", s_se, ev)
                    W("tensor", s_ve, slot_rv[oc % 2])
                    swv = op("tensor", "mm", out=sw[:, 0:512], lhsT=SWAP[lay], rhs=raw,
                             start=True, stop=True, inc=True)
                    slot_tv[oc % 2] = swv
                    C = CSb[0:rows, 0:512]
                    S = CSb[0:rows, 512:1024]
                    W("vector", s_sy, cv)
                    W("vector", s_se, ev)
                    W("vector", s_te, swv)
                    t1 = P3[0:rows, 0:512]
                    t2 = P3[0:rows, 512:1024]
                    op("vector", "tt", out=t1, in0=raw[0:rows, :], in1=C, op=ALU.mult)
                    op("vector", "tt", out=t2, in0=sw[0:rows, 0:512], in1=S, op=ALU.mult)
                    if lay == 1 and oc == 2:
                        dst = A1[0:64, 16384 + tt * 512: 16384 + (tt + 1) * 512]
                    else:
                        dst = A1[:, oc * 8192 + tt * 512: oc * 8192 + (tt + 1) * 512]
                    rv = op("vector", "tt", out=dst, in0=t1, in1=t2, op=ALU.add, inc=True)
                    slot_rv[oc % 2] = rv
                    last_rope = rv
                    if lay == 1 and oc == 2:
                        W("scalar", s_ve, raw32_rv)
                        ev2 = op("scalar", "act", out=P3[64:128, 1024:1536],
                                 in_=pb[oc][64:128, 0:512], func=AF.Copy, inc=True)
                        last_ev[oc] = ev2
                        W("tensor", s_se, ev2)
                        for b in range(4):
                            tb = pb[6 + (b % 2)]
                            W("tensor", s_ve, last_tr.get(b % 2, 0))
                            tv = op("tensor", "tr", out=tb[:, 0:64],
                                    in_=P3[64:128, 1024 + b * 128:1024 + (b + 1) * 128],
                                    ident=IDF[64:128, 64:128], tp=(64, 0), inc=True)
                            W("vector", s_te, tv)
                            n = tt * 4 + b
                            cpv = op("vector", "tc",
                                     out=A1[:, 24576 + n * 65: 24576 + n * 65 + 64],
                                     in_=tb[:, 0:64], inc=True)
                            last_tr[b % 2] = cpv
                            raw32_rv = cpv
            barrier()
            if lay == 1:
                dv = dma("sync", A1[64:128, 16384:24576], A1[0:64, 16384:24576])
                for eng in ENGS:
                    W(eng, s_sy, dv)

        def emit_bias(lay):
            scale = SC0 if lay == 0 else SC1
            krows = 128 if lay == 0 else 64
            srcs = [(A1[:, 0:8192], 128), (A1[:, 8192:16384], 128),
                    (A1[0:krows, 16384:24576], krows)]
            last_mm = 0
            for si, (src, rows) in enumerate(srcs):
                for tt in range(NTT):
                    sq = P4[0:rows, 0:512]
                    W("gpsimd", s_te, last_mm)
                    gv = op("gpsimd", "tt", out=sq, in0=src[:, tt * 512:(tt + 1) * 512],
                            in1=src[:, tt * 512:(tt + 1) * 512], op=ALU.mult, inc=True)
                    W("tensor", s_ge, gv)
                    last_mm = op("tensor", "mm", out=pb[6][0:1, 0:512], lhsT=ONESM[0:rows, 0:1],
                                 rhs=sq, start=True, stop=True, inc=True)
                    W("vector", s_te, last_mm)
                    v = op("vector", "tc", out=STAT[0:1, 0:512], in_=pb[6][0:1, 0:512], inc=True)
                    W("vector", s_ve, v)
                    v = op("vector", "rmax", out=STAT[0:1, 1024 + si * 16 + tt: 1025 + si * 16 + tt],
                           in_=STAT[0:1, 0:512], inc=True)
                    W("vector", s_ve, v)
                v = op("vector", "rmax", out=STAT[0:1, 1072 + si:1073 + si],
                       in_=STAT[0:1, 1024 + si * 16:1024 + si * 16 + 16], inc=True)
                W("vector", s_ve, v)
            for c in range(2):
                v = op("vector", "tt", out=STAT[0:1, 1076 + c:1077 + c],
                       in0=STAT[0:1, 1072 + c:1073 + c], in1=STAT[0:1, 1074:1075],
                       op=ALU.mult, inc=True)
                W("scalar", s_ve, v)
                op("scalar", "act", out=STAT[0:1, 1080 + c:1081 + c],
                   in_=STAT[0:1, 1076 + c:1077 + c], func=AF.Ln, bias=0.0, scale=1.0)
                v = op("scalar", "act", out=STAT[0:1, 1084 + c:1085 + c],
                       in_=STAT[0:1, 1080 + c:1081 + c], func=AF.Exp, bias=0.0, scale=0.5, inc=True)
                W("vector", s_se, v)
                v = op("vector", "ts", out=STAT[0:1, 1088 + c:1089 + c],
                       in_=STAT[0:1, 1084 + c:1085 + c], s=-scale, op=ALU.mult, inc=True)
                W("vector", s_ve, v)
                v = op("vector", "tc", out=STATB[0:1, 1024 + c:1025 + c],
                       in_=STAT[0:1, 1088 + c:1089 + c], inc=True)
                W("vector", s_ve, v)
            W("tensor", s_ve, s_ve.n)
            v = op("tensor", "mm", out=pb[6][:, 0:2], lhsT=ONESM[0:1, 0:128],
                   rhs=STATB[0:1, 1024:1026], start=True, stop=True, inc=True)
            W("vector", s_te, v)
            v = op("vector", "tc", out=BIAS[:, 0:2], in_=pb[6][:, 0:2], inc=True)
            W("vector", s_ve, v)
            op("vector", "tt", out=BIAS[:, 2:3], in0=BIAS[:, 0:1], in1=BIAS[:, 1:2],
               op=ALU.min, inc=True)
            barrier()

        def emit_attn_l0():
            exp_done = [0, 0]
            av_mm = [0, 0]
            av_mm_old = [0, 0]
            av_read = [0, 0]
            sums_read = 0
            last_norm = [0, 0]
            bc_done = [0, 0]
            dma_done = [0, 0]

            def emit_av(pm, pmi, pblo, pbhi, plast, pexp, ppar):
                for h in range(2):
                    W("tensor", s_se, pexp[h])
                    if pmi == 0:
                        W("tensor", s_se, av_read[h])
                        W("tensor", s_se, sums_read)
                    vap = A1[:, 24576 + pm * 128:24576 + (pm + 1) * 128]
                    pt = P4[:, ppar * 2048 + h * 1024 + pblo: ppar * 2048 + h * 1024 + pbhi]
                    op("tensor", "mm", out=AVB[:, h * 512 + pblo: h * 512 + pbhi],
                       lhsT=vap, rhs=pt, start=(pmi == 0), stop=plast)
                    av_mm[h] = op("tensor", "mm",
                                  out=AVB[32 * h:32 * h + 1, 1024 + pblo:1024 + pbhi],
                                  lhsT=ONESM[:, 0:1], rhs=pt, start=(pmi == 0), stop=plast,
                                  tp=(0, 32 * h), inc=True)

            for j in range(NTT):
                if j >= 1:
                    ms = [4 * j - 1] + [m for m in range(max(0, 4 * j - 4), 4 * j + 4)
                                        if m != 4 * j - 1]
                else:
                    ms = list(range(0, 4))
                pend = None
                for mi, m in enumerate(ms):
                    smin = max(m, 4 * j)
                    smax = min(m + 4, 4 * j + 3)
                    blo = (smin - 4 * j) * 128
                    bhi = (smax - 4 * j) * 128 + 128
                    last = (mi == len(ms) - 1)
                    par = mi % 2
                    smv = {}
                    for h in range(2):
                        W("tensor", s_se, exp_done[h])
                        if mi == 0:
                            W("tensor", s_ve, last_norm[h])
                        kap = A1[:, 16384 + m * 128:16384 + (m + 1) * 128]
                        qap = A1[:, h * 8192 + j * 512 + blo: h * 8192 + j * 512 + bhi]
                        smv[h] = op("tensor", "mm", out=SAB[:, h * 512 + blo: h * 512 + bhi],
                                    lhsT=kap, rhs=qap, start=True, stop=True, inc=True)
                    if pend is not None:
                        av_mm_old[:] = av_mm
                        emit_av(*pend)
                    evals = [0, 0]
                    for h in range(2):
                        mv = None
                        if 4 * j <= m + 4 <= 4 * j + 3:
                            c0 = (m + 4 - 4 * j) * 128
                            W("vector", s_te, smv[h])
                            mv = op("vector", "tt", out=SAB[:, h * 512 + c0: h * 512 + c0 + 128],
                                    in0=SAB[:, h * 512 + c0: h * 512 + c0 + 128],
                                    in1=EDGE, op=ALU.add, inc=True)
                        if m >= 4 * j:
                            c0 = (m - 4 * j) * 128
                            W("vector", s_te, smv[h])
                            mv = op("vector", "tt", out=SAB[:, h * 512 + c0: h * 512 + c0 + 128],
                                    in0=SAB[:, h * 512 + c0: h * 512 + c0 + 128],
                                    in1=TRI, op=ALU.add, inc=True)
                        W("scalar", s_te, smv[h])
                        W("scalar", s_te, av_mm_old[h])
                        if mv:
                            W("scalar", s_ve, mv)
                        pt = P4[:, par * 2048 + h * 1024 + blo: par * 2048 + h * 1024 + bhi]
                        evals[h] = op("scalar", "act", out=pt, in_=SAB[:, h * 512 + blo: h * 512 + bhi],
                                      func=AF.Exp, bias=BIAS[:, h:h + 1], scale=SC0, inc=True)
                        exp_done[h] = evals[h]
                    pend = (m, mi, blo, bhi, last, list(evals), par)
                emit_av(*pend)
                jb, jc = j // 2, (j % 2) * 512
                W("scalar", s_te, max(av_mm))
                W("scalar", s_te, max(bc_done))
                op("scalar", "act", out=STAT[0:1, 0:512], in_=AVB[0:1, 1024:1536],
                   func=AF.Ln, bias=0.0, scale=1.0)
                op("scalar", "act", out=STAT[32:33, 0:512], in_=AVB[32:33, 1024:1536],
                   func=AF.Ln, bias=0.0, scale=1.0)
                op("scalar", "act", out=STATB[0:1, 0:512], in_=STAT[0:1, 0:512],
                   func=AF.Exp, bias=0.0, scale=-1.0)
                sums_read = op("scalar", "act", out=STATB[32:33, 0:512], in_=STAT[32:33, 0:512],
                               func=AF.Exp, bias=0.0, scale=-1.0, inc=True)
                for h in range(2):
                    W("tensor", s_se, sums_read)
                    W("tensor", s_ve, last_norm[h])
                    bc_done[h] = op("tensor", "mm", out=SCD[:, h * 512:(h + 1) * 512],
                                    lhsT=ONESM[32 * h:32 * h + 1, 0:128],
                                    rhs=STATB[32 * h:32 * h + 1, 0:512], start=True, stop=True,
                                    tp=(32 * h, 0), inc=True)
                    W("scalar", s_te, bc_done[h])
                    W("scalar", s_ve, last_norm[h])
                    av_read[h] = op("scalar", "act",
                                    out=P4[:, 4096 + h * 512: 4096 + (h + 1) * 512],
                                    in_=AVB[:, h * 512:(h + 1) * 512], func=AF.Copy, inc=True)
                    W("vector", s_se, av_read[h])
                    W("vector", s_te, bc_done[h])
                    W("vector", s_sy, dma_done[h])
                    last_norm[h] = op("vector", "tt",
                                      out=P4[:, 5120 + h * 512: 5120 + (h + 1) * 512],
                                      in0=P4[:, 4096 + h * 512: 4096 + (h + 1) * 512],
                                      in1=SCD[:, h * 512:(h + 1) * 512], op=ALU.mult, inc=True)
                    W("sync", s_ve, last_norm[h])
                    dma_done[h] = dma("sync",
                                      avi[0][jb * 256 + h * 128: jb * 256 + (h + 1) * 128,
                                             jc:jc + 512],
                                      P4[:, 5120 + h * 512: 5120 + (h + 1) * 512])
            barrier()

        def emit_attn_l1():
            exp_done = [0, 0]
            av_mm = [0, 0, 0, 0]
            av_mm_old = [0, 0, 0, 0]
            av_read = 0
            last_norm = 0
            last_dma = 0
            bc_done = [0, 0, 0, 0]
            SP = [SAB, SCD]

            def emit_av(pn, poff, plast, pexp):
                ppar = pn % 2
                for h in range(4):
                    c = h // 2
                    W("tensor", s_se, pexp[c])
                    if pn == 0:
                        W("tensor", s_se, av_read)
                    vap = A1[:, 24576 + pn * 65: 24576 + pn * 65 + 65]
                    base = ppar * 4096 + c * 1024 + (h % 2) * 512
                    pt = P4[:, base + poff: base + 512]
                    av_mm[h] = op("tensor", "mm",
                                  out=AVB[0:65, h * 512 + poff: h * 512 + 512],
                                  lhsT=vap, rhs=pt, start=(pn == 0), stop=plast, inc=True)

            for j in range(NTT):
                nmax = 4 * j + 3
                pend = None
                for n in range(nmax + 1):
                    diag = n >= 4 * j
                    off = (n - 4 * j) * 128 if diag else 0
                    last = (n == nmax)
                    par = n % 2
                    smv = {}
                    for h in range(4):
                        c = h // 2
                        rb = (h % 2) * 64
                        W("tensor", s_se, exp_done[c])
                        if n == 0:
                            W("tensor", s_ve, last_norm)
                        kap = A1[rb:rb + 64, 16384 + n * 128:16384 + (n + 1) * 128]
                        qap = A1[rb:rb + 64, c * 8192 + j * 512 + off: c * 8192 + (j + 1) * 512]
                        smv[h] = op("tensor", "mm",
                                    out=SP[c][:, (h % 2) * 512 + off:(h % 2) * 512 + 512],
                                    lhsT=kap, rhs=qap, start=True, stop=True,
                                    tp=(rb, 0), inc=True)
                    if pend is not None:
                        av_mm_old[:] = av_mm
                        emit_av(*pend)
                    if not diag:
                        W("scalar", s_te, smv[3])
                        for h4 in range(4):
                            W("scalar", s_te, av_mm_old[h4])
                        ev = op("scalar", "act", out=P4[:, par * 4096: par * 4096 + 2048],
                                in_=SALL[:, 0:2048], func=AF.Exp,
                                bias=BIAS[:, 2:3], scale=SC1, inc=True)
                        exp_done[0] = exp_done[1] = ev
                    else:
                        for c in range(2):
                            mv = None
                            for hh in range(2):
                                W("vector", s_te, smv[2 * c + hh])
                                mv = op("vector", "tt",
                                        out=SP[c][:, hh * 512 + off: hh * 512 + off + 128],
                                        in0=SP[c][:, hh * 512 + off: hh * 512 + off + 128],
                                        in1=TRI, op=ALU.add, inc=True)
                            W("scalar", s_te, smv[2 * c + 1])
                            W("scalar", s_te, av_mm_old[2 * c])
                            W("scalar", s_te, av_mm_old[2 * c + 1])
                            if mv:
                                W("scalar", s_ve, mv)
                            base = par * 4096 + c * 1024
                            op("scalar", "act", out=P4[:, base + off: base + 512],
                               in_=SP[c][:, off:512], func=AF.Exp,
                               bias=BIAS[:, 2:3], scale=SC1)
                            exp_done[c] = op("scalar", "act",
                                             out=P4[:, base + 512 + off: base + 1024],
                                             in_=SP[c][:, 512 + off:1024], func=AF.Exp,
                                             bias=BIAS[:, 2:3], scale=SC1, inc=True)
                    pend = (n, off, last, list(exp_done))
                emit_av(*pend)
                jb, jc = j // 2, (j % 2) * 512
                for h in range(4):
                    c = h // 2
                    avb = AVB[:, h * 512:(h + 1) * 512]
                    W("scalar", s_te, av_mm[h])
                    W("scalar", s_te, bc_done[h])
                    op("scalar", "act", out=STAT[64:65, h * 512:(h + 1) * 512],
                       in_=avb[64:65, 0:512], func=AF.Ln, bias=0.0, scale=1.0)
                    rcpv = op("scalar", "act", out=STATB[64:65, h * 512:(h + 1) * 512],
                              in_=STAT[64:65, h * 512:(h + 1) * 512], func=AF.Exp,
                              bias=0.0, scale=-1.0, inc=True)
                    W("scalar", s_ve, last_norm)
                    av_read = op("scalar", "act", out=STATB[0:64, 0:512],
                                 in_=avb[0:64, 0:512], func=AF.Copy, inc=True)
                    W("tensor", s_se, rcpv)
                    W("tensor", s_se, exp_done[c])
                    bc_done[h] = op("tensor", "mm",
                                    out=SP[c][0:64, (h % 2) * 512:(h % 2) * 512 + 512],
                                    lhsT=ONESM[64:65, 0:64],
                                    rhs=STATB[64:65, h * 512:(h + 1) * 512],
                                    start=True, stop=True, tp=(64, 0), inc=True)
                    W("vector", s_se, av_read)
                    W("vector", s_te, bc_done[h])
                    W("vector", s_sy, last_dma)
                    last_norm = op("vector", "tt", out=STATB[0:64, 512:1024],
                                   in0=STATB[0:64, 0:512],
                                   in1=SP[c][0:64, (h % 2) * 512:(h % 2) * 512 + 512],
                                   op=ALU.mult, inc=True)
                    W("sync", s_ve, last_norm)
                    last_dma = dma("sync",
                                   avi[1][jb * 256 + c * 128 + (h % 2) * 64:
                                          jb * 256 + c * 128 + (h % 2) * 64 + 64,
                                          jc:jc + 512],
                                   STATB[0:64, 512:1024])
            barrier()

        def emit_out_ln_add(tt, mms):
            evac_v = {}
            store_v = {}
            cast_v = {}
            stat_v = None
            pending = []

            def flush_stat(oc):
                nonlocal stat_v
                cv, gv = cast_v[oc]
                W("tensor", s_ve, cv)
                W("tensor", s_ge, gv)
                s = (oc % 2) * 512
                op("tensor", "mm", out=pb[6][0:1, 0:512], lhsT=ONESM[:, 0:1],
                   rhs=STATB[:, s:s + 512], start=(oc == 0), stop=(oc == ECH - 1), tp=(0, 0))
                stat_v = op("tensor", "mm", out=pb[6][32:33, 0:512], lhsT=ONESM[:, 0:1],
                            rhs=STATB[:, 1024 + s:1024 + s + 512], start=(oc == 0),
                            stop=(oc == ECH - 1), tp=(0, 32), inc=True)

            for oc in range(ECH):
                mv = mms(oc)
                W("scalar", s_te, mv)
                if oc >= 2:
                    W("scalar", s_sy, store_v[oc - 2])
                slot = P3[:, 1024 + (oc % 2) * 512: 1024 + (oc % 2) * 512 + 512]
                evac_v[oc] = op("scalar", "act", out=slot, in_=pb[4 + oc % 2][:, 0:512],
                                func=AF.Copy, inc=True)
                W("sync", s_se, evac_v[oc])
                store_v[oc] = dma("sync", aod[oc * 128:(oc + 1) * 128, :], slot)
                s = (oc % 2) * 512
                W("vector", s_se, evac_v[oc])
                if oc >= 2:
                    W("vector", s_te, stat_v)
                cv = op("vector", "tc", out=STATB[:, s:s + 512], in_=slot, inc=True)
                W("gpsimd", s_se, evac_v[oc])
                if oc >= 2:
                    W("gpsimd", s_te, stat_v)
                gv = op("gpsimd", "tt", out=STATB[:, 1024 + s:1024 + s + 512], in0=slot,
                        in1=slot, op=ALU.mult, inc=True)
                cast_v[oc] = (cv, gv)
                if oc >= 1:
                    flush_stat(oc - 1)
            flush_stat(ECH - 1)
            bcv = ln_tail(stat_v, 512)
            bcv = ln_tail(stat_v, 512)
            lv = {}
            for ec in range(ECH):
                buf = ec % 2
                aslot = P3[:, 2048 + buf * 512: 2048 + buf * 512 + 512]
                if ec >= 2:
                    W("sync", s_ve, lv[ec - 2])
                adv = dma("sync", aslot, aod[ec * 128:(ec + 1) * 128, :])
                W("vector", s_sy, adv)
                W("vector", s_te, bcv)
                op("vector", "tt", out=STAT[:, 1024:1536], in0=aslot, in1=pb[0][:, 0:512],
                   op=ALU.mult)
                op("vector", "tt", out=STAT[:, 1536:2048], in0=STAT[:, 1024:1536],
                   in1=pb[1][:, 0:512], op=ALU.subtract)
                dst = xr[:, ec * TOWN + tt * 512: ec * TOWN + (tt + 1) * 512]
                lv[ec] = op("vector", "tt", out=dst, in0=dst, in1=STAT[:, 1536:2048],
                            op=ALU.add, inc=True)
            barrier()

        def emit_wo(lay):
            W("sync", s_cast, cast_done[(f"wo{lay}", 1)])
            wv = dma("sync", A1[:, 0:32768].rearrange("p (dc e) -> p dc e", e=2048),
                     wob[lay].rearrange("(dc p) e -> p dc e", p=128))
            for tt in range(2):
                buf = tt % 2
                avt = A2[:, buf * 8192: (buf + 1) * 8192]
                lv = dma("sync", avt.rearrange("p (dc t) -> p dc t", t=512),
                         avo[lay][:, tt * 512:(tt + 1) * 512].rearrange("(dc p) t -> p dc t", p=128))

                def mms(oc, avt=avt, lv=lv, wv=wv):
                    W("tensor", s_sy, lv)
                    W("tensor", s_sy, wv)
                    v = None
                    for dc in range(ECH):
                        v = op("tensor", "mm", out=pb[4 + oc % 2][:, 0:512],
                               lhsT=A1[:, dc * 2048 + oc * 128: dc * 2048 + (oc + 1) * 128],
                               rhs=avt[:, dc * 512:(dc + 1) * 512],
                               start=(dc == 0), stop=(dc == ECH - 1), inc=(dc == ECH - 1))
                    return v

                emit_out_ln_add(tt, mms)

        def emit_ffn(lay):
            for tt in range(2):
                emit_ln_pre(lay, to_ag=False, tt=tt)
                if lay == 0:
                    order = list(range(HCH))
                else:
                    order = [c + 16 * m for c in range(16) for m in range(4)]
                wdma_v = {}
                humul_v = {}
                W1SLOT = [A2[:, 8192:12288], A2[:, 12288:16384], P4[:, 4096:8192]]
                for i, hc in enumerate(order):
                    slot = i % 3
                    if i >= 3:
                        W("gpsimd", s_te, wdma_v[i - 3][1])
                    wv = dma("gpsimd", W1SLOT[slot], w1[lay][hc * 128:(hc + 1) * 128, :])
                    gb, ub = pb[2 * (i % 2)], pb[2 * (i % 2) + 1]
                    W("tensor", s_gq, wv)
                    if i >= 2:
                        W("tensor", s_se, humul_v[i - 2][0])
                        W("tensor", s_ve, humul_v[i - 2][1])
                    mv = None
                    wt = W1SLOT[slot]
                    for ec in range(ECH):
                        op("tensor", "mm", out=gb[:, 0:512], lhsT=wt[:, ec * 256:ec * 256 + 128],
                           rhs=A2[:, ec * 512:(ec + 1) * 512], start=(ec == 0), stop=(ec == ECH - 1))
                        mv = op("tensor", "mm", out=ub[:, 0:512],
                                lhsT=wt[:, ec * 256 + 128:ec * 256 + 256],
                                rhs=A2[:, ec * 512:(ec + 1) * 512], start=(ec == 0),
                                stop=(ec == ECH - 1), inc=(ec == ECH - 1))
                    wdma_v[i] = (wv, mv)
                    W("scalar", s_te, mv)
                    if i >= 2:
                        W("scalar", s_ve, humul_v[i - 2][1])
                    s2 = i % 2
                    sv = op("scalar", "act", out=P4[:, s2 * 512: (s2 + 1) * 512],
                            in_=gb[:, 0:512], func=AF.Silu, inc=True)
                    W("vector", s_te, mv)
                    W("vector", s_se, sv)
                    uv = op("vector", "tc", out=P4[:, 1024 + s2 * 512: 1024 + (s2 + 1) * 512],
                            in_=ub[:, 0:512], inc=True)
                    hv = op("vector", "tt", out=A1[:, hc * 512:(hc + 1) * 512],
                            in0=P4[:, s2 * 512:(s2 + 1) * 512],
                            in1=P4[:, 1024 + s2 * 512: 1024 + (s2 + 1) * 512],
                            op=ALU.mult, inc=True)
                    humul_v[i] = (sv, hv)
                    if lay == 1 and i % 4 == 3:
                        emit_sparsify(order[i - 3:i + 1], hv)
                barrier()
                wdown_v = {}
                mm_done = {}

                def mkmms():
                    W2SLOT = [A2[:, 0:8192], A2[:, 8192:16384], P4[:, 0:8192]]

                    def mms(oc):
                        slot = oc % 3
                        if oc >= 3:
                            W("gpsimd", s_te, mm_done[oc - 3])
                        wv = dma("gpsimd", W2SLOT[slot], w2[lay][oc * 128:(oc + 1) * 128, :])
                        W("tensor", s_gq, wv)
                        v = None
                        wt = W2SLOT[slot]
                        for hc in range(HCH):
                            v = op("tensor", "mm", out=pb[4 + oc % 2][:, 0:512],
                                   lhsT=wt[:, hc * 128:(hc + 1) * 128],
                                   rhs=A1[:, hc * 512:(hc + 1) * 512],
                                   start=(hc == 0), stop=(hc == HCH - 1), inc=(hc == HCH - 1))
                        mm_done[oc] = v
                        return v
                    return mms

                emit_out_ln_add(tt, mkmms())

        def emit_sparsify(quartet, last_hv):
            H = [A1[:, hc * 512:(hc + 1) * 512] for hc in quartet]
            sq = [P3[:, k * 512:(k + 1) * 512] for k in range(4)]
            s1 = STAT[:, 1024:1536]
            t1 = STAT[:, 1536:2048]
            s2 = STAT[:, 0:512]
            t2 = STAT[:, 512:1024]
            W("vector", s_ve, last_hv)
            for k in range(4):
                op("vector", "tt", out=sq[k], in0=H[k], in1=H[k], op=ALU.mult)
            op("vector", "tt", out=s1, in0=sq[0], in1=sq[1], op=ALU.max)
            op("vector", "tt", out=t1, in0=sq[0], in1=sq[1], op=ALU.min)
            op("vector", "tt", out=s2, in0=sq[2], in1=sq[3], op=ALU.max)
            op("vector", "tt", out=t2, in0=sq[2], in1=sq[3], op=ALU.min)
            op("vector", "tt", out=s1, in0=s1, in1=s2, op=ALU.min)
            op("vector", "tt", out=t1, in0=t1, in1=t2, op=ALU.max)
            op("vector", "tt", out=s1, in0=s1, in1=t1, op=ALU.max)
            for k in range(4):
                op("vector", "tt", out=t2, in0=sq[k], in1=s1, op=ALU.is_ge)
                op("vector", "tt", out=H[k], in0=H[k], in1=t2, op=ALU.mult,
                   inc=(k == 3))

        def tap_dram(name, src, shape, dtype):
            t = mktap(name, shape, dtype)
            if t is not None:
                v = dma("sync", t, src)
                for eng in ENGS:
                    W(eng, s_sy, v)

        def tap_xr(name):
            t = mktap(name, [E, TOWN], F32)
            if t is not None:
                v = None
                for ec in range(ECH):
                    v = dma("sync", t[ec * 128:(ec + 1) * 128, :], xr[:, ec * TOWN:(ec + 1) * TOWN])
                for eng in ENGS:
                    W(eng, s_sy, v)

        def tap_sbuf(name, src, shape, dtype):
            t = mktap(name, shape, dtype)
            if t is not None:
                v = dma("sync", t, src)
                for eng in ENGS:
                    W(eng, s_sy, v)

        step = [0]

        def go():
            step[0] += 1
            return step[0] <= upto

        for lay in range(2):
            if go():
                emit_ln_pre(lay, to_ag=True, tt=0)
                cc("AllGather", [agi[lay][0][:]], [ago[lay][0][:]])
                emit_ln_pre(lay, to_ag=True, tt=1)
            if go():
                v = cc("AllGather", [agi[lay][1][:]], [ago[lay][1][:]])
                if lay == 0:
                    bg_cast_all()
                for eng in ENGS:
                    W(eng, s_cc, v)
            if go():
                emit_qkv(lay)
                tap_sbuf(f"qkv{lay}", A1[:, 0:32768], [128, 32768], BF16)
            if go():
                emit_bias(lay)
                tap_sbuf(f"bias{lay}", BIAS[:, 0:16], [128, 16], F32)
            if go():
                if lay == 0:
                    emit_attn_l0()
                else:
                    emit_attn_l1()
                tap_dram(f"av{lay}", avi[lay][:], [E, TOWN], BF16)
            if go():
                v = cc("AllToAll", [avi[lay][:]], [avo[lay][:]])
                for eng in ENGS:
                    W(eng, s_cc, v)
            if go():
                emit_wo(lay)
                tap_xr(f"xa{lay}")
            if go():
                emit_ffn(lay)
                tap_xr(f"xf{lay}")

        barrier()
        v = None
        for ec in range(ECH):
            v = dma("sync", y_out[ec * 128:(ec + 1) * 128, :], xr[:, ec * TOWN:(ec + 1) * TOWN])
        for eng in ENGS:
            W(eng, s_sy, v)

        block = st_.enter_context(nc.Block())
        with allow_ici_all_to_all():
            @block.sync
            def _(e):
                p.emit_into(e, "sync")

            @block.gpsimd
            def _(e):
                p.emit_into(e, "gpsimd")

            @block.tensor
            def _(e):
                p.emit_into(e, "tensor")

            @block.scalar
            def _(e):
                p.emit_into(e, "scalar")

            @block.vector
            def _(e):
                p.emit_into(e, "vector")

    ninstr = {e: len(p.q[e]) for e in ENGS}
    return nc, tap_aps, ninstr


def _host_tables():
    inv0 = (10000.0 ** (-(np.arange(0, 128, 2, dtype=np.float64)) / 128.0))
    ang0 = np.arange(TFULL, dtype=np.float64)[None, :] * inv0[:, None]
    c0 = np.cos(ang0)
    s0 = np.sin(ang0)
    cs0 = np.empty((256, TFULL), np.float32)
    cs0[0:64] = c0
    cs0[64:128] = c0
    cs0[128:192] = -s0
    cs0[192:256] = s0
    inv1 = (10000.0 ** (-(np.arange(0, 64, 2, dtype=np.float64)) / 64.0))
    ang1 = np.arange(TFULL, dtype=np.float64)[None, :] * inv1[:, None]
    c1 = np.cos(ang1)
    s1 = np.sin(ang1)
    cs1 = np.empty((256, TFULL), np.float32)
    for blk in range(2):
        b = blk * 64
        cs1[b + 0:b + 32] = c1
        cs1[b + 32:b + 64] = c1
        cs1[128 + b + 0:128 + b + 32] = -s1
        cs1[128 + b + 32:128 + b + 64] = s1
    return cs0, cs1


def _host_consts():
    pq = np.arange(128)
    tri = np.where(pq[:, None] <= pq[None, :], 0.0, -1e9).astype(np.float32)
    edge = np.where(pq[None, :] < pq[:, None], 0.0, -1e9).astype(np.float32)
    ident = np.eye(128, dtype=np.float32)
    sw0 = np.zeros((128, 128), np.float32)
    sw0[(pq + 64) % 128, pq] = 1.0
    sig1 = np.where(pq % 64 < 32, pq + 32, pq - 32)
    sw1 = np.zeros((128, 128), np.float32)
    sw1[sig1, pq] = 1.0
    return np.concatenate([tri, edge, ident, sw0, sw1], axis=0)


def _blocks(a, nblk, rows):
    R, C = a.shape
    assert C == nblk * rows
    return np.ascontiguousarray(a.reshape(R, nblk, rows).transpose(1, 0, 2).reshape(nblk * R, rows))


_BUILD_CACHE = {}


def _get_build(taps=(), upto=99):
    key = (tuple(sorted(taps)), upto)
    if key not in _BUILD_CACHE:
        _BUILD_CACHE[key] = build(taps, upto)
    return _BUILD_CACHE[key]


def kernel(L0_attn_x, L0_attn_Wq, L0_attn_Wk, L0_attn_Wv, L0_attn_Wo,
           L0_ffn_w_gate, L0_ffn_w_up, L0_ffn_w_down,
           L1_attn_wq, L1_attn_wk, L1_attn_wv, L1_attn_wo,
           L1_ffn_w_gate, L1_ffn_w_up, L1_ffn_w_down, _taps=(), _upto=99):
    f = np.float32
    x = np.asarray(L0_attn_x, f).reshape(TFULL, E)
    cs0, cs1 = _host_tables()
    consts = _host_consts()

    wq0T = np.asarray(L0_attn_Wq, f).T
    wk0T = np.asarray(L0_attn_Wk, f).T
    wv0T = np.asarray(L0_attn_Wv, f).T
    wo0 = np.ascontiguousarray(np.asarray(L0_attn_Wo, f).T)
    g0T = np.asarray(L0_ffn_w_gate, f).T
    u0T = np.asarray(L0_ffn_w_up, f).T
    d0T = np.asarray(L0_ffn_w_down, f).T
    def _packw1(gT, uT):
        blk = np.concatenate([gT.reshape(E, HCH, 128), uT.reshape(E, HCH, 128)],
                             axis=2).transpose(1, 0, 2)
        return np.ascontiguousarray(
            blk.reshape(HCH, 16, 128, 256).transpose(0, 2, 1, 3).reshape(HCH * 128, 4096))

    def _packw2(dT):
        blk = dT.reshape(HID, ECH, 128).transpose(1, 0, 2)
        return np.ascontiguousarray(
            blk.reshape(ECH, 64, 128, 128).transpose(0, 2, 1, 3).reshape(ECH * 128, 8192))

    w10 = _packw1(g0T, u0T)
    w20 = _packw2(d0T)

    wq1 = np.asarray(L1_attn_wq, f)
    wk1 = np.asarray(L1_attn_wk, f)
    wv1 = np.asarray(L1_attn_wv, f)
    wo1 = np.ascontiguousarray(np.asarray(L1_attn_wo, f))
    perm = 4 * (np.arange(HID) % 2048) + (np.arange(HID) // 2048)
    g1 = np.asarray(L1_ffn_w_gate, f)[:, perm]
    u1 = np.asarray(L1_ffn_w_up, f)[:, perm]
    d1 = np.asarray(L1_ffn_w_down, f)[perm, :]
    w11 = _packw1(g1, u1)
    w21 = _packw2(d1)

    in_maps = []
    for i in range(NCORES):
        kv0 = i // 2
        wqkv0 = np.ascontiguousarray(np.concatenate(
            [wq0T[:, 256 * i:256 * (i + 1)],
             wk0T[:, 128 * kv0:128 * (kv0 + 1)],
             wv0T[:, 128 * kv0:128 * (kv0 + 1)]], axis=1))
        wqkv1 = np.ascontiguousarray(np.concatenate(
            [wq1[:, 256 * i:256 * (i + 1)],
             wk1[:, 64 * i:64 * (i + 1)],
             wv1[:, 64 * i:64 * (i + 1)]], axis=1))
        x_own = np.ascontiguousarray(x[TOWN * i:TOWN * (i + 1), :].T)
        in_maps.append(dict(
            x_in=x_own, wqkv0=wqkv0, wo0=wo0, w10=w10, w20=w20,
            wqkv1=wqkv1, wo1=wo1, w11=w11, w21=w21,
            cs0=cs0, cs1=cs1, consts=consts))

    nc, tap_aps, _ = _get_build(_taps, _upto)
    res = run_bass_kernel_spmd(nc, in_maps, core_ids=list(range(NCORES)))
    outs = [np.asarray(res.results[i]["y_out"]).T for i in range(NCORES)]
    y = np.concatenate(outs, axis=0).reshape(1, TFULL, E)
    if _taps:
        kernel.tap_results = [
            {k: np.asarray(res.results[i]["tap_" + k]) for k in _taps}
            for i in range(NCORES)]
    return y


if __name__ == "__main__":
    nc, tap_aps, ninstr = build()
    print("instr counts:", ninstr, "total:", sum(ninstr.values()))


# version 13
# speedup vs baseline: 1.1129x; 1.0425x over previous
import contextlib
import math
import numpy as np
import ml_dtypes
import concourse.bass as bass
import concourse.mybir as mybir
from concourse.bass_utils import run_bass_kernel_spmd
from concourse.aot_env import allow_ici_all_to_all

F32 = mybir.dt.float32
BF16 = mybir.dt.bfloat16
AF = mybir.ActivationFunctionType
ALU = mybir.AluOpType
bfnp = ml_dtypes.bfloat16

NCORES = 8
E = 2048
ECH = 16
TFULL = 8192
TOWN = 1024
NTT = 16
HID = 8192
HCH = 64
SC0 = 1.0 / math.sqrt(128.0)
SC1 = 0.125

ENGS = ("sync", "gpsimd", "tensor", "scalar", "vector")


class Sem:
    def __init__(self, h):
        self.h = h
        self.n = 0

    def inc(self, k=1):
        self.n += k
        return self.n


class Prog:
    def __init__(self):
        self.q = {e: [] for e in ENGS}

    def emit_into(self, eng, name):
        for kind, a in self.q[name]:
            if kind == "wait":
                eng.wait_ge(a[0], a[1])
                continue
            if kind == "dma":
                inst = eng.dma_start(out=a["out"], in_=a["in_"])
            elif kind == "mm":
                kw = dict(out=a["out"], lhsT=a["lhsT"], rhs=a["rhs"],
                          start=a["start"], stop=a["stop"])
                if a.get("tp") is not None:
                    kw["tile_position"] = a["tp"]
                inst = eng.matmul(**kw)
            elif kind == "tr":
                kw = {}
                if a.get("tp") is not None:
                    kw["tile_position"] = a["tp"]
                inst = eng.transpose(a["out"], a["in_"], a["ident"], **kw)
            elif kind == "act":
                kw = dict(out=a["out"], in_=a["in_"], func=a["func"])
                if a.get("bias") is not None:
                    kw["bias"] = a["bias"]
                if a.get("scale") is not None:
                    kw["scale"] = a["scale"]
                inst = eng.activation(**kw)
            elif kind == "tt":
                inst = eng.tensor_tensor(out=a["out"], in0=a["in0"], in1=a["in1"], op=a["op"])
            elif kind == "tc":
                inst = eng.tensor_copy(out=a["out"], in_=a["in_"])
            elif kind == "ts":
                inst = eng.tensor_single_scalar(out=a["out"], in_=a["in_"], scalar=a["s"], op=a["op"])
            elif kind == "recipf":
                inst = eng.reciprocal_approx_fast(a["out"], a["in_"])
            elif kind == "rmax":
                inst = eng.reduce_max(a["out"], a["in_"], axis=mybir.AxisListType.X)
            elif kind == "memset":
                inst = eng.memset(a["ap"], a["val"])
            elif kind == "cc":
                inst = eng.collective_compute(a["kind"], a["op"], replica_groups=a["rg"],
                                              ins=a["ins"], outs=a["outs"])
            else:
                raise AssertionError(kind)
            if a.get("inc") is not None:
                s, k = a["inc"]
                inst.then_inc(s, k)


def build(taps=(), upto=99):
    taps = set(taps)
    nc = bass.Bass("TRN2", target_bir_lowering=False)

    x_in = nc.dram_tensor("x_in", [E, TOWN], F32, kind="ExternalInput").ap()
    wqkv = [nc.dram_tensor("wqkv0", [E, 512], F32, kind="ExternalInput").ap(),
            nc.dram_tensor("wqkv1", [E, 384], F32, kind="ExternalInput").ap()]
    wo = [nc.dram_tensor("wo0", [E, E], F32, kind="ExternalInput").ap(),
          nc.dram_tensor("wo1", [E, E], F32, kind="ExternalInput").ap()]
    w1 = [nc.dram_tensor("w10", [HCH * 128, 4096], F32, kind="ExternalInput").ap(),
          nc.dram_tensor("w11", [HCH * 128, 4096], F32, kind="ExternalInput").ap()]
    w2 = [nc.dram_tensor("w20", [16 * 128, 8192], F32, kind="ExternalInput").ap(),
          nc.dram_tensor("w21", [16 * 128, 8192], F32, kind="ExternalInput").ap()]
    w1b = [nc.dram_tensor(f"w1b{l}", [HCH * 128, 4096], BF16, kind="Internal").ap() for l in range(2)]
    w2b = [nc.dram_tensor(f"w2b{l}", [16 * 128, 8192], BF16, kind="Internal").ap() for l in range(2)]
    wob = [nc.dram_tensor(f"wob{l}", [E, E], BF16, kind="Internal").ap() for l in range(2)]
    wqb1 = nc.dram_tensor("wqb1", [E, 384], BF16, kind="Internal").ap()
    cs = [nc.dram_tensor("cs0", [256, TFULL], F32, kind="ExternalInput").ap(),
          nc.dram_tensor("cs1", [256, TFULL], F32, kind="ExternalInput").ap()]
    consts = nc.dram_tensor("consts", [5 * 128, 128], F32, kind="ExternalInput").ap()
    y_out = nc.dram_tensor("y_out", [E, TOWN], F32, kind="ExternalOutput").ap()

    agi = [nc.dram_tensor(f"ag{l}i", [E, TOWN], BF16, kind="Internal").ap() for l in range(2)]
    ago = [nc.dram_tensor(f"ag{l}o", [NCORES * E, TOWN], BF16, kind="Internal", addr_space="Shared").ap() for l in range(2)]
    avi = [nc.dram_tensor(f"av{l}i", [E, TOWN], BF16, kind="Internal").ap() for l in range(2)]
    avo = [nc.dram_tensor(f"av{l}o", [E, TOWN], BF16, kind="Internal").ap() for l in range(2)]
    aod = nc.dram_tensor("aod", [E, 512], F32, kind="Internal").ap()

    tap_aps = {}

    def mktap(name, shape, dtype):
        if name in taps:
            tap_aps[name] = nc.dram_tensor("tap_" + name, shape, dtype, kind="ExternalOutput").ap()
        return tap_aps.get(name)

    st_ = contextlib.ExitStack()
    with st_:
        xr = st_.enter_context(nc.sbuf_tensor("xr", [128, ECH * TOWN], F32))
        A1 = st_.enter_context(nc.sbuf_tensor("A1", [128, 32768], BF16))
        A2 = st_.enter_context(nc.sbuf_tensor("A2", [128, 16384], BF16))
        P3 = st_.enter_context(nc.sbuf_tensor("P3", [128, 3072], F32))
        P4 = st_.enter_context(nc.sbuf_tensor("P4", [128, 8192], BF16))
        CONS = st_.enter_context(nc.sbuf_tensor("CONS", [128, 640], BF16))
        ONESM = st_.enter_context(nc.sbuf_tensor("ONESM", [128, 128], BF16))
        STAT = st_.enter_context(nc.sbuf_tensor("STAT", [128, 2048], F32))
        STATB = st_.enter_context(nc.sbuf_tensor("STATB", [128, 2048], BF16))
        CSb = st_.enter_context(nc.sbuf_tensor("CSb", [128, 1024], F32))
        BIAS = st_.enter_context(nc.sbuf_tensor("BIAS", [128, 16], F32))
        IDF = st_.enter_context(nc.sbuf_tensor("IDF", [128, 128], F32))

        SAB = st_.enter_context(nc.psum_tensor("SAB", [128, 1024], F32))
        SCD = st_.enter_context(nc.psum_tensor("SCD", [128, 1024], F32))
        AVB = st_.enter_context(nc.psum_tensor("AVB", [128, 2048], F32))
        pb = [SAB[:, 0:512], SAB[:, 512:1024], SCD[:, 0:512], SCD[:, 512:1024],
              AVB[:, 0:512], AVB[:, 512:1024], AVB[:, 1024:1536], AVB[:, 1536:2048]]

        s_sy = Sem(st_.enter_context(nc.semaphore("s_sy")))
        s_gq = Sem(st_.enter_context(nc.semaphore("s_gq")))
        s_cc = Sem(st_.enter_context(nc.semaphore("s_cc")))
        s_te = Sem(st_.enter_context(nc.semaphore("s_te")))
        s_se = Sem(st_.enter_context(nc.semaphore("s_se")))
        s_ve = Sem(st_.enter_context(nc.semaphore("s_ve")))
        s_ge = Sem(st_.enter_context(nc.semaphore("s_ge")))
        s_cast = Sem(st_.enter_context(nc.semaphore("s_cast")))
        ALLSEMS = [s_sy, s_gq, s_cc, s_te, s_se, s_ve, s_ge]
        ESEM = {"tensor": s_te, "scalar": s_se, "vector": s_ve, "gpsimd": s_ge}

        p = Prog()

        def W(eng, sem, val):
            if val:
                p.q[eng].append(("wait", (sem.h, val)))

        def dma(eng, out, in_, sem=None):
            s = sem if sem is not None else (s_sy if eng == "sync" else s_gq)
            v = s.inc(16)
            p.q[eng].append(("dma", dict(out=out, in_=in_, inc=(s.h, 16))))
            return v

        def op(eng, kind, inc=False, **a):
            s = ESEM[eng]
            v = None
            if inc:
                v = s.inc(1)
                a["inc"] = (s.h, 1)
            p.q[eng].append((kind, a))
            return v

        def cc(kind, ins, outs):
            v = s_cc.inc(1)
            p.q["gpsimd"].append(("cc", dict(kind=kind, op=ALU.bypass,
                                             rg=[list(range(NCORES))], ins=ins, outs=outs,
                                             inc=(s_cc.h, 1))))
            return v

        def barrier():
            vals = [(s, s.n) for s in ALLSEMS]
            for eng in ENGS:
                for s, v in vals:
                    W(eng, s, v)

        TRI = CONS[:, 0:128]
        EDGE = CONS[:, 128:256]
        IDENT = CONS[:, 256:384]
        SWAP = [CONS[:, 384:512], CONS[:, 512:640]]

        dma("gpsimd", CONS[:].rearrange("p (n c) -> p n c", c=128),
            consts.rearrange("(n p) c -> p n c", p=128))
        dma("sync", IDF[:], consts[256:384, :])
        op("gpsimd", "memset", ap=ONESM[:], val=1.0, inc=True)
        qkv0_wv = dma("gpsimd", P4[:, 0:ECH * 512].rearrange("p (c f) -> p c f", f=512),
                      wqkv[0].rearrange("(c p) f -> p c f", p=128))
        for ec in range(ECH):
            dma("sync", xr[:, ec * TOWN:(ec + 1) * TOWN], x_in[ec * 128:(ec + 1) * 128, :])
        cast_done = {}

        def bg_cast(key, dst, srcten, nrows, piece_rows):
            for p0 in range(0, nrows, piece_rows):
                v = dma("gpsimd", dst[p0:p0 + piece_rows, :], srcten[p0:p0 + piece_rows, :],
                        sem=s_cast)
                cast_done[(key, p0 // piece_rows)] = v

        def bg_cast_all():
            bg_cast("wo0", wob[0], wo[0], E, 1024)
            bg_cast("wo1", wob[1], wo[1], E, 1024)
        barrier()

        def ln_tail(mm_v, ncols):
            A = STAT[0:1, 0:512][:, 0:ncols]
            B = STAT[0:1, 512:1024][:, 0:ncols]
            C = STAT[0:1, 1024:1536][:, 0:ncols]
            D = STAT[0:1, 1536:2048][:, 0:ncols]
            W("vector", s_te, mm_v)
            op("vector", "ts", out=A, in_=pb[6][0:1, 0:ncols], s=1.0 / E, op=ALU.mult)
            op("vector", "ts", out=B, in_=pb[6][32:33, 0:ncols], s=1.0 / E, op=ALU.mult)
            op("vector", "tt", out=C, in0=A, in1=A, op=ALU.mult)
            op("vector", "tt", out=B, in0=B, in1=C, op=ALU.subtract)
            v = op("vector", "ts", out=B, in_=B, s=1e-5, op=ALU.add, inc=True)
            W("scalar", s_ve, v)
            op("scalar", "act", out=B, in_=B, func=AF.Ln, bias=0.0, scale=1.0)
            sv = op("scalar", "act", out=D, in_=B, func=AF.Exp, bias=0.0, scale=-0.5, inc=True)
            W("vector", s_se, sv)
            op("vector", "tt", out=C, in0=A, in1=D, op=ALU.mult)
            op("vector", "tc", out=STATB[0:1, 0:ncols], in_=D)
            op("vector", "tc", out=A, in_=STATB[0:1, 0:ncols])
            op("vector", "tt", out=A, in0=D, in1=A, op=ALU.subtract)
            op("vector", "tc", out=STATB[0:1, 512:512 + ncols], in_=A)
            op("vector", "tc", out=STATB[0:1, 1024:1024 + ncols], in_=C)
            op("vector", "tc", out=B, in_=STATB[0:1, 1024:1024 + ncols])
            op("vector", "tt", out=B, in0=C, in1=B, op=ALU.subtract)
            v = op("vector", "tc", out=STATB[0:1, 1536:1536 + ncols], in_=B, inc=True)
            W("tensor", s_ve, v)
            op("tensor", "mm", out=pb[0][:, 0:ncols], lhsT=ONESM[0:1, 0:128],
               rhs=STATB[0:1, 0:ncols], start=True, stop=False)
            op("tensor", "mm", out=pb[0][:, 0:ncols], lhsT=ONESM[0:1, 0:128],
               rhs=STATB[0:1, 512:512 + ncols], start=False, stop=True)
            op("tensor", "mm", out=pb[1][:, 0:ncols], lhsT=ONESM[0:1, 0:128],
               rhs=STATB[0:1, 1024:1024 + ncols], start=True, stop=False)
            v = op("tensor", "mm", out=pb[1][:, 0:ncols], lhsT=ONESM[0:1, 0:128],
                   rhs=STATB[0:1, 1536:1536 + ncols], start=False, stop=True, inc=True)
            return v

        def emit_ln_pre(lay, to_ag, tt):
            vv = tv = None
            for ec in range(ECH):
                src = xr[:, ec * TOWN + tt * 512: ec * TOWN + (tt + 1) * 512]
                vv = op("vector", "tc", out=A2[:, ec * 512:(ec + 1) * 512], in_=src, inc=True)
                vv = op("vector", "tt", out=A2[:, 8192 + ec * 512: 8192 + (ec + 1) * 512],
                        in0=src, in1=src, op=ALU.mult, inc=True)
            W("tensor", s_ve, vv)
            mmv = None
            for ec in range(ECH):
                op("tensor", "mm", out=pb[6][0:1, 0:512], lhsT=ONESM[:, 0:1],
                   rhs=A2[:, ec * 512:(ec + 1) * 512], start=(ec == 0), stop=(ec == ECH - 1),
                   tp=(0, 0))
                mmv = op("tensor", "mm", out=pb[6][32:33, 0:512], lhsT=ONESM[:, 0:1],
                         rhs=A2[:, 8192 + ec * 512: 8192 + (ec + 1) * 512], start=(ec == 0),
                         stop=(ec == ECH - 1), tp=(0, 32), inc=True)
            bcv = ln_tail(mmv, 512)
            for ec in range(ECH):
                src = xr[:, ec * TOWN + tt * 512: ec * TOWN + (tt + 1) * 512]
                W("vector", s_te, bcv)
                op("vector", "tt", out=STAT[:, 1024:1536], in0=src, in1=pb[0][:, 0:512], op=ALU.mult)
                if to_ag:
                    dst = A2[:, 8192 + ec * 512: 8192 + (ec + 1) * 512]
                else:
                    dst = A2[:, ec * 512:(ec + 1) * 512]
                vv = op("vector", "tt", out=dst, in0=STAT[:, 1024:1536], in1=pb[1][:, 0:512],
                        op=ALU.subtract, inc=True)
                if to_ag:
                    W("sync", s_ve, vv)
                    dma("sync", agi[lay][ec * 128:(ec + 1) * 128, tt * 512:(tt + 1) * 512], dst)
            barrier()

        def emit_qkv(lay):
            nw = 512 if lay == 0 else 384
            noc = 4 if lay == 0 else 3
            vstride = 128 if lay == 0 else 65
            vdw = 128 if lay == 0 else 64
            if lay == 0:
                wv = qkv0_wv
                W("tensor", s_gq, wv)
            else:
                wv = dma("gpsimd", P4[:, 0:ECH * nw].rearrange("p (c f) -> p c f", f=nw),
                         wqkv[lay].rearrange("(c p) f -> p c f", p=128))
                W("tensor", s_gq, wv)
            if lay == 1:
                mz = op("gpsimd", "memset", ap=A1[:, 24576:24576 + 65 * 64], val=1.0, inc=True)
                W("vector", s_ge, mz)
            slot_rv = {0: 0, 1: 0}
            slot_tv = {0: 0, 1: 0}
            last_ev = {}
            last_mm = {0: 0, 1: 0}
            last_rope = 0
            last_tr = {}
            raw32_rv = 0
            for tt in range(NTT):
                buf = tt % 2
                xh = A2[:, buf * 8192:(buf + 1) * 8192]
                r, c0 = tt // 2, (tt % 2) * 512
                W("sync", s_te, last_mm[buf])
                lv = dma("sync", xh.rearrange("p (c t) -> p c t", t=512),
                         ago[lay][E * r:E * (r + 1), c0:c0 + 512].rearrange("(c p) t -> p c t", p=128))
                W("sync", s_ve, last_rope)
                dma("sync", CSb[:, 0:512], cs[lay][0:128, tt * 512:(tt + 1) * 512])
                cv = dma("sync", CSb[:, 512:1024], cs[lay][128:256, tt * 512:(tt + 1) * 512])
                W("tensor", s_sy, lv)
                mmv = {}
                for oc in range(noc):
                    W("tensor", s_se, last_ev.get(oc, 0))
                    for ec in range(ECH):
                        mmv[oc] = op("tensor", "mm", out=pb[oc][:, 0:512],
                                     lhsT=P4[:, ec * nw + oc * 128: ec * nw + (oc + 1) * 128],
                                     rhs=xh[:, ec * 512:(ec + 1) * 512],
                                     start=(ec == 0), stop=(ec == ECH - 1),
                                     inc=(ec == ECH - 1))
                last_mm[buf] = mmv[noc - 1]
                for oc in range(noc):
                    is_v0 = (lay == 0 and oc == 3)
                    W("scalar", s_te, mmv[oc])
                    if is_v0:
                        W("scalar", s_ve, raw32_rv)
                        ev = op("scalar", "act", out=P3[:, 1024:1536], in_=pb[oc][:, 0:512],
                                func=AF.Copy, inc=True)
                        last_ev[oc] = ev
                        W("tensor", s_se, ev)
                        for b in range(4):
                            tb = pb[6 + (b % 2)]
                            W("tensor", s_ve, last_tr.get(b % 2, 0))
                            tv = op("tensor", "tr", out=tb[:, 0:128],
                                    in_=P3[:, 1024 + b * 128:1024 + (b + 1) * 128],
                                    ident=IDF[:, 0:128], inc=True)
                            W("vector", s_te, tv)
                            n = tt * 4 + b
                            cpv = op("vector", "tc",
                                     out=A1[:, 24576 + n * 128: 24576 + (n + 1) * 128],
                                     in_=tb[:, 0:128], inc=True)
                            last_tr[b % 2] = cpv
                            raw32_rv = cpv
                        continue
                    raw = STATB[:, (oc % 2) * 512:(oc % 2) * 512 + 512]
                    W("scalar", s_ve, slot_rv[oc % 2])
                    W("scalar", s_te, slot_tv[oc % 2])
                    ev = op("scalar", "act", out=raw, in_=pb[oc][:, 0:512], func=AF.Copy, inc=True)
                    last_ev[oc] = ev
                    rows = 64 if (lay == 1 and oc == 2) else 128
                    sw = pb[4 + (oc % 2)]
                    W("tensor", s_se, ev)
                    W("tensor", s_ve, slot_rv[oc % 2])
                    swv = op("tensor", "mm", out=sw[:, 0:512], lhsT=SWAP[lay], rhs=raw,
                             start=True, stop=True, inc=True)
                    slot_tv[oc % 2] = swv
                    C = CSb[0:rows, 0:512]
                    S = CSb[0:rows, 512:1024]
                    W("vector", s_sy, cv)
                    W("vector", s_se, ev)
                    W("vector", s_te, swv)
                    t1 = P3[0:rows, 0:512]
                    t2 = P3[0:rows, 512:1024]
                    op("vector", "tt", out=t1, in0=raw[0:rows, :], in1=C, op=ALU.mult)
                    op("vector", "tt", out=t2, in0=sw[0:rows, 0:512], in1=S, op=ALU.mult)
                    if lay == 1 and oc == 2:
                        dst = A1[0:64, 16384 + tt * 512: 16384 + (tt + 1) * 512]
                    else:
                        dst = A1[:, oc * 8192 + tt * 512: oc * 8192 + (tt + 1) * 512]
                    rv = op("vector", "tt", out=dst, in0=t1, in1=t2, op=ALU.add, inc=True)
                    slot_rv[oc % 2] = rv
                    last_rope = rv
                    if lay == 1 and oc == 2:
                        W("scalar", s_ve, raw32_rv)
                        ev2 = op("scalar", "act", out=P3[64:128, 1024:1536],
                                 in_=pb[oc][64:128, 0:512], func=AF.Copy, inc=True)
                        last_ev[oc] = ev2
                        W("tensor", s_se, ev2)
                        for b in range(4):
                            tb = pb[6 + (b % 2)]
                            W("tensor", s_ve, last_tr.get(b % 2, 0))
                            tv = op("tensor", "tr", out=tb[:, 0:64],
                                    in_=P3[64:128, 1024 + b * 128:1024 + (b + 1) * 128],
                                    ident=IDF[64:128, 64:128], tp=(64, 0), inc=True)
                            W("vector", s_te, tv)
                            n = tt * 4 + b
                            cpv = op("vector", "tc",
                                     out=A1[:, 24576 + n * 65: 24576 + n * 65 + 64],
                                     in_=tb[:, 0:64], inc=True)
                            last_tr[b % 2] = cpv
                            raw32_rv = cpv
            barrier()
            if lay == 1:
                dv = dma("sync", A1[64:128, 16384:24576], A1[0:64, 16384:24576])
                for eng in ENGS:
                    W(eng, s_sy, dv)

        def emit_bias(lay):
            scale = SC0 if lay == 0 else SC1
            krows = 128 if lay == 0 else 64
            srcs = [(A1[:, 0:8192], 128), (A1[:, 8192:16384], 128),
                    (A1[0:krows, 16384:24576], krows)]
            last_mm = 0
            for si, (src, rows) in enumerate(srcs):
                for tt in range(NTT):
                    sq = P4[0:rows, 0:512]
                    W("gpsimd", s_te, last_mm)
                    gv = op("gpsimd", "tt", out=sq, in0=src[:, tt * 512:(tt + 1) * 512],
                            in1=src[:, tt * 512:(tt + 1) * 512], op=ALU.mult, inc=True)
                    W("tensor", s_ge, gv)
                    last_mm = op("tensor", "mm", out=pb[6][0:1, 0:512], lhsT=ONESM[0:rows, 0:1],
                                 rhs=sq, start=True, stop=True, inc=True)
                    W("vector", s_te, last_mm)
                    v = op("vector", "tc", out=STAT[0:1, 0:512], in_=pb[6][0:1, 0:512], inc=True)
                    W("vector", s_ve, v)
                    v = op("vector", "rmax", out=STAT[0:1, 1024 + si * 16 + tt: 1025 + si * 16 + tt],
                           in_=STAT[0:1, 0:512], inc=True)
                    W("vector", s_ve, v)
                v = op("vector", "rmax", out=STAT[0:1, 1072 + si:1073 + si],
                       in_=STAT[0:1, 1024 + si * 16:1024 + si * 16 + 16], inc=True)
                W("vector", s_ve, v)
            for c in range(2):
                v = op("vector", "tt", out=STAT[0:1, 1076 + c:1077 + c],
                       in0=STAT[0:1, 1072 + c:1073 + c], in1=STAT[0:1, 1074:1075],
                       op=ALU.mult, inc=True)
                W("scalar", s_ve, v)
                op("scalar", "act", out=STAT[0:1, 1080 + c:1081 + c],
                   in_=STAT[0:1, 1076 + c:1077 + c], func=AF.Ln, bias=0.0, scale=1.0)
                v = op("scalar", "act", out=STAT[0:1, 1084 + c:1085 + c],
                       in_=STAT[0:1, 1080 + c:1081 + c], func=AF.Exp, bias=0.0, scale=0.5, inc=True)
                W("vector", s_se, v)
                v = op("vector", "ts", out=STAT[0:1, 1088 + c:1089 + c],
                       in_=STAT[0:1, 1084 + c:1085 + c], s=-scale, op=ALU.mult, inc=True)
                W("vector", s_ve, v)
                v = op("vector", "tc", out=STATB[0:1, 1024 + c:1025 + c],
                       in_=STAT[0:1, 1088 + c:1089 + c], inc=True)
                W("vector", s_ve, v)
            W("tensor", s_ve, s_ve.n)
            v = op("tensor", "mm", out=pb[6][:, 0:2], lhsT=ONESM[0:1, 0:128],
                   rhs=STATB[0:1, 1024:1026], start=True, stop=True, inc=True)
            W("vector", s_te, v)
            op("vector", "tc", out=BIAS[:, 0:2], in_=pb[6][:, 0:2], inc=True)
            barrier()

        def emit_attn_l0():
            exp_done = [0, 0]
            av_mm = [0, 0]
            av_mm_old = [0, 0]
            av_read = [0, 0]
            sums_read = 0
            last_norm = [0, 0]
            bc_done = [0, 0]
            dma_done = [0, 0]

            def emit_av(pm, pmi, pblo, pbhi, plast, pexp, ppar):
                for h in range(2):
                    W("tensor", s_se, pexp[h])
                    if pmi == 0:
                        W("tensor", s_se, av_read[h])
                        W("tensor", s_se, sums_read)
                    vap = A1[:, 24576 + pm * 128:24576 + (pm + 1) * 128]
                    pt = P4[:, ppar * 2048 + h * 1024 + pblo: ppar * 2048 + h * 1024 + pbhi]
                    op("tensor", "mm", out=AVB[:, h * 512 + pblo: h * 512 + pbhi],
                       lhsT=vap, rhs=pt, start=(pmi == 0), stop=plast)
                    av_mm[h] = op("tensor", "mm",
                                  out=AVB[32 * h:32 * h + 1, 1024 + pblo:1024 + pbhi],
                                  lhsT=ONESM[:, 0:1], rhs=pt, start=(pmi == 0), stop=plast,
                                  tp=(0, 32 * h), inc=True)

            for j in range(NTT):
                if j >= 1:
                    ms = [4 * j - 1] + [m for m in range(max(0, 4 * j - 4), 4 * j + 4)
                                        if m != 4 * j - 1]
                else:
                    ms = list(range(0, 4))
                pend = None
                for mi, m in enumerate(ms):
                    smin = max(m, 4 * j)
                    smax = min(m + 4, 4 * j + 3)
                    blo = (smin - 4 * j) * 128
                    bhi = (smax - 4 * j) * 128 + 128
                    last = (mi == len(ms) - 1)
                    par = mi % 2
                    smv = {}
                    for h in range(2):
                        W("tensor", s_se, exp_done[h])
                        if mi == 0:
                            W("tensor", s_ve, last_norm[h])
                        kap = A1[:, 16384 + m * 128:16384 + (m + 1) * 128]
                        qap = A1[:, h * 8192 + j * 512 + blo: h * 8192 + j * 512 + bhi]
                        smv[h] = op("tensor", "mm", out=SAB[:, h * 512 + blo: h * 512 + bhi],
                                    lhsT=kap, rhs=qap, start=True, stop=True, inc=True)
                    if pend is not None:
                        av_mm_old[:] = av_mm
                        emit_av(*pend)
                    evals = [0, 0]
                    for h in range(2):
                        mv = None
                        if 4 * j <= m + 4 <= 4 * j + 3:
                            c0 = (m + 4 - 4 * j) * 128
                            W("vector", s_te, smv[h])
                            mv = op("vector", "tt", out=SAB[:, h * 512 + c0: h * 512 + c0 + 128],
                                    in0=SAB[:, h * 512 + c0: h * 512 + c0 + 128],
                                    in1=EDGE, op=ALU.add, inc=True)
                        if m >= 4 * j:
                            c0 = (m - 4 * j) * 128
                            W("vector", s_te, smv[h])
                            mv = op("vector", "tt", out=SAB[:, h * 512 + c0: h * 512 + c0 + 128],
                                    in0=SAB[:, h * 512 + c0: h * 512 + c0 + 128],
                                    in1=TRI, op=ALU.add, inc=True)
                        W("scalar", s_te, smv[h])
                        W("scalar", s_te, av_mm_old[h])
                        if mv:
                            W("scalar", s_ve, mv)
                        pt = P4[:, par * 2048 + h * 1024 + blo: par * 2048 + h * 1024 + bhi]
                        evals[h] = op("scalar", "act", out=pt, in_=SAB[:, h * 512 + blo: h * 512 + bhi],
                                      func=AF.Exp, bias=BIAS[:, h:h + 1], scale=SC0, inc=True)
                        exp_done[h] = evals[h]
                    pend = (m, mi, blo, bhi, last, list(evals), par)
                emit_av(*pend)
                jb, jc = j // 2, (j % 2) * 512
                W("scalar", s_te, max(av_mm))
                W("scalar", s_te, max(bc_done))
                op("scalar", "act", out=STAT[0:1, 0:512], in_=AVB[0:1, 1024:1536],
                   func=AF.Ln, bias=0.0, scale=1.0)
                op("scalar", "act", out=STAT[32:33, 0:512], in_=AVB[32:33, 1024:1536],
                   func=AF.Ln, bias=0.0, scale=1.0)
                op("scalar", "act", out=STATB[0:1, 0:512], in_=STAT[0:1, 0:512],
                   func=AF.Exp, bias=0.0, scale=-1.0)
                sums_read = op("scalar", "act", out=STATB[32:33, 0:512], in_=STAT[32:33, 0:512],
                               func=AF.Exp, bias=0.0, scale=-1.0, inc=True)
                for h in range(2):
                    W("tensor", s_se, sums_read)
                    W("tensor", s_ve, last_norm[h])
                    bc_done[h] = op("tensor", "mm", out=SCD[:, h * 512:(h + 1) * 512],
                                    lhsT=ONESM[32 * h:32 * h + 1, 0:128],
                                    rhs=STATB[32 * h:32 * h + 1, 0:512], start=True, stop=True,
                                    tp=(32 * h, 0), inc=True)
                    W("scalar", s_te, bc_done[h])
                    W("scalar", s_ve, last_norm[h])
                    av_read[h] = op("scalar", "act",
                                    out=P4[:, 4096 + h * 512: 4096 + (h + 1) * 512],
                                    in_=AVB[:, h * 512:(h + 1) * 512], func=AF.Copy, inc=True)
                    W("vector", s_se, av_read[h])
                    W("vector", s_te, bc_done[h])
                    W("vector", s_sy, dma_done[h])
                    last_norm[h] = op("vector", "tt",
                                      out=P4[:, 5120 + h * 512: 5120 + (h + 1) * 512],
                                      in0=P4[:, 4096 + h * 512: 4096 + (h + 1) * 512],
                                      in1=SCD[:, h * 512:(h + 1) * 512], op=ALU.mult, inc=True)
                    W("sync", s_ve, last_norm[h])
                    dma_done[h] = dma("sync",
                                      avi[0][jb * 256 + h * 128: jb * 256 + (h + 1) * 128,
                                             jc:jc + 512],
                                      P4[:, 5120 + h * 512: 5120 + (h + 1) * 512])
            barrier()

        def emit_attn_l1():
            exp_done = [0, 0]
            av_mm = [0, 0, 0, 0]
            av_mm_old = [0, 0, 0, 0]
            av_read = 0
            last_norm = 0
            last_dma = 0
            bc_done = [0, 0, 0, 0]
            SP = [SAB, SCD]

            def emit_av(pn, poff, plast, pexp):
                ppar = pn % 2
                for h in range(4):
                    c = h // 2
                    W("tensor", s_se, pexp[c])
                    if pn == 0:
                        W("tensor", s_se, av_read)
                    vap = A1[:, 24576 + pn * 65: 24576 + pn * 65 + 65]
                    base = ppar * 4096 + c * 1024 + (h % 2) * 512
                    pt = P4[:, base + poff: base + 512]
                    av_mm[h] = op("tensor", "mm",
                                  out=AVB[0:65, h * 512 + poff: h * 512 + 512],
                                  lhsT=vap, rhs=pt, start=(pn == 0), stop=plast, inc=True)

            for j in range(NTT):
                nmax = 4 * j + 3
                pend = None
                for n in range(nmax + 1):
                    diag = n >= 4 * j
                    off = (n - 4 * j) * 128 if diag else 0
                    last = (n == nmax)
                    par = n % 2
                    smv = {}
                    for h in range(4):
                        c = h // 2
                        rb = (h % 2) * 64
                        W("tensor", s_se, exp_done[c])
                        if n == 0:
                            W("tensor", s_ve, last_norm)
                        kap = A1[rb:rb + 64, 16384 + n * 128:16384 + (n + 1) * 128]
                        qap = A1[rb:rb + 64, c * 8192 + j * 512 + off: c * 8192 + (j + 1) * 512]
                        smv[h] = op("tensor", "mm",
                                    out=SP[c][:, (h % 2) * 512 + off:(h % 2) * 512 + 512],
                                    lhsT=kap, rhs=qap, start=True, stop=True,
                                    tp=(rb, 0), inc=True)
                    if pend is not None:
                        av_mm_old[:] = av_mm
                        emit_av(*pend)
                    evals = [0, 0]
                    for c in range(2):
                        mv = None
                        if diag:
                            for hh in range(2):
                                W("vector", s_te, smv[2 * c + hh])
                                mv = op("vector", "tt",
                                        out=SP[c][:, hh * 512 + off: hh * 512 + off + 128],
                                        in0=SP[c][:, hh * 512 + off: hh * 512 + off + 128],
                                        in1=TRI, op=ALU.add, inc=True)
                        W("scalar", s_te, smv[2 * c + 1])
                        W("scalar", s_te, av_mm_old[2 * c])
                        W("scalar", s_te, av_mm_old[2 * c + 1])
                        if mv:
                            W("scalar", s_ve, mv)
                        base = par * 4096 + c * 1024
                        if off == 0:
                            evals[c] = op("scalar", "act", out=P4[:, base: base + 1024],
                                          in_=SP[c][:, 0:1024], func=AF.Exp,
                                          bias=BIAS[:, c:c + 1], scale=SC1, inc=True)
                        else:
                            op("scalar", "act", out=P4[:, base + off: base + 512],
                               in_=SP[c][:, off:512], func=AF.Exp,
                               bias=BIAS[:, c:c + 1], scale=SC1)
                            evals[c] = op("scalar", "act", out=P4[:, base + 512 + off: base + 1024],
                                          in_=SP[c][:, 512 + off:1024], func=AF.Exp,
                                          bias=BIAS[:, c:c + 1], scale=SC1, inc=True)
                        exp_done[c] = evals[c]
                    pend = (n, off, last, list(exp_done))
                emit_av(*pend)
                jb, jc = j // 2, (j % 2) * 512
                for h in range(4):
                    c = h // 2
                    avb = AVB[:, h * 512:(h + 1) * 512]
                    W("scalar", s_te, av_mm[h])
                    W("scalar", s_te, bc_done[h])
                    op("scalar", "act", out=STAT[64:65, h * 512:(h + 1) * 512],
                       in_=avb[64:65, 0:512], func=AF.Ln, bias=0.0, scale=1.0)
                    rcpv = op("scalar", "act", out=STATB[64:65, h * 512:(h + 1) * 512],
                              in_=STAT[64:65, h * 512:(h + 1) * 512], func=AF.Exp,
                              bias=0.0, scale=-1.0, inc=True)
                    W("scalar", s_ve, last_norm)
                    av_read = op("scalar", "act", out=STATB[0:64, 0:512],
                                 in_=avb[0:64, 0:512], func=AF.Copy, inc=True)
                    W("tensor", s_se, rcpv)
                    W("tensor", s_se, exp_done[c])
                    bc_done[h] = op("tensor", "mm",
                                    out=SP[c][0:64, (h % 2) * 512:(h % 2) * 512 + 512],
                                    lhsT=ONESM[64:65, 0:64],
                                    rhs=STATB[64:65, h * 512:(h + 1) * 512],
                                    start=True, stop=True, tp=(64, 0), inc=True)
                    W("vector", s_se, av_read)
                    W("vector", s_te, bc_done[h])
                    W("vector", s_sy, last_dma)
                    last_norm = op("vector", "tt", out=STATB[0:64, 512:1024],
                                   in0=STATB[0:64, 0:512],
                                   in1=SP[c][0:64, (h % 2) * 512:(h % 2) * 512 + 512],
                                   op=ALU.mult, inc=True)
                    W("sync", s_ve, last_norm)
                    last_dma = dma("sync",
                                   avi[1][jb * 256 + c * 128 + (h % 2) * 64:
                                          jb * 256 + c * 128 + (h % 2) * 64 + 64,
                                          jc:jc + 512],
                                   STATB[0:64, 512:1024])
            barrier()

        def emit_out_ln_add(tt, mms):
            evac_v = {}
            store_v = {}
            cast_v = {}
            stat_v = None
            pending = []

            def flush_stat(oc):
                nonlocal stat_v
                cv, gv = cast_v[oc]
                W("tensor", s_ve, cv)
                W("tensor", s_ge, gv)
                s = (oc % 2) * 512
                op("tensor", "mm", out=pb[6][0:1, 0:512], lhsT=ONESM[:, 0:1],
                   rhs=STATB[:, s:s + 512], start=(oc == 0), stop=(oc == ECH - 1), tp=(0, 0))
                stat_v = op("tensor", "mm", out=pb[6][32:33, 0:512], lhsT=ONESM[:, 0:1],
                            rhs=STATB[:, 1024 + s:1024 + s + 512], start=(oc == 0),
                            stop=(oc == ECH - 1), tp=(0, 32), inc=True)

            for oc in range(ECH):
                mv = mms(oc)
                W("scalar", s_te, mv)
                if oc >= 2:
                    W("scalar", s_sy, store_v[oc - 2])
                slot = P3[:, 1024 + (oc % 2) * 512: 1024 + (oc % 2) * 512 + 512]
                evac_v[oc] = op("scalar", "act", out=slot, in_=pb[4 + oc % 2][:, 0:512],
                                func=AF.Copy, inc=True)
                W("sync", s_se, evac_v[oc])
                store_v[oc] = dma("sync", aod[oc * 128:(oc + 1) * 128, :], slot)
                s = (oc % 2) * 512
                W("vector", s_se, evac_v[oc])
                if oc >= 2:
                    W("vector", s_te, stat_v)
                cv = op("vector", "tc", out=STATB[:, s:s + 512], in_=slot, inc=True)
                W("gpsimd", s_se, evac_v[oc])
                if oc >= 2:
                    W("gpsimd", s_te, stat_v)
                gv = op("gpsimd", "tt", out=STATB[:, 1024 + s:1024 + s + 512], in0=slot,
                        in1=slot, op=ALU.mult, inc=True)
                cast_v[oc] = (cv, gv)
                if oc >= 1:
                    flush_stat(oc - 1)
            flush_stat(ECH - 1)
            bcv = ln_tail(stat_v, 512)
            bcv = ln_tail(stat_v, 512)
            W("vector", s_te, bcv)
            op("vector", "tc", out=P3[:, 0:512], in_=pb[0][:, 0:512])
            bcv2 = op("vector", "tc", out=P3[:, 512:1024], in_=pb[1][:, 0:512], inc=True)
            lv = {}
            gvp = 0
            for ec in range(ECH):
                buf = ec % 4
                aslot = P3[:, 1024 + buf * 512: 1024 + buf * 512 + 512]
                if ec >= 4:
                    W("sync", s_ve, lv.get(ec - 4, 0))
                    W("sync", s_ge, gvp)
                adv = dma("sync", aslot, aod[ec * 128:(ec + 1) * 128, :])
                eng = "vector" if ec % 2 == 0 else "gpsimd"
                W(eng, s_sy, adv)
                W(eng, s_ve, bcv2)
                t1 = STAT[:, 1024:1536] if eng == "vector" else STAT[:, 1536:2048]
                op(eng, "tt", out=t1, in0=aslot, in1=P3[:, 0:512], op=ALU.mult)
                op(eng, "tt", out=t1, in0=t1, in1=P3[:, 512:1024], op=ALU.subtract)
                dst = xr[:, ec * TOWN + tt * 512: ec * TOWN + (tt + 1) * 512]
                v = op(eng, "tt", out=dst, in0=dst, in1=t1, op=ALU.add, inc=True)
                if eng == "vector":
                    lv[ec] = v
                else:
                    gvp = v
            barrier()

        def emit_wo(lay):
            W("sync", s_cast, cast_done[(f"wo{lay}", 1)])
            wv = dma("sync", A1[:, 0:32768].rearrange("p (dc e) -> p dc e", e=2048),
                     wob[lay].rearrange("(dc p) e -> p dc e", p=128))
            for tt in range(2):
                buf = tt % 2
                avt = A2[:, buf * 8192: (buf + 1) * 8192]
                lv = dma("sync", avt.rearrange("p (dc t) -> p dc t", t=512),
                         avo[lay][:, tt * 512:(tt + 1) * 512].rearrange("(dc p) t -> p dc t", p=128))

                def mms(oc, avt=avt, lv=lv, wv=wv):
                    W("tensor", s_sy, lv)
                    W("tensor", s_sy, wv)
                    v = None
                    for dc in range(ECH):
                        v = op("tensor", "mm", out=pb[4 + oc % 2][:, 0:512],
                               lhsT=A1[:, dc * 2048 + oc * 128: dc * 2048 + (oc + 1) * 128],
                               rhs=avt[:, dc * 512:(dc + 1) * 512],
                               start=(dc == 0), stop=(dc == ECH - 1), inc=(dc == ECH - 1))
                    return v

                emit_out_ln_add(tt, mms)

        def emit_ffn(lay):
            for tt in range(2):
                emit_ln_pre(lay, to_ag=False, tt=tt)
                if lay == 0:
                    order = list(range(HCH))
                else:
                    order = [c + 16 * m for c in range(16) for m in range(4)]
                wdma_v = {}
                humul_v = {}
                W1SLOT = [A2[:, 8192:12288], A2[:, 12288:16384], P4[:, 4096:8192]]
                for i, hc in enumerate(order):
                    slot = i % 3
                    if i >= 3:
                        W("gpsimd", s_te, wdma_v[i - 3][1])
                    wv = dma("gpsimd", W1SLOT[slot], w1[lay][hc * 128:(hc + 1) * 128, :])
                    gb, ub = pb[2 * (i % 2)], pb[2 * (i % 2) + 1]
                    W("tensor", s_gq, wv)
                    if i >= 2:
                        W("tensor", s_se, humul_v[i - 2][0])
                        W("tensor", s_ve, humul_v[i - 2][1])
                    mv = None
                    wt = W1SLOT[slot]
                    for ec in range(ECH):
                        op("tensor", "mm", out=gb[:, 0:512], lhsT=wt[:, ec * 256:ec * 256 + 128],
                           rhs=A2[:, ec * 512:(ec + 1) * 512], start=(ec == 0), stop=(ec == ECH - 1))
                        mv = op("tensor", "mm", out=ub[:, 0:512],
                                lhsT=wt[:, ec * 256 + 128:ec * 256 + 256],
                                rhs=A2[:, ec * 512:(ec + 1) * 512], start=(ec == 0),
                                stop=(ec == ECH - 1), inc=(ec == ECH - 1))
                    wdma_v[i] = (wv, mv)
                    W("scalar", s_te, mv)
                    if i >= 2:
                        W("scalar", s_ve, humul_v[i - 2][1])
                    s2 = i % 2
                    sv = op("scalar", "act", out=P4[:, s2 * 512: (s2 + 1) * 512],
                            in_=gb[:, 0:512], func=AF.Silu, inc=True)
                    W("vector", s_te, mv)
                    W("vector", s_se, sv)
                    uv = op("vector", "tc", out=P4[:, 1024 + s2 * 512: 1024 + (s2 + 1) * 512],
                            in_=ub[:, 0:512], inc=True)
                    hv = op("vector", "tt", out=A1[:, hc * 512:(hc + 1) * 512],
                            in0=P4[:, s2 * 512:(s2 + 1) * 512],
                            in1=P4[:, 1024 + s2 * 512: 1024 + (s2 + 1) * 512],
                            op=ALU.mult, inc=True)
                    humul_v[i] = (sv, hv)
                    if lay == 1 and i % 4 == 3:
                        emit_sparsify(order[i - 3:i + 1], hv)
                barrier()
                wdown_v = {}
                mm_done = {}

                def mkmms():
                    W2SLOT = [A2[:, 0:8192], A2[:, 8192:16384], P4[:, 0:8192]]

                    def mms(oc):
                        slot = oc % 3
                        if oc >= 3:
                            W("gpsimd", s_te, mm_done[oc - 3])
                        wv = dma("gpsimd", W2SLOT[slot], w2[lay][oc * 128:(oc + 1) * 128, :])
                        W("tensor", s_gq, wv)
                        v = None
                        wt = W2SLOT[slot]
                        for hc in range(HCH):
                            v = op("tensor", "mm", out=pb[4 + oc % 2][:, 0:512],
                                   lhsT=wt[:, hc * 128:(hc + 1) * 128],
                                   rhs=A1[:, hc * 512:(hc + 1) * 512],
                                   start=(hc == 0), stop=(hc == HCH - 1), inc=(hc == HCH - 1))
                        mm_done[oc] = v
                        return v
                    return mms

                emit_out_ln_add(tt, mkmms())

        def emit_sparsify(quartet, last_hv):
            H = [A1[:, hc * 512:(hc + 1) * 512] for hc in quartet]
            sq = [P3[:, k * 512:(k + 1) * 512] for k in range(4)]
            s1 = STAT[:, 1024:1536]
            t1 = STAT[:, 1536:2048]
            s2 = STAT[:, 0:512]
            t2 = STAT[:, 512:1024]
            W("vector", s_ve, last_hv)
            for k in range(4):
                op("vector", "tt", out=sq[k], in0=H[k], in1=H[k], op=ALU.mult)
            op("vector", "tt", out=s1, in0=sq[0], in1=sq[1], op=ALU.max)
            op("vector", "tt", out=t1, in0=sq[0], in1=sq[1], op=ALU.min)
            op("vector", "tt", out=s2, in0=sq[2], in1=sq[3], op=ALU.max)
            op("vector", "tt", out=t2, in0=sq[2], in1=sq[3], op=ALU.min)
            op("vector", "tt", out=s1, in0=s1, in1=s2, op=ALU.min)
            op("vector", "tt", out=t1, in0=t1, in1=t2, op=ALU.max)
            op("vector", "tt", out=s1, in0=s1, in1=t1, op=ALU.max)
            for k in range(4):
                op("vector", "tt", out=t2, in0=sq[k], in1=s1, op=ALU.is_ge)
                op("vector", "tt", out=H[k], in0=H[k], in1=t2, op=ALU.mult,
                   inc=(k == 3))

        def tap_dram(name, src, shape, dtype):
            t = mktap(name, shape, dtype)
            if t is not None:
                v = dma("sync", t, src)
                for eng in ENGS:
                    W(eng, s_sy, v)

        def tap_xr(name):
            t = mktap(name, [E, TOWN], F32)
            if t is not None:
                v = None
                for ec in range(ECH):
                    v = dma("sync", t[ec * 128:(ec + 1) * 128, :], xr[:, ec * TOWN:(ec + 1) * TOWN])
                for eng in ENGS:
                    W(eng, s_sy, v)

        def tap_sbuf(name, src, shape, dtype):
            t = mktap(name, shape, dtype)
            if t is not None:
                v = dma("sync", t, src)
                for eng in ENGS:
                    W(eng, s_sy, v)

        step = [0]

        def go():
            step[0] += 1
            return step[0] <= upto

        for lay in range(2):
            if go():
                emit_ln_pre(lay, to_ag=True, tt=0)
                emit_ln_pre(lay, to_ag=True, tt=1)
                tap_dram(f"xh{lay}", agi[lay][:], [E, TOWN], BF16)
            if go():
                v = cc("AllGather", [agi[lay][:]], [ago[lay][:]])
                if lay == 0:
                    bg_cast_all()
                for eng in ENGS:
                    W(eng, s_cc, v)
            if go():
                emit_qkv(lay)
                tap_sbuf(f"qkv{lay}", A1[:, 0:32768], [128, 32768], BF16)
            if go():
                emit_bias(lay)
                tap_sbuf(f"bias{lay}", BIAS[:, 0:16], [128, 16], F32)
            if go():
                if lay == 0:
                    emit_attn_l0()
                else:
                    emit_attn_l1()
                tap_dram(f"av{lay}", avi[lay][:], [E, TOWN], BF16)
            if go():
                v = cc("AllToAll", [avi[lay][:]], [avo[lay][:]])
                for eng in ENGS:
                    W(eng, s_cc, v)
            if go():
                emit_wo(lay)
                tap_xr(f"xa{lay}")
            if go():
                emit_ffn(lay)
                tap_xr(f"xf{lay}")

        barrier()
        v = None
        for ec in range(ECH):
            v = dma("sync", y_out[ec * 128:(ec + 1) * 128, :], xr[:, ec * TOWN:(ec + 1) * TOWN])
        for eng in ENGS:
            W(eng, s_sy, v)

        block = st_.enter_context(nc.Block())
        with allow_ici_all_to_all():
            @block.sync
            def _(e):
                p.emit_into(e, "sync")

            @block.gpsimd
            def _(e):
                p.emit_into(e, "gpsimd")

            @block.tensor
            def _(e):
                p.emit_into(e, "tensor")

            @block.scalar
            def _(e):
                p.emit_into(e, "scalar")

            @block.vector
            def _(e):
                p.emit_into(e, "vector")

    ninstr = {e: len(p.q[e]) for e in ENGS}
    return nc, tap_aps, ninstr


def _host_tables():
    inv0 = (10000.0 ** (-(np.arange(0, 128, 2, dtype=np.float64)) / 128.0))
    ang0 = np.arange(TFULL, dtype=np.float64)[None, :] * inv0[:, None]
    c0 = np.cos(ang0)
    s0 = np.sin(ang0)
    cs0 = np.empty((256, TFULL), np.float32)
    cs0[0:64] = c0
    cs0[64:128] = c0
    cs0[128:192] = -s0
    cs0[192:256] = s0
    inv1 = (10000.0 ** (-(np.arange(0, 64, 2, dtype=np.float64)) / 64.0))
    ang1 = np.arange(TFULL, dtype=np.float64)[None, :] * inv1[:, None]
    c1 = np.cos(ang1)
    s1 = np.sin(ang1)
    cs1 = np.empty((256, TFULL), np.float32)
    for blk in range(2):
        b = blk * 64
        cs1[b + 0:b + 32] = c1
        cs1[b + 32:b + 64] = c1
        cs1[128 + b + 0:128 + b + 32] = -s1
        cs1[128 + b + 32:128 + b + 64] = s1
    return cs0, cs1


def _host_consts():
    pq = np.arange(128)
    tri = np.where(pq[:, None] <= pq[None, :], 0.0, -1e9).astype(np.float32)
    edge = np.where(pq[None, :] < pq[:, None], 0.0, -1e9).astype(np.float32)
    ident = np.eye(128, dtype=np.float32)
    sw0 = np.zeros((128, 128), np.float32)
    sw0[(pq + 64) % 128, pq] = 1.0
    sig1 = np.where(pq % 64 < 32, pq + 32, pq - 32)
    sw1 = np.zeros((128, 128), np.float32)
    sw1[sig1, pq] = 1.0
    return np.concatenate([tri, edge, ident, sw0, sw1], axis=0)


def _blocks(a, nblk, rows):
    R, C = a.shape
    assert C == nblk * rows
    return np.ascontiguousarray(a.reshape(R, nblk, rows).transpose(1, 0, 2).reshape(nblk * R, rows))


_BUILD_CACHE = {}


def _get_build(taps=(), upto=99):
    key = (tuple(sorted(taps)), upto)
    if key not in _BUILD_CACHE:
        _BUILD_CACHE[key] = build(taps, upto)
    return _BUILD_CACHE[key]


def kernel(L0_attn_x, L0_attn_Wq, L0_attn_Wk, L0_attn_Wv, L0_attn_Wo,
           L0_ffn_w_gate, L0_ffn_w_up, L0_ffn_w_down,
           L1_attn_wq, L1_attn_wk, L1_attn_wv, L1_attn_wo,
           L1_ffn_w_gate, L1_ffn_w_up, L1_ffn_w_down, _taps=(), _upto=99):
    f = np.float32
    x = np.asarray(L0_attn_x, f).reshape(TFULL, E)
    cs0, cs1 = _host_tables()
    consts = _host_consts()

    wq0T = np.asarray(L0_attn_Wq, f).T
    wk0T = np.asarray(L0_attn_Wk, f).T
    wv0T = np.asarray(L0_attn_Wv, f).T
    wo0 = np.ascontiguousarray(np.asarray(L0_attn_Wo, f).T)
    g0T = np.asarray(L0_ffn_w_gate, f).T
    u0T = np.asarray(L0_ffn_w_up, f).T
    d0T = np.asarray(L0_ffn_w_down, f).T
    def _packw1(gT, uT):
        blk = np.concatenate([gT.reshape(E, HCH, 128), uT.reshape(E, HCH, 128)],
                             axis=2).transpose(1, 0, 2)
        return np.ascontiguousarray(
            blk.reshape(HCH, 16, 128, 256).transpose(0, 2, 1, 3).reshape(HCH * 128, 4096))

    def _packw2(dT):
        blk = dT.reshape(HID, ECH, 128).transpose(1, 0, 2)
        return np.ascontiguousarray(
            blk.reshape(ECH, 64, 128, 128).transpose(0, 2, 1, 3).reshape(ECH * 128, 8192))

    w10 = _packw1(g0T, u0T)
    w20 = _packw2(d0T)

    wq1 = np.asarray(L1_attn_wq, f)
    wk1 = np.asarray(L1_attn_wk, f)
    wv1 = np.asarray(L1_attn_wv, f)
    wo1 = np.ascontiguousarray(np.asarray(L1_attn_wo, f))
    perm = 4 * (np.arange(HID) % 2048) + (np.arange(HID) // 2048)
    g1 = np.asarray(L1_ffn_w_gate, f)[:, perm]
    u1 = np.asarray(L1_ffn_w_up, f)[:, perm]
    d1 = np.asarray(L1_ffn_w_down, f)[perm, :]
    w11 = _packw1(g1, u1)
    w21 = _packw2(d1)

    in_maps = []
    for i in range(NCORES):
        kv0 = i // 2
        wqkv0 = np.ascontiguousarray(np.concatenate(
            [wq0T[:, 256 * i:256 * (i + 1)],
             wk0T[:, 128 * kv0:128 * (kv0 + 1)],
             wv0T[:, 128 * kv0:128 * (kv0 + 1)]], axis=1))
        wqkv1 = np.ascontiguousarray(np.concatenate(
            [wq1[:, 256 * i:256 * (i + 1)],
             wk1[:, 64 * i:64 * (i + 1)],
             wv1[:, 64 * i:64 * (i + 1)]], axis=1))
        x_own = np.ascontiguousarray(x[TOWN * i:TOWN * (i + 1), :].T)
        in_maps.append(dict(
            x_in=x_own, wqkv0=wqkv0, wo0=wo0, w10=w10, w20=w20,
            wqkv1=wqkv1, wo1=wo1, w11=w11, w21=w21,
            cs0=cs0, cs1=cs1, consts=consts))

    nc, tap_aps, _ = _get_build(_taps, _upto)
    res = run_bass_kernel_spmd(nc, in_maps, core_ids=list(range(NCORES)))
    outs = [np.asarray(res.results[i]["y_out"]).T for i in range(NCORES)]
    y = np.concatenate(outs, axis=0).reshape(1, TFULL, E)
    if _taps:
        kernel.tap_results = [
            {k: np.asarray(res.results[i]["tap_" + k]) for k in _taps}
            for i in range(NCORES)]
    return y


if __name__ == "__main__":
    nc, tap_aps, ninstr = build()
    print("instr counts:", ninstr, "total:", sum(ninstr.values()))


# version 15
# speedup vs baseline: 1.2547x; 1.1274x over previous
import contextlib
import math
import numpy as np
import ml_dtypes
import concourse.bass as bass
import concourse.mybir as mybir
from concourse.bass_utils import run_bass_kernel_spmd
from concourse.aot_env import allow_ici_all_to_all

F32 = mybir.dt.float32
BF16 = mybir.dt.bfloat16
AF = mybir.ActivationFunctionType
ALU = mybir.AluOpType
bfnp = ml_dtypes.bfloat16

NCORES = 8
E = 2048
ECH = 16
TFULL = 8192
TOWN = 1024
NTT = 16
HID = 8192
HCH = 64
SC0 = 1.0 / math.sqrt(128.0)
SC1 = 0.125

ENGS = ("sync", "gpsimd", "tensor", "scalar", "vector")


class Sem:
    def __init__(self, h):
        self.h = h
        self.n = 0

    def inc(self, k=1):
        self.n += k
        return self.n


class Prog:
    def __init__(self):
        self.q = {e: [] for e in ENGS}

    def emit_into(self, eng, name):
        for kind, a in self.q[name]:
            if kind == "wait":
                eng.wait_ge(a[0], a[1])
                continue
            if kind == "dma":
                inst = eng.dma_start(out=a["out"], in_=a["in_"])
            elif kind == "mm":
                kw = dict(out=a["out"], lhsT=a["lhsT"], rhs=a["rhs"],
                          start=a["start"], stop=a["stop"])
                if a.get("tp") is not None:
                    kw["tile_position"] = a["tp"]
                inst = eng.matmul(**kw)
            elif kind == "tr":
                kw = {}
                if a.get("tp") is not None:
                    kw["tile_position"] = a["tp"]
                inst = eng.transpose(a["out"], a["in_"], a["ident"], **kw)
            elif kind == "act":
                kw = dict(out=a["out"], in_=a["in_"], func=a["func"])
                if a.get("bias") is not None:
                    kw["bias"] = a["bias"]
                if a.get("scale") is not None:
                    kw["scale"] = a["scale"]
                inst = eng.activation(**kw)
            elif kind == "tt":
                inst = eng.tensor_tensor(out=a["out"], in0=a["in0"], in1=a["in1"], op=a["op"])
            elif kind == "tc":
                inst = eng.tensor_copy(out=a["out"], in_=a["in_"])
            elif kind == "ts":
                inst = eng.tensor_single_scalar(out=a["out"], in_=a["in_"], scalar=a["s"], op=a["op"])
            elif kind == "recipf":
                inst = eng.reciprocal_approx_fast(a["out"], a["in_"])
            elif kind == "rmax":
                inst = eng.reduce_max(a["out"], a["in_"], axis=mybir.AxisListType.X)
            elif kind == "memset":
                inst = eng.memset(a["ap"], a["val"])
            elif kind == "cc":
                inst = eng.collective_compute(a["kind"], a["op"], replica_groups=a["rg"],
                                              ins=a["ins"], outs=a["outs"])
            else:
                raise AssertionError(kind)
            if a.get("inc") is not None:
                s, k = a["inc"]
                inst.then_inc(s, k)


def build(taps=(), upto=99):
    taps = set(taps)
    nc = bass.Bass("TRN2", target_bir_lowering=False)

    x_in = nc.dram_tensor("x_in", [E, TOWN], F32, kind="ExternalInput").ap()
    wqkv = [nc.dram_tensor("wqkv0", [E, 512], F32, kind="ExternalInput").ap(),
            nc.dram_tensor("wqkv1", [E, 384], F32, kind="ExternalInput").ap()]
    wo = [nc.dram_tensor("wo0", [E, E], F32, kind="ExternalInput").ap(),
          nc.dram_tensor("wo1", [E, E], F32, kind="ExternalInput").ap()]
    w1 = [nc.dram_tensor("w10", [HCH * 128, 4096], F32, kind="ExternalInput").ap(),
          nc.dram_tensor("w11", [HCH * 128, 4096], F32, kind="ExternalInput").ap()]
    w2 = [nc.dram_tensor("w20", [16 * 128, 8192], F32, kind="ExternalInput").ap(),
          nc.dram_tensor("w21", [16 * 128, 8192], F32, kind="ExternalInput").ap()]
    w1b = [nc.dram_tensor(f"w1b{l}", [HCH * 128, 4096], BF16, kind="Internal").ap() for l in range(2)]
    w2b = [nc.dram_tensor(f"w2b{l}", [16 * 128, 8192], BF16, kind="Internal").ap() for l in range(2)]
    wob = [nc.dram_tensor(f"wob{l}", [E, E], BF16, kind="Internal").ap() for l in range(2)]
    wqb1 = nc.dram_tensor("wqb1", [E, 384], BF16, kind="Internal").ap()
    cs = [nc.dram_tensor("cs0", [256, TFULL], F32, kind="ExternalInput").ap(),
          nc.dram_tensor("cs1", [256, TFULL], F32, kind="ExternalInput").ap()]
    consts = nc.dram_tensor("consts", [5 * 128, 128], F32, kind="ExternalInput").ap()
    y_out = nc.dram_tensor("y_out", [E, TOWN], F32, kind="ExternalOutput").ap()

    agi = [nc.dram_tensor(f"ag{l}i", [E, TOWN], BF16, kind="Internal").ap() for l in range(2)]
    ago = [nc.dram_tensor(f"ag{l}o", [NCORES * E, TOWN], BF16, kind="Internal", addr_space="Shared").ap() for l in range(2)]
    avi = [nc.dram_tensor(f"av{l}i", [E, TOWN], BF16, kind="Internal").ap() for l in range(2)]
    avo = [nc.dram_tensor(f"av{l}o", [E, TOWN], BF16, kind="Internal").ap() for l in range(2)]
    aod = nc.dram_tensor("aod", [E, 512], F32, kind="Internal").ap()

    tap_aps = {}

    def mktap(name, shape, dtype):
        if name in taps:
            tap_aps[name] = nc.dram_tensor("tap_" + name, shape, dtype, kind="ExternalOutput").ap()
        return tap_aps.get(name)

    st_ = contextlib.ExitStack()
    with st_:
        xr = st_.enter_context(nc.sbuf_tensor("xr", [128, ECH * TOWN], F32))
        A1 = st_.enter_context(nc.sbuf_tensor("A1", [128, 32768], BF16))
        A2 = st_.enter_context(nc.sbuf_tensor("A2", [128, 16384], BF16))
        P3 = st_.enter_context(nc.sbuf_tensor("P3", [128, 3072], F32))
        P4 = st_.enter_context(nc.sbuf_tensor("P4", [128, 8192], BF16))
        CONS = st_.enter_context(nc.sbuf_tensor("CONS", [128, 640], BF16))
        ONESM = st_.enter_context(nc.sbuf_tensor("ONESM", [128, 128], BF16))
        STAT = st_.enter_context(nc.sbuf_tensor("STAT", [128, 2048], F32))
        STATB = st_.enter_context(nc.sbuf_tensor("STATB", [128, 2048], BF16))
        CSb = st_.enter_context(nc.sbuf_tensor("CSb", [128, 1024], F32))
        BIAS = st_.enter_context(nc.sbuf_tensor("BIAS", [128, 16], F32))
        IDF = st_.enter_context(nc.sbuf_tensor("IDF", [128, 128], F32))

        SAB = st_.enter_context(nc.psum_tensor("SAB", [128, 1024], F32))
        SCD = st_.enter_context(nc.psum_tensor("SCD", [128, 1024], F32))
        AVB = st_.enter_context(nc.psum_tensor("AVB", [128, 2048], F32))
        pb = [SAB[:, 0:512], SAB[:, 512:1024], SCD[:, 0:512], SCD[:, 512:1024],
              AVB[:, 0:512], AVB[:, 512:1024], AVB[:, 1024:1536], AVB[:, 1536:2048]]

        s_sy = Sem(st_.enter_context(nc.semaphore("s_sy")))
        s_gq = Sem(st_.enter_context(nc.semaphore("s_gq")))
        s_cc = Sem(st_.enter_context(nc.semaphore("s_cc")))
        s_te = Sem(st_.enter_context(nc.semaphore("s_te")))
        s_se = Sem(st_.enter_context(nc.semaphore("s_se")))
        s_ve = Sem(st_.enter_context(nc.semaphore("s_ve")))
        s_ge = Sem(st_.enter_context(nc.semaphore("s_ge")))
        s_cast = Sem(st_.enter_context(nc.semaphore("s_cast")))
        ALLSEMS = [s_sy, s_gq, s_cc, s_te, s_se, s_ve, s_ge]
        ESEM = {"tensor": s_te, "scalar": s_se, "vector": s_ve, "gpsimd": s_ge}

        p = Prog()

        def W(eng, sem, val):
            if val:
                p.q[eng].append(("wait", (sem.h, val)))

        def dma(eng, out, in_, sem=None):
            s = sem if sem is not None else (s_sy if eng == "sync" else s_gq)
            v = s.inc(16)
            p.q[eng].append(("dma", dict(out=out, in_=in_, inc=(s.h, 16))))
            return v

        def op(eng, kind, inc=False, **a):
            s = ESEM[eng]
            v = None
            if inc:
                v = s.inc(1)
                a["inc"] = (s.h, 1)
            p.q[eng].append((kind, a))
            return v

        def cc(kind, ins, outs):
            v = s_cc.inc(1)
            p.q["gpsimd"].append(("cc", dict(kind=kind, op=ALU.bypass,
                                             rg=[list(range(NCORES))], ins=ins, outs=outs,
                                             inc=(s_cc.h, 1))))
            return v

        def barrier():
            vals = [(s, s.n) for s in ALLSEMS]
            for eng in ENGS:
                for s, v in vals:
                    W(eng, s, v)

        TRI = CONS[:, 0:128]
        EDGE = CONS[:, 128:256]
        IDENT = CONS[:, 256:384]
        SWAP = [CONS[:, 384:512], CONS[:, 512:640]]

        dma("gpsimd", CONS[:].rearrange("p (n c) -> p n c", c=128),
            consts.rearrange("(n p) c -> p n c", p=128))
        dma("sync", IDF[:], consts[256:384, :])
        op("gpsimd", "memset", ap=ONESM[:], val=1.0, inc=True)
        qkv0_wv = dma("gpsimd", P4[:, 0:ECH * 512].rearrange("p (c f) -> p c f", f=512),
                      wqkv[0].rearrange("(c p) f -> p c f", p=128))
        for ec in range(ECH):
            dma("sync", xr[:, ec * TOWN:(ec + 1) * TOWN], x_in[ec * 128:(ec + 1) * 128, :])
        cast_done = {}

        def bg_cast(key, dst, srcten, nrows, piece_rows):
            for p0 in range(0, nrows, piece_rows):
                v = dma("gpsimd", dst[p0:p0 + piece_rows, :], srcten[p0:p0 + piece_rows, :],
                        sem=s_cast)
                cast_done[(key, p0 // piece_rows)] = v

        def bg_cast_all():
            bg_cast("wo0", wob[0], wo[0], E, 1024)
            bg_cast("wo1", wob[1], wo[1], E, 1024)
        barrier()

        def ln_tail(mm_v, ncols):
            A = STAT[0:1, 0:512][:, 0:ncols]
            B = STAT[0:1, 512:1024][:, 0:ncols]
            C = STAT[0:1, 1024:1536][:, 0:ncols]
            D = STAT[0:1, 1536:2048][:, 0:ncols]
            W("vector", s_te, mm_v)
            op("vector", "ts", out=A, in_=pb[6][0:1, 0:ncols], s=1.0 / E, op=ALU.mult)
            op("vector", "ts", out=B, in_=pb[6][32:33, 0:ncols], s=1.0 / E, op=ALU.mult)
            op("vector", "tt", out=C, in0=A, in1=A, op=ALU.mult)
            op("vector", "tt", out=B, in0=B, in1=C, op=ALU.subtract)
            v = op("vector", "ts", out=B, in_=B, s=1e-5, op=ALU.add, inc=True)
            W("scalar", s_ve, v)
            op("scalar", "act", out=B, in_=B, func=AF.Ln, bias=0.0, scale=1.0)
            sv = op("scalar", "act", out=D, in_=B, func=AF.Exp, bias=0.0, scale=-0.5, inc=True)
            W("vector", s_se, sv)
            op("vector", "tt", out=C, in0=A, in1=D, op=ALU.mult)
            op("vector", "tc", out=STATB[0:1, 0:ncols], in_=D)
            op("vector", "tc", out=A, in_=STATB[0:1, 0:ncols])
            op("vector", "tt", out=A, in0=D, in1=A, op=ALU.subtract)
            op("vector", "tc", out=STATB[0:1, 512:512 + ncols], in_=A)
            op("vector", "tc", out=STATB[0:1, 1024:1024 + ncols], in_=C)
            op("vector", "tc", out=B, in_=STATB[0:1, 1024:1024 + ncols])
            op("vector", "tt", out=B, in0=C, in1=B, op=ALU.subtract)
            v = op("vector", "tc", out=STATB[0:1, 1536:1536 + ncols], in_=B, inc=True)
            W("tensor", s_ve, v)
            op("tensor", "mm", out=pb[0][:, 0:ncols], lhsT=ONESM[0:1, 0:128],
               rhs=STATB[0:1, 0:ncols], start=True, stop=False)
            op("tensor", "mm", out=pb[0][:, 0:ncols], lhsT=ONESM[0:1, 0:128],
               rhs=STATB[0:1, 512:512 + ncols], start=False, stop=True)
            op("tensor", "mm", out=pb[1][:, 0:ncols], lhsT=ONESM[0:1, 0:128],
               rhs=STATB[0:1, 1024:1024 + ncols], start=True, stop=False)
            v = op("tensor", "mm", out=pb[1][:, 0:ncols], lhsT=ONESM[0:1, 0:128],
                   rhs=STATB[0:1, 1536:1536 + ncols], start=False, stop=True, inc=True)
            return v

        def emit_ln_pre(lay, to_ag, tt):
            vv = tv = None
            for ec in range(ECH):
                src = xr[:, ec * TOWN + tt * 512: ec * TOWN + (tt + 1) * 512]
                vv = op("vector", "tc", out=A2[:, ec * 512:(ec + 1) * 512], in_=src, inc=True)
                vv = op("vector", "tt", out=A2[:, 8192 + ec * 512: 8192 + (ec + 1) * 512],
                        in0=src, in1=src, op=ALU.mult, inc=True)
            W("tensor", s_ve, vv)
            mmv = None
            for ec in range(ECH):
                op("tensor", "mm", out=pb[6][0:1, 0:512], lhsT=ONESM[:, 0:1],
                   rhs=A2[:, ec * 512:(ec + 1) * 512], start=(ec == 0), stop=(ec == ECH - 1),
                   tp=(0, 0))
                mmv = op("tensor", "mm", out=pb[6][32:33, 0:512], lhsT=ONESM[:, 0:1],
                         rhs=A2[:, 8192 + ec * 512: 8192 + (ec + 1) * 512], start=(ec == 0),
                         stop=(ec == ECH - 1), tp=(0, 32), inc=True)
            bcv = ln_tail(mmv, 512)
            for ec in range(ECH):
                src = xr[:, ec * TOWN + tt * 512: ec * TOWN + (tt + 1) * 512]
                W("vector", s_te, bcv)
                op("vector", "tt", out=STAT[:, 1024:1536], in0=src, in1=pb[0][:, 0:512], op=ALU.mult)
                if to_ag:
                    dst = A2[:, 8192 + ec * 512: 8192 + (ec + 1) * 512]
                else:
                    dst = A2[:, ec * 512:(ec + 1) * 512]
                vv = op("vector", "tt", out=dst, in0=STAT[:, 1024:1536], in1=pb[1][:, 0:512],
                        op=ALU.subtract, inc=True)
                if to_ag:
                    W("sync", s_ve, vv)
                    dma("sync", agi[lay][ec * 128:(ec + 1) * 128, tt * 512:(tt + 1) * 512], dst)
            barrier()

        def emit_qkv(lay):
            nw = 512 if lay == 0 else 384
            noc = 4 if lay == 0 else 3
            vstride = 128 if lay == 0 else 65
            vdw = 128 if lay == 0 else 64
            if lay == 0:
                wv = qkv0_wv
                W("tensor", s_gq, wv)
            else:
                wv = dma("gpsimd", P4[:, 0:ECH * nw].rearrange("p (c f) -> p c f", f=nw),
                         wqkv[lay].rearrange("(c p) f -> p c f", p=128))
                W("tensor", s_gq, wv)
            if lay == 1:
                mz = op("gpsimd", "memset", ap=A1[:, 24576:24576 + 65 * 64], val=1.0, inc=True)
                W("vector", s_ge, mz)
            slot_rv = {0: 0, 1: 0}
            slot_tv = {0: 0, 1: 0}
            last_ev = {}
            last_mm = {0: 0, 1: 0}
            last_rope = 0
            last_tr = {}
            raw32_rv = 0
            for tt in range(NTT):
                buf = tt % 2
                xh = A2[:, buf * 8192:(buf + 1) * 8192]
                r, c0 = tt // 2, (tt % 2) * 512
                W("sync", s_te, last_mm[buf])
                lv = dma("sync", xh.rearrange("p (c t) -> p c t", t=512),
                         ago[lay][E * r:E * (r + 1), c0:c0 + 512].rearrange("(c p) t -> p c t", p=128))
                W("sync", s_ve, last_rope)
                dma("sync", CSb[:, 0:512], cs[lay][0:128, tt * 512:(tt + 1) * 512])
                cv = dma("sync", CSb[:, 512:1024], cs[lay][128:256, tt * 512:(tt + 1) * 512])
                W("tensor", s_sy, lv)
                mmv = {}
                for oc in range(noc):
                    W("tensor", s_se, last_ev.get(oc, 0))
                    for ec in range(ECH):
                        mmv[oc] = op("tensor", "mm", out=pb[oc][:, 0:512],
                                     lhsT=P4[:, ec * nw + oc * 128: ec * nw + (oc + 1) * 128],
                                     rhs=xh[:, ec * 512:(ec + 1) * 512],
                                     start=(ec == 0), stop=(ec == ECH - 1),
                                     inc=(ec == ECH - 1))
                last_mm[buf] = mmv[noc - 1]
                for oc in range(noc):
                    is_v0 = (lay == 0 and oc == 3)
                    W("scalar", s_te, mmv[oc])
                    if is_v0:
                        W("scalar", s_ve, raw32_rv)
                        ev = op("scalar", "act", out=P3[:, 1024:1536], in_=pb[oc][:, 0:512],
                                func=AF.Copy, inc=True)
                        last_ev[oc] = ev
                        W("tensor", s_se, ev)
                        for b in range(4):
                            tb = pb[6 + (b % 2)]
                            W("tensor", s_ve, last_tr.get(b % 2, 0))
                            tv = op("tensor", "tr", out=tb[:, 0:128],
                                    in_=P3[:, 1024 + b * 128:1024 + (b + 1) * 128],
                                    ident=IDF[:, 0:128], inc=True)
                            W("vector", s_te, tv)
                            n = tt * 4 + b
                            cpv = op("vector", "tc",
                                     out=A1[:, 24576 + n * 128: 24576 + (n + 1) * 128],
                                     in_=tb[:, 0:128], inc=True)
                            last_tr[b % 2] = cpv
                            raw32_rv = cpv
                        continue
                    raw = STATB[:, (oc % 2) * 512:(oc % 2) * 512 + 512]
                    W("scalar", s_ve, slot_rv[oc % 2])
                    W("scalar", s_te, slot_tv[oc % 2])
                    ev = op("scalar", "act", out=raw, in_=pb[oc][:, 0:512], func=AF.Copy, inc=True)
                    last_ev[oc] = ev
                    rows = 64 if (lay == 1 and oc == 2) else 128
                    sw = pb[4 + (oc % 2)]
                    W("tensor", s_se, ev)
                    W("tensor", s_ve, slot_rv[oc % 2])
                    swv = op("tensor", "mm", out=sw[:, 0:512], lhsT=SWAP[lay], rhs=raw,
                             start=True, stop=True, inc=True)
                    slot_tv[oc % 2] = swv
                    C = CSb[0:rows, 0:512]
                    S = CSb[0:rows, 512:1024]
                    W("vector", s_sy, cv)
                    W("vector", s_se, ev)
                    W("vector", s_te, swv)
                    t1 = P3[0:rows, 0:512]
                    t2 = P3[0:rows, 512:1024]
                    op("vector", "tt", out=t1, in0=raw[0:rows, :], in1=C, op=ALU.mult)
                    op("vector", "tt", out=t2, in0=sw[0:rows, 0:512], in1=S, op=ALU.mult)
                    if lay == 1 and oc == 2:
                        dst = A1[0:64, 16384 + tt * 512: 16384 + (tt + 1) * 512]
                    else:
                        dst = A1[:, oc * 8192 + tt * 512: oc * 8192 + (tt + 1) * 512]
                    rv = op("vector", "tt", out=dst, in0=t1, in1=t2, op=ALU.add, inc=True)
                    slot_rv[oc % 2] = rv
                    last_rope = rv
                    if lay == 1 and oc == 2:
                        W("scalar", s_ve, raw32_rv)
                        ev2 = op("scalar", "act", out=P3[64:128, 1024:1536],
                                 in_=pb[oc][64:128, 0:512], func=AF.Copy, inc=True)
                        last_ev[oc] = ev2
                        W("tensor", s_se, ev2)
                        for b in range(4):
                            tb = pb[6 + (b % 2)]
                            W("tensor", s_ve, last_tr.get(b % 2, 0))
                            tv = op("tensor", "tr", out=tb[:, 0:64],
                                    in_=P3[64:128, 1024 + b * 128:1024 + (b + 1) * 128],
                                    ident=IDF[64:128, 64:128], tp=(64, 0), inc=True)
                            W("vector", s_te, tv)
                            n = tt * 4 + b
                            cpv = op("vector", "tc",
                                     out=A1[:, 24576 + n * 65: 24576 + n * 65 + 64],
                                     in_=tb[:, 0:64], inc=True)
                            last_tr[b % 2] = cpv
                            raw32_rv = cpv
            barrier()
            if lay == 1:
                dv = dma("sync", A1[64:128, 16384:24576], A1[0:64, 16384:24576])
                for eng in ENGS:
                    W(eng, s_sy, dv)

        def emit_bias(lay):
            scale = SC0 if lay == 0 else SC1
            krows = 128 if lay == 0 else 64
            srcs = [(A1[:, 0:8192], 128), (A1[:, 8192:16384], 128),
                    (A1[0:krows, 16384:24576], krows)]
            last_mm = 0
            for si, (src, rows) in enumerate(srcs):
                for tt in range(NTT):
                    sq = P4[0:rows, 0:512]
                    W("gpsimd", s_te, last_mm)
                    gv = op("gpsimd", "tt", out=sq, in0=src[:, tt * 512:(tt + 1) * 512],
                            in1=src[:, tt * 512:(tt + 1) * 512], op=ALU.mult, inc=True)
                    W("tensor", s_ge, gv)
                    last_mm = op("tensor", "mm", out=pb[6][0:1, 0:512], lhsT=ONESM[0:rows, 0:1],
                                 rhs=sq, start=True, stop=True, inc=True)
                    W("vector", s_te, last_mm)
                    v = op("vector", "tc", out=STAT[0:1, 0:512], in_=pb[6][0:1, 0:512], inc=True)
                    W("vector", s_ve, v)
                    v = op("vector", "rmax", out=STAT[0:1, 1024 + si * 16 + tt: 1025 + si * 16 + tt],
                           in_=STAT[0:1, 0:512], inc=True)
                    W("vector", s_ve, v)
                v = op("vector", "rmax", out=STAT[0:1, 1072 + si:1073 + si],
                       in_=STAT[0:1, 1024 + si * 16:1024 + si * 16 + 16], inc=True)
                W("vector", s_ve, v)
            for c in range(2):
                v = op("vector", "tt", out=STAT[0:1, 1076 + c:1077 + c],
                       in0=STAT[0:1, 1072 + c:1073 + c], in1=STAT[0:1, 1074:1075],
                       op=ALU.mult, inc=True)
                W("scalar", s_ve, v)
                op("scalar", "act", out=STAT[0:1, 1080 + c:1081 + c],
                   in_=STAT[0:1, 1076 + c:1077 + c], func=AF.Ln, bias=0.0, scale=1.0)
                v = op("scalar", "act", out=STAT[0:1, 1084 + c:1085 + c],
                       in_=STAT[0:1, 1080 + c:1081 + c], func=AF.Exp, bias=0.0, scale=0.5, inc=True)
                W("vector", s_se, v)
                v = op("vector", "ts", out=STAT[0:1, 1088 + c:1089 + c],
                       in_=STAT[0:1, 1084 + c:1085 + c], s=-scale, op=ALU.mult, inc=True)
                W("vector", s_ve, v)
                v = op("vector", "tc", out=STATB[0:1, 1024 + c:1025 + c],
                       in_=STAT[0:1, 1088 + c:1089 + c], inc=True)
                W("vector", s_ve, v)
            W("tensor", s_ve, s_ve.n)
            v = op("tensor", "mm", out=pb[6][:, 0:2], lhsT=ONESM[0:1, 0:128],
                   rhs=STATB[0:1, 1024:1026], start=True, stop=True, inc=True)
            W("vector", s_te, v)
            op("vector", "tc", out=BIAS[:, 0:2], in_=pb[6][:, 0:2], inc=True)
            barrier()

        def emit_attn_l0():
            exp_done = [0, 0]
            av_mm = [0, 0]
            av_mm_old = [0, 0]
            av_read = [0, 0]
            sums_read = 0
            last_norm = [0, 0]
            bc_done = [0, 0]
            dma_done = [0, 0]

            def emit_av(pm, pmi, pblo, pbhi, plast, pexp, ppar):
                for h in range(2):
                    W("tensor", s_se, pexp[h])
                    if pmi == 0:
                        W("tensor", s_se, av_read[h])
                        W("tensor", s_se, sums_read)
                    vap = A1[:, 24576 + pm * 128:24576 + (pm + 1) * 128]
                    pt = P4[:, ppar * 2048 + h * 1024 + pblo: ppar * 2048 + h * 1024 + pbhi]
                    op("tensor", "mm", out=AVB[:, h * 512 + pblo: h * 512 + pbhi],
                       lhsT=vap, rhs=pt, start=(pmi == 0), stop=plast)
                    av_mm[h] = op("tensor", "mm",
                                  out=AVB[32 * h:32 * h + 1, 1024 + pblo:1024 + pbhi],
                                  lhsT=ONESM[:, 0:1], rhs=pt, start=(pmi == 0), stop=plast,
                                  tp=(0, 32 * h), inc=True)

            for j in range(NTT):
                if j >= 1:
                    ms = [4 * j - 1] + [m for m in range(max(0, 4 * j - 4), 4 * j + 4)
                                        if m != 4 * j - 1]
                else:
                    ms = list(range(0, 4))
                pend = None
                for mi, m in enumerate(ms):
                    smin = max(m, 4 * j)
                    smax = min(m + 4, 4 * j + 3)
                    blo = (smin - 4 * j) * 128
                    bhi = (smax - 4 * j) * 128 + 128
                    last = (mi == len(ms) - 1)
                    par = mi % 2
                    smv = {}
                    for h in range(2):
                        W("tensor", s_se, exp_done[h])
                        if mi == 0:
                            W("tensor", s_ve, last_norm[h])
                        kap = A1[:, 16384 + m * 128:16384 + (m + 1) * 128]
                        qap = A1[:, h * 8192 + j * 512 + blo: h * 8192 + j * 512 + bhi]
                        smv[h] = op("tensor", "mm", out=SAB[:, h * 512 + blo: h * 512 + bhi],
                                    lhsT=kap, rhs=qap, start=True, stop=True, inc=True)
                    if pend is not None:
                        av_mm_old[:] = av_mm
                        emit_av(*pend)
                    evals = [0, 0]
                    for h in range(2):
                        mv = None
                        if 4 * j <= m + 4 <= 4 * j + 3:
                            c0 = (m + 4 - 4 * j) * 128
                            W("vector", s_te, smv[h])
                            mv = op("vector", "tt", out=SAB[:, h * 512 + c0: h * 512 + c0 + 128],
                                    in0=SAB[:, h * 512 + c0: h * 512 + c0 + 128],
                                    in1=EDGE, op=ALU.add, inc=True)
                        if m >= 4 * j:
                            c0 = (m - 4 * j) * 128
                            W("vector", s_te, smv[h])
                            mv = op("vector", "tt", out=SAB[:, h * 512 + c0: h * 512 + c0 + 128],
                                    in0=SAB[:, h * 512 + c0: h * 512 + c0 + 128],
                                    in1=TRI, op=ALU.add, inc=True)
                        W("scalar", s_te, smv[h])
                        W("scalar", s_te, av_mm_old[h])
                        if mv:
                            W("scalar", s_ve, mv)
                        pt = P4[:, par * 2048 + h * 1024 + blo: par * 2048 + h * 1024 + bhi]
                        evals[h] = op("scalar", "act", out=pt, in_=SAB[:, h * 512 + blo: h * 512 + bhi],
                                      func=AF.Exp, bias=BIAS[:, h:h + 1], scale=SC0, inc=True)
                        exp_done[h] = evals[h]
                    pend = (m, mi, blo, bhi, last, list(evals), par)
                emit_av(*pend)
                jb, jc = j // 2, (j % 2) * 512
                W("scalar", s_te, max(av_mm))
                W("scalar", s_te, max(bc_done))
                op("scalar", "act", out=STAT[0:1, 0:512], in_=AVB[0:1, 1024:1536],
                   func=AF.Ln, bias=0.0, scale=1.0)
                op("scalar", "act", out=STAT[32:33, 0:512], in_=AVB[32:33, 1024:1536],
                   func=AF.Ln, bias=0.0, scale=1.0)
                op("scalar", "act", out=STATB[0:1, 0:512], in_=STAT[0:1, 0:512],
                   func=AF.Exp, bias=0.0, scale=-1.0)
                sums_read = op("scalar", "act", out=STATB[32:33, 0:512], in_=STAT[32:33, 0:512],
                               func=AF.Exp, bias=0.0, scale=-1.0, inc=True)
                for h in range(2):
                    W("tensor", s_se, sums_read)
                    W("tensor", s_ve, last_norm[h])
                    bc_done[h] = op("tensor", "mm", out=SCD[:, h * 512:(h + 1) * 512],
                                    lhsT=ONESM[32 * h:32 * h + 1, 0:128],
                                    rhs=STATB[32 * h:32 * h + 1, 0:512], start=True, stop=True,
                                    tp=(32 * h, 0), inc=True)
                    W("scalar", s_te, bc_done[h])
                    W("scalar", s_ve, last_norm[h])
                    av_read[h] = op("scalar", "act",
                                    out=P4[:, 4096 + h * 512: 4096 + (h + 1) * 512],
                                    in_=AVB[:, h * 512:(h + 1) * 512], func=AF.Copy, inc=True)
                    W("vector", s_se, av_read[h])
                    W("vector", s_te, bc_done[h])
                    W("vector", s_sy, dma_done[h])
                    last_norm[h] = op("vector", "tt",
                                      out=P4[:, 5120 + h * 512: 5120 + (h + 1) * 512],
                                      in0=P4[:, 4096 + h * 512: 4096 + (h + 1) * 512],
                                      in1=SCD[:, h * 512:(h + 1) * 512], op=ALU.mult, inc=True)
                    W("sync", s_ve, last_norm[h])
                    dma_done[h] = dma("sync",
                                      avi[0][jb * 256 + h * 128: jb * 256 + (h + 1) * 128,
                                             jc:jc + 512],
                                      P4[:, 5120 + h * 512: 5120 + (h + 1) * 512])
            barrier()

        def emit_attn_l1():
            exp_done = [0, 0]
            av_mm = [0, 0, 0, 0]
            av_mm_old = [0, 0, 0, 0]
            av_read = 0
            last_norm = 0
            last_dma = 0
            bc_done = [0, 0, 0, 0]
            SP = [SAB, SCD]

            def emit_av(pn, poff, plast, pexp):
                ppar = pn % 2
                for h in range(4):
                    c = h // 2
                    W("tensor", s_se, pexp[c])
                    if pn == 0:
                        W("tensor", s_se, av_read)
                    vap = A1[:, 24576 + pn * 65: 24576 + pn * 65 + 65]
                    base = ppar * 4096 + c * 1024 + (h % 2) * 512
                    pt = P4[:, base + poff: base + 512]
                    av_mm[h] = op("tensor", "mm",
                                  out=AVB[0:65, h * 512 + poff: h * 512 + 512],
                                  lhsT=vap, rhs=pt, start=(pn == 0), stop=plast, inc=True)

            for j in range(NTT):
                nmax = 4 * j + 3
                pend = None
                for n in range(nmax + 1):
                    diag = n >= 4 * j
                    off = (n - 4 * j) * 128 if diag else 0
                    last = (n == nmax)
                    par = n % 2
                    smv = {}
                    for h in range(4):
                        c = h // 2
                        rb = (h % 2) * 64
                        W("tensor", s_se, exp_done[c])
                        if n == 0:
                            W("tensor", s_ve, last_norm)
                        kap = A1[rb:rb + 64, 16384 + n * 128:16384 + (n + 1) * 128]
                        qap = A1[rb:rb + 64, c * 8192 + j * 512 + off: c * 8192 + (j + 1) * 512]
                        smv[h] = op("tensor", "mm",
                                    out=SP[c][:, (h % 2) * 512 + off:(h % 2) * 512 + 512],
                                    lhsT=kap, rhs=qap, start=True, stop=True,
                                    tp=(rb, 0), inc=True)
                    if pend is not None:
                        av_mm_old[:] = av_mm
                        emit_av(*pend)
                    evals = [0, 0]
                    for c in range(2):
                        mv = None
                        if diag:
                            for hh in range(2):
                                W("vector", s_te, smv[2 * c + hh])
                                mv = op("vector", "tt",
                                        out=SP[c][:, hh * 512 + off: hh * 512 + off + 128],
                                        in0=SP[c][:, hh * 512 + off: hh * 512 + off + 128],
                                        in1=TRI, op=ALU.add, inc=True)
                        W("scalar", s_te, smv[2 * c + 1])
                        W("scalar", s_te, av_mm_old[2 * c])
                        W("scalar", s_te, av_mm_old[2 * c + 1])
                        if mv:
                            W("scalar", s_ve, mv)
                        base = par * 4096 + c * 1024
                        if off == 0:
                            evals[c] = op("scalar", "act", out=P4[:, base: base + 1024],
                                          in_=SP[c][:, 0:1024], func=AF.Exp,
                                          bias=BIAS[:, c:c + 1], scale=SC1, inc=True)
                        else:
                            op("scalar", "act", out=P4[:, base + off: base + 512],
                               in_=SP[c][:, off:512], func=AF.Exp,
                               bias=BIAS[:, c:c + 1], scale=SC1)
                            evals[c] = op("scalar", "act", out=P4[:, base + 512 + off: base + 1024],
                                          in_=SP[c][:, 512 + off:1024], func=AF.Exp,
                                          bias=BIAS[:, c:c + 1], scale=SC1, inc=True)
                        exp_done[c] = evals[c]
                    pend = (n, off, last, list(exp_done))
                emit_av(*pend)
                jb, jc = j // 2, (j % 2) * 512
                for h in range(4):
                    c = h // 2
                    avb = AVB[:, h * 512:(h + 1) * 512]
                    W("scalar", s_te, av_mm[h])
                    W("scalar", s_te, bc_done[h])
                    op("scalar", "act", out=STAT[64:65, h * 512:(h + 1) * 512],
                       in_=avb[64:65, 0:512], func=AF.Ln, bias=0.0, scale=1.0)
                    rcpv = op("scalar", "act", out=STATB[64:65, h * 512:(h + 1) * 512],
                              in_=STAT[64:65, h * 512:(h + 1) * 512], func=AF.Exp,
                              bias=0.0, scale=-1.0, inc=True)
                    W("scalar", s_ve, last_norm)
                    av_read = op("scalar", "act", out=STATB[0:64, 0:512],
                                 in_=avb[0:64, 0:512], func=AF.Copy, inc=True)
                    W("tensor", s_se, rcpv)
                    W("tensor", s_se, exp_done[c])
                    bc_done[h] = op("tensor", "mm",
                                    out=SP[c][0:64, (h % 2) * 512:(h % 2) * 512 + 512],
                                    lhsT=ONESM[64:65, 0:64],
                                    rhs=STATB[64:65, h * 512:(h + 1) * 512],
                                    start=True, stop=True, tp=(64, 0), inc=True)
                    W("vector", s_se, av_read)
                    W("vector", s_te, bc_done[h])
                    W("vector", s_sy, last_dma)
                    last_norm = op("vector", "tt", out=STATB[0:64, 512:1024],
                                   in0=STATB[0:64, 0:512],
                                   in1=SP[c][0:64, (h % 2) * 512:(h % 2) * 512 + 512],
                                   op=ALU.mult, inc=True)
                    W("sync", s_ve, last_norm)
                    last_dma = dma("sync",
                                   avi[1][jb * 256 + c * 128 + (h % 2) * 64:
                                          jb * 256 + c * 128 + (h % 2) * 64 + 64,
                                          jc:jc + 512],
                                   STATB[0:64, 512:1024])
            barrier()

        def emit_out_ln_add(tt, mms):
            evac_v = {}
            store_v = {}
            cast_v = {}
            stat_v = None
            pending = []

            def flush_stat(oc):
                nonlocal stat_v
                cv, gv = cast_v[oc]
                W("tensor", s_ve, gv)
                s = (oc % 2) * 512
                op("tensor", "mm", out=pb[6][0:1, 0:512], lhsT=ONESM[:, 0:1],
                   rhs=STATB[:, s:s + 512], start=(oc == 0), stop=(oc == ECH - 1), tp=(0, 0))
                stat_v = op("tensor", "mm", out=pb[6][32:33, 0:512], lhsT=ONESM[:, 0:1],
                            rhs=STATB[:, 1024 + s:1024 + s + 512], start=(oc == 0),
                            stop=(oc == ECH - 1), tp=(0, 32), inc=True)

            for oc in range(ECH):
                mv = mms(oc)
                W("scalar", s_te, mv)
                if oc >= 2:
                    W("scalar", s_sy, store_v[oc - 2])
                    W("scalar", s_ve, cast_v[oc - 2][1])
                slot = P3[:, 1024 + (oc % 2) * 512: 1024 + (oc % 2) * 512 + 512]
                evac_v[oc] = op("scalar", "act", out=slot, in_=pb[4 + oc % 2][:, 0:512],
                                func=AF.Copy, inc=True)
                W("sync", s_se, evac_v[oc])
                store_v[oc] = dma("sync", aod[oc * 128:(oc + 1) * 128, :], slot)
                s = (oc % 2) * 512
                W("vector", s_se, evac_v[oc])
                if oc >= 2:
                    W("vector", s_te, stat_v)
                cv = op("vector", "tc", out=STATB[:, s:s + 512], in_=slot, inc=True)
                gv = op("vector", "tt", out=STATB[:, 1024 + s:1024 + s + 512], in0=slot,
                        in1=slot, op=ALU.mult, inc=True)
                cast_v[oc] = (cv, gv)
                if oc >= 1:
                    flush_stat(oc - 1)
            flush_stat(ECH - 1)
            bcv = ln_tail(stat_v, 512)
            bcv = ln_tail(stat_v, 512)
            lv = {}
            for ec in range(ECH):
                buf = ec % 2
                aslot = P3[:, 2048 + buf * 512: 2048 + buf * 512 + 512]
                if ec >= 2:
                    W("sync", s_ve, lv[ec - 2])
                adv = dma("sync", aslot, aod[ec * 128:(ec + 1) * 128, :])
                W("vector", s_sy, adv)
                W("vector", s_te, bcv)
                op("vector", "tt", out=STAT[:, 1024:1536], in0=aslot, in1=pb[0][:, 0:512],
                   op=ALU.mult)
                op("vector", "tt", out=STAT[:, 1536:2048], in0=STAT[:, 1024:1536],
                   in1=pb[1][:, 0:512], op=ALU.subtract)
                dst = xr[:, ec * TOWN + tt * 512: ec * TOWN + (tt + 1) * 512]
                lv[ec] = op("vector", "tt", out=dst, in0=dst, in1=STAT[:, 1536:2048],
                            op=ALU.add, inc=True)
            barrier()

        def emit_wo(lay):
            W("sync", s_cast, cast_done[(f"wo{lay}", 1)])
            wv = dma("sync", A1[:, 0:32768].rearrange("p (dc e) -> p dc e", e=2048),
                     wob[lay].rearrange("(dc p) e -> p dc e", p=128))
            for tt in range(2):
                buf = tt % 2
                avt = A2[:, buf * 8192: (buf + 1) * 8192]
                lv = dma("sync", avt.rearrange("p (dc t) -> p dc t", t=512),
                         avo[lay][:, tt * 512:(tt + 1) * 512].rearrange("(dc p) t -> p dc t", p=128))

                def mms(oc, avt=avt, lv=lv, wv=wv):
                    W("tensor", s_sy, lv)
                    W("tensor", s_sy, wv)
                    v = None
                    for dc in range(ECH):
                        v = op("tensor", "mm", out=pb[4 + oc % 2][:, 0:512],
                               lhsT=A1[:, dc * 2048 + oc * 128: dc * 2048 + (oc + 1) * 128],
                               rhs=avt[:, dc * 512:(dc + 1) * 512],
                               start=(dc == 0), stop=(dc == ECH - 1), inc=(dc == ECH - 1))
                    return v

                emit_out_ln_add(tt, mms)

        def emit_ffn(lay):
            for tt in range(2):
                emit_ln_pre(lay, to_ag=False, tt=tt)
                if lay == 0:
                    order = list(range(HCH))
                else:
                    order = [c + 16 * m for c in range(16) for m in range(4)]
                wdma_v = {}
                humul_v = {}
                W1SLOT = [A2[:, 8192:12288], A2[:, 12288:16384], P4[:, 4096:8192]]
                for i, hc in enumerate(order):
                    slot = i % 3
                    if i >= 3:
                        W("gpsimd", s_te, wdma_v[i - 3][1])
                    wv = dma("gpsimd", W1SLOT[slot], w1[lay][hc * 128:(hc + 1) * 128, :])
                    gb, ub = pb[2 * (i % 2)], pb[2 * (i % 2) + 1]
                    W("tensor", s_gq, wv)
                    if i >= 2:
                        W("tensor", s_se, humul_v[i - 2][0])
                        W("tensor", s_ve, humul_v[i - 2][1])
                    mv = None
                    wt = W1SLOT[slot]
                    for ec in range(ECH):
                        op("tensor", "mm", out=gb[:, 0:512], lhsT=wt[:, ec * 256:ec * 256 + 128],
                           rhs=A2[:, ec * 512:(ec + 1) * 512], start=(ec == 0), stop=(ec == ECH - 1))
                        mv = op("tensor", "mm", out=ub[:, 0:512],
                                lhsT=wt[:, ec * 256 + 128:ec * 256 + 256],
                                rhs=A2[:, ec * 512:(ec + 1) * 512], start=(ec == 0),
                                stop=(ec == ECH - 1), inc=(ec == ECH - 1))
                    wdma_v[i] = (wv, mv)
                    W("scalar", s_te, mv)
                    if i >= 2:
                        W("scalar", s_ve, humul_v[i - 2][1])
                    s2 = i % 2
                    sv = op("scalar", "act", out=P4[:, s2 * 512: (s2 + 1) * 512],
                            in_=gb[:, 0:512], func=AF.Silu, inc=True)
                    W("vector", s_te, mv)
                    W("vector", s_se, sv)
                    uv = op("vector", "tc", out=P4[:, 1024 + s2 * 512: 1024 + (s2 + 1) * 512],
                            in_=ub[:, 0:512], inc=True)
                    hv = op("vector", "tt", out=A1[:, hc * 512:(hc + 1) * 512],
                            in0=P4[:, s2 * 512:(s2 + 1) * 512],
                            in1=P4[:, 1024 + s2 * 512: 1024 + (s2 + 1) * 512],
                            op=ALU.mult, inc=True)
                    humul_v[i] = (sv, hv)
                    if lay == 1 and i % 4 == 3:
                        emit_sparsify(order[i - 3:i + 1], hv)
                barrier()
                wdown_v = {}
                mm_done = {}

                def mkmms():
                    W2SLOT = [A2[:, 0:8192], A2[:, 8192:16384], P4[:, 0:8192]]

                    def mms(oc):
                        slot = oc % 3
                        if oc >= 3:
                            W("gpsimd", s_te, mm_done[oc - 3])
                        wv = dma("gpsimd", W2SLOT[slot], w2[lay][oc * 128:(oc + 1) * 128, :])
                        W("tensor", s_gq, wv)
                        v = None
                        wt = W2SLOT[slot]
                        for hc in range(HCH):
                            v = op("tensor", "mm", out=pb[4 + oc % 2][:, 0:512],
                                   lhsT=wt[:, hc * 128:(hc + 1) * 128],
                                   rhs=A1[:, hc * 512:(hc + 1) * 512],
                                   start=(hc == 0), stop=(hc == HCH - 1), inc=(hc == HCH - 1))
                        mm_done[oc] = v
                        return v
                    return mms

                emit_out_ln_add(tt, mkmms())

        def emit_sparsify(quartet, last_hv):
            H = [A1[:, hc * 512:(hc + 1) * 512] for hc in quartet]
            sq = [P3[:, k * 512:(k + 1) * 512] for k in range(4)]
            s1 = STAT[:, 1024:1536]
            t1 = STAT[:, 1536:2048]
            s2 = STAT[:, 0:512]
            t2 = STAT[:, 512:1024]
            W("vector", s_ve, last_hv)
            for k in range(4):
                op("vector", "tt", out=sq[k], in0=H[k], in1=H[k], op=ALU.mult)
            op("vector", "tt", out=s1, in0=sq[0], in1=sq[1], op=ALU.max)
            op("vector", "tt", out=t1, in0=sq[0], in1=sq[1], op=ALU.min)
            op("vector", "tt", out=s2, in0=sq[2], in1=sq[3], op=ALU.max)
            op("vector", "tt", out=t2, in0=sq[2], in1=sq[3], op=ALU.min)
            op("vector", "tt", out=s1, in0=s1, in1=s2, op=ALU.min)
            op("vector", "tt", out=t1, in0=t1, in1=t2, op=ALU.max)
            op("vector", "tt", out=s1, in0=s1, in1=t1, op=ALU.max)
            for k in range(4):
                op("vector", "tt", out=t2, in0=sq[k], in1=s1, op=ALU.is_ge)
                op("vector", "tt", out=H[k], in0=H[k], in1=t2, op=ALU.mult,
                   inc=(k == 3))

        def tap_dram(name, src, shape, dtype):
            t = mktap(name, shape, dtype)
            if t is not None:
                v = dma("sync", t, src)
                for eng in ENGS:
                    W(eng, s_sy, v)

        def tap_xr(name):
            t = mktap(name, [E, TOWN], F32)
            if t is not None:
                v = None
                for ec in range(ECH):
                    v = dma("sync", t[ec * 128:(ec + 1) * 128, :], xr[:, ec * TOWN:(ec + 1) * TOWN])
                for eng in ENGS:
                    W(eng, s_sy, v)

        def tap_sbuf(name, src, shape, dtype):
            t = mktap(name, shape, dtype)
            if t is not None:
                v = dma("sync", t, src)
                for eng in ENGS:
                    W(eng, s_sy, v)

        step = [0]

        def go():
            step[0] += 1
            return step[0] <= upto

        for lay in range(2):
            if go():
                emit_ln_pre(lay, to_ag=True, tt=0)
                emit_ln_pre(lay, to_ag=True, tt=1)
                tap_dram(f"xh{lay}", agi[lay][:], [E, TOWN], BF16)
            if go():
                v = cc("AllGather", [agi[lay][:]], [ago[lay][:]])
                if lay == 0:
                    bg_cast_all()
                for eng in ENGS:
                    W(eng, s_cc, v)
            if go():
                emit_qkv(lay)
                tap_sbuf(f"qkv{lay}", A1[:, 0:32768], [128, 32768], BF16)
            if go():
                emit_bias(lay)
                tap_sbuf(f"bias{lay}", BIAS[:, 0:16], [128, 16], F32)
            if go():
                if lay == 0:
                    emit_attn_l0()
                else:
                    emit_attn_l1()
                tap_dram(f"av{lay}", avi[lay][:], [E, TOWN], BF16)
            if go():
                v = cc("AllToAll", [avi[lay][:]], [avo[lay][:]])
                for eng in ENGS:
                    W(eng, s_cc, v)
            if go():
                emit_wo(lay)
                tap_xr(f"xa{lay}")
            if go():
                emit_ffn(lay)
                tap_xr(f"xf{lay}")

        barrier()
        v = None
        for ec in range(ECH):
            v = dma("sync", y_out[ec * 128:(ec + 1) * 128, :], xr[:, ec * TOWN:(ec + 1) * TOWN])
        for eng in ENGS:
            W(eng, s_sy, v)

        block = st_.enter_context(nc.Block())
        with allow_ici_all_to_all():
            @block.sync
            def _(e):
                p.emit_into(e, "sync")

            @block.gpsimd
            def _(e):
                p.emit_into(e, "gpsimd")

            @block.tensor
            def _(e):
                p.emit_into(e, "tensor")

            @block.scalar
            def _(e):
                p.emit_into(e, "scalar")

            @block.vector
            def _(e):
                p.emit_into(e, "vector")

    ninstr = {e: len(p.q[e]) for e in ENGS}
    return nc, tap_aps, ninstr


def _host_tables():
    inv0 = (10000.0 ** (-(np.arange(0, 128, 2, dtype=np.float64)) / 128.0))
    ang0 = np.arange(TFULL, dtype=np.float64)[None, :] * inv0[:, None]
    c0 = np.cos(ang0)
    s0 = np.sin(ang0)
    cs0 = np.empty((256, TFULL), np.float32)
    cs0[0:64] = c0
    cs0[64:128] = c0
    cs0[128:192] = -s0
    cs0[192:256] = s0
    inv1 = (10000.0 ** (-(np.arange(0, 64, 2, dtype=np.float64)) / 64.0))
    ang1 = np.arange(TFULL, dtype=np.float64)[None, :] * inv1[:, None]
    c1 = np.cos(ang1)
    s1 = np.sin(ang1)
    cs1 = np.empty((256, TFULL), np.float32)
    for blk in range(2):
        b = blk * 64
        cs1[b + 0:b + 32] = c1
        cs1[b + 32:b + 64] = c1
        cs1[128 + b + 0:128 + b + 32] = -s1
        cs1[128 + b + 32:128 + b + 64] = s1
    return cs0, cs1


def _host_consts():
    pq = np.arange(128)
    tri = np.where(pq[:, None] <= pq[None, :], 0.0, -1e9).astype(np.float32)
    edge = np.where(pq[None, :] < pq[:, None], 0.0, -1e9).astype(np.float32)
    ident = np.eye(128, dtype=np.float32)
    sw0 = np.zeros((128, 128), np.float32)
    sw0[(pq + 64) % 128, pq] = 1.0
    sig1 = np.where(pq % 64 < 32, pq + 32, pq - 32)
    sw1 = np.zeros((128, 128), np.float32)
    sw1[sig1, pq] = 1.0
    return np.concatenate([tri, edge, ident, sw0, sw1], axis=0)


def _blocks(a, nblk, rows):
    R, C = a.shape
    assert C == nblk * rows
    return np.ascontiguousarray(a.reshape(R, nblk, rows).transpose(1, 0, 2).reshape(nblk * R, rows))


_BUILD_CACHE = {}


def _get_build(taps=(), upto=99):
    key = (tuple(sorted(taps)), upto)
    if key not in _BUILD_CACHE:
        _BUILD_CACHE[key] = build(taps, upto)
    return _BUILD_CACHE[key]


def kernel(L0_attn_x, L0_attn_Wq, L0_attn_Wk, L0_attn_Wv, L0_attn_Wo,
           L0_ffn_w_gate, L0_ffn_w_up, L0_ffn_w_down,
           L1_attn_wq, L1_attn_wk, L1_attn_wv, L1_attn_wo,
           L1_ffn_w_gate, L1_ffn_w_up, L1_ffn_w_down, _taps=(), _upto=99):
    f = np.float32
    x = np.asarray(L0_attn_x, f).reshape(TFULL, E)
    cs0, cs1 = _host_tables()
    consts = _host_consts()

    wq0T = np.asarray(L0_attn_Wq, f).T
    wk0T = np.asarray(L0_attn_Wk, f).T
    wv0T = np.asarray(L0_attn_Wv, f).T
    wo0 = np.ascontiguousarray(np.asarray(L0_attn_Wo, f).T)
    g0T = np.asarray(L0_ffn_w_gate, f).T
    u0T = np.asarray(L0_ffn_w_up, f).T
    d0T = np.asarray(L0_ffn_w_down, f).T
    def _packw1(gT, uT):
        blk = np.concatenate([gT.reshape(E, HCH, 128), uT.reshape(E, HCH, 128)],
                             axis=2).transpose(1, 0, 2)
        return np.ascontiguousarray(
            blk.reshape(HCH, 16, 128, 256).transpose(0, 2, 1, 3).reshape(HCH * 128, 4096))

    def _packw2(dT):
        blk = dT.reshape(HID, ECH, 128).transpose(1, 0, 2)
        return np.ascontiguousarray(
            blk.reshape(ECH, 64, 128, 128).transpose(0, 2, 1, 3).reshape(ECH * 128, 8192))

    w10 = _packw1(g0T, u0T)
    w20 = _packw2(d0T)

    wq1 = np.asarray(L1_attn_wq, f)
    wk1 = np.asarray(L1_attn_wk, f)
    wv1 = np.asarray(L1_attn_wv, f)
    wo1 = np.ascontiguousarray(np.asarray(L1_attn_wo, f))
    perm = 4 * (np.arange(HID) % 2048) + (np.arange(HID) // 2048)
    g1 = np.asarray(L1_ffn_w_gate, f)[:, perm]
    u1 = np.asarray(L1_ffn_w_up, f)[:, perm]
    d1 = np.asarray(L1_ffn_w_down, f)[perm, :]
    w11 = _packw1(g1, u1)
    w21 = _packw2(d1)

    in_maps = []
    for i in range(NCORES):
        kv0 = i // 2
        wqkv0 = np.ascontiguousarray(np.concatenate(
            [wq0T[:, 256 * i:256 * (i + 1)],
             wk0T[:, 128 * kv0:128 * (kv0 + 1)],
             wv0T[:, 128 * kv0:128 * (kv0 + 1)]], axis=1))
        wqkv1 = np.ascontiguousarray(np.concatenate(
            [wq1[:, 256 * i:256 * (i + 1)],
             wk1[:, 64 * i:64 * (i + 1)],
             wv1[:, 64 * i:64 * (i + 1)]], axis=1))
        x_own = np.ascontiguousarray(x[TOWN * i:TOWN * (i + 1), :].T)
        in_maps.append(dict(
            x_in=x_own, wqkv0=wqkv0, wo0=wo0, w10=w10, w20=w20,
            wqkv1=wqkv1, wo1=wo1, w11=w11, w21=w21,
            cs0=cs0, cs1=cs1, consts=consts))

    nc, tap_aps, _ = _get_build(_taps, _upto)
    res = run_bass_kernel_spmd(nc, in_maps, core_ids=list(range(NCORES)))
    outs = [np.asarray(res.results[i]["y_out"]).T for i in range(NCORES)]
    y = np.concatenate(outs, axis=0).reshape(1, TFULL, E)
    if _taps:
        kernel.tap_results = [
            {k: np.asarray(res.results[i]["tap_" + k]) for k in _taps}
            for i in range(NCORES)]
    return y


if __name__ == "__main__":
    nc, tap_aps, ninstr = build()
    print("instr counts:", ninstr, "total:", sum(ninstr.values()))
